# version 13
# baseline (speedup 1.0000x reference)
import math
from contextlib import ExitStack

import numpy as np

import concourse.bass as bass
import concourse.mybir as mybir
from concourse.bass_core_rust import IndirectOffsetOnAxis
from concourse.bass_utils import run_bass_kernel_spmd

F32 = mybir.dt.float32
BF16 = mybir.dt.bfloat16
U32 = mybir.dt.uint32
AF = mybir.ActivationFunctionType
ALU = mybir.AluOpType
AX = mybir.AxisListType

T = 2048
E = 1024
NCH = 8
NTT = 4
TT = 512
H0, D0 = 8, 512
C_L0 = 24.0
C_L1 = 4.0

_FUSED = True
_PHASES = "ABCD"


_NQSEM = 12
_QUEUES = ["qsync", "qscalar", "qgp"]
_STREAMS = (["tensor", "vector", "scalar", "gpsimd", "sync"]
            + [f"{q}{i}" for q in _QUEUES for i in range(_NQSEM)])
_QUEUE_OF = {"sync": "qsync", "scalar": "qscalar", "gpsimd": "qgp"}


class Prog:
    def __init__(self, nc, ctx):
        self.nc = nc
        self.sem = {n: ctx.enter_context(nc.semaphore("sem_" + n))
                    for n in _STREAMS}
        self.cnt = {n: 0 for n in _STREAMS}
        self.qrot = {q: 0 for q in _QUEUES}
        self.hwm = {}
        self.writers = {}
        self.readers = {}
        self.items = []

    def op(self, engine, emit, reads=(), writes=(), dma=False):
        waits = {}

        def need(d):
            for s, v in (d or {}).items():
                if waits.get(s, 0) < v:
                    waits[s] = v

        for r in reads:
            need(self.writers.get(r))
        for w in writes:
            need(self.writers.get(w))
            need(self.readers.get(w))
        if dma:
            q = _QUEUE_OF[engine]
            i = self.qrot[q]
            self.qrot[q] = (i + 1) % _NQSEM
            stream, amt = f"{q}{i}", 16
            if self.cnt[stream] and waits.get(stream, 0) < self.cnt[stream]:
                waits[stream] = self.cnt[stream]
        else:
            stream, amt = engine, 1
        real = {}
        for s, v in waits.items():
            if engine == "tensor" and s == "tensor":
                continue
            k = (engine, s)
            if self.hwm.get(k, 0) >= v:
                continue
            self.hwm[k] = v
            real[s] = v
        self.cnt[stream] += amt
        sig = (stream, self.cnt[stream])
        for w in writes:
            d = self.writers.setdefault(w, {})
            d[sig[0]] = sig[1]
            self.readers.pop(w, None)
        for r in reads:
            d = self.readers.setdefault(r, {})
            if d.get(sig[0], 0) < sig[1]:
                d[sig[0]] = sig[1]
        self.items.append((engine, real, emit, stream, amt))

    def barrier(self):
        snapshot = dict(self.cnt)
        for eng in ["tensor", "vector", "scalar", "gpsimd", "sync"]:
            real = {}
            for s, v in snapshot.items():
                if v == 0 or eng == s:
                    continue
                k = (eng, s)
                if self.hwm.get(k, 0) >= v:
                    continue
                self.hwm[k] = v
                real[s] = v
            if real:
                self.items.append((eng, real, None, None, 0))

    def emit(self, block):
        final = dict(self.cnt)

        def body_for(name):
            def body(e):
                for engine, waits, emitf, stream, amt in self.items:
                    if engine != name:
                        continue
                    for s in sorted(waits):
                        e.wait_ge(self.sem[s], waits[s])
                    if emitf is not None:
                        emitf(e).then_inc(self.sem[stream], amt)
                if name == "gpsimd":
                    for s, v in final.items():
                        if v and s != "gpsimd":
                            e.wait_ge(self.sem[s], v)
            return body

        block.tensor(body_for("tensor"))
        block.vector(body_for("vector"))
        block.scalar(body_for("scalar"))
        block.gpsimd(body_for("gpsimd"))
        block.sync(body_for("sync"))


class Overlay:

    def __init__(self, nc, arena):
        self.nc = nc
        self.base = nc.lookup_mloc(arena).addr
        self.size = int(np.prod(arena.shape[1:]))
        self.off = 0
        self.peak = 0

    def reset(self):
        self.off = 0

    def mark(self):
        return self.off

    def rollback(self, m):
        self.off = m

    def sb(self, name, shape, dt):
        nb = int(np.prod(shape[1:])) * mybir.dt.size(dt)
        nb = (nb + 31) // 32 * 32
        assert self.off + nb <= self.size, (
            f"overlay overflow allocating {name}: {self.off}+{nb} > {self.size}")
        t = self.nc.alloc_sbuf_tensor_at(name, shape, dt,
                                         offset=self.base + self.off)
        self.off += nb
        self.peak = max(self.peak, self.off)
        return t



def emit_ln(P, g, gate_msum=None, gtag=""):
    x, xln = g["x"], g["xln"]
    ln = g["ln"]
    ones = ln["ones128"]
    sq2, stm, stv, stt = ln["sq2"], ln["stm"], ln["stv"], ln["sttmp"]
    ps = g["psum"]
    for tt in range(NTT):
        s = slice(tt * TT, (tt + 1) * TT)
        for c in range(NCH):
            P.op("vector",
                 lambda e, c=c, s=s: e.tensor_copy(xln[:, c, s], x[:, c, s]),
                 reads=[f"rx{c}_{tt}"], writes=[f"xb{c}_{tt}", f"xln{c}_{tt}"])
    for tt in range(NTT):
        s = slice(tt * TT, (tt + 1) * TT)
        for c in range(NCH):
            sq = sq2[:, (c % 2) * TT:(c % 2 + 1) * TT]
            P.op("scalar",
                 lambda e, c=c, s=s, sq=sq: e.activation(sq, xln[:, c, s],
                                                         AF.Square),
                 reads=[f"xb{c}_{tt}"], writes=[f"lnsq{c % 2}"])
            P.op("tensor",
                 lambda e, c=c, s=s: e.matmul(ps[0][:], ones[:], xln[:, c, s],
                                              start=(c == 0), stop=(c == NCH - 1)),
                 reads=[f"xb{c}_{tt}", "consts"], writes=["pb0"])
            P.op("tensor",
                 lambda e, c=c, sq=sq: e.matmul(ps[1][:], ones[:], sq,
                                                start=(c == 0), stop=(c == NCH - 1)),
                 reads=[f"lnsq{c % 2}", "consts"], writes=["pb1"])
        P.op("vector", lambda e: e.tensor_scalar(stm[:], ps[0][:], 1.0 / E, None,
                                                 ALU.mult),
             reads=["pb0"], writes=["lnstm"])
        P.op("vector", lambda e: e.tensor_scalar(stv[:], ps[1][:], 1.0 / E, 1e-5,
                                                 ALU.mult, ALU.add),
             reads=["pb1"], writes=["lnstv"])
        P.op("vector", lambda e: e.tensor_mul(stt[:], stm[:], stm[:]),
             reads=["lnstm", "pb1"], writes=["lnstt", "lnsq0", "lnsq1"])
        P.op("vector", lambda e: e.tensor_sub(stv[:], stv[:], stt[:]),
             reads=["lnstv", "lnstt"], writes=["lnstv"])
        P.op("scalar", lambda e: e.activation(stt[:], stv[:], AF.Ln),
             reads=["lnstv"], writes=["lnstt"])
        P.op("scalar", lambda e: e.activation(stv[:], stt[:], AF.Exp, scale=-0.5),
             reads=["lnstt"], writes=["lnstv"])
        P.op("vector", lambda e: e.tensor_mul(stm[:], stm[:], stv[:]),
             reads=["lnstm", "lnstv"], writes=["lnstm"])
        for c in range(NCH):
            P.op("vector",
                 lambda e, c=c, s=s: e.tensor_mul(xln[:, c, s], x[:, c, s], stv[:]),
                 reads=[f"rx{c}_{tt}", "lnstv"],
                 writes=[f"xln{c}_{tt}", f"xb{c}_{tt}"])
            P.op("vector",
                 lambda e, c=c, s=s: e.tensor_sub(xln[:, c, s], xln[:, c, s], stm[:]),
                 reads=["lnstm"],
                 writes=[f"xln{c}_{tt}"])
    if gate_msum is not None:
        for c in range(NCH):
            P.op("vector",
                 lambda e, c=c: e.reduce_sum(gate_msum[:, c:c + 1], xln[:, c, :],
                                             axis=AX.X),
                 reads=[f"xln{c}_{t}" for t in range(NTT)],
                 writes=[f"{gtag}gmsum{c}"])


def drain_bg(g, k):
    for _ in range(k):
        if g["bg"]:
            g["bg"].pop(0)()


def xln_reads(c, tt=None):
    if tt is None:
        return [f"xln{c}_{t}" for t in range(NTT)]
    return [f"xln{c}_{tt}"]


def rx_w(c, tt=None):
    if tt is None:
        return [f"rx{c}_{t}" for t in range(NTT)]
    return [f"rx{c}_{tt}"]



def emit_phaseA(P, ctx, g):
    sb = g["ov"].sb
    x, xln, ps = g["x"], g["xln"], g["psum"]

    wq = sb("A_wq", [128, NCH, 512], BF16)
    wk = sb("A_wk", [128, NCH, 512], BF16)
    wv = sb("A_wv", [128, NCH, 512], BF16)
    wo = sb("A_wo", [128, 4, 1024], BF16)
    pe = sb("A_pe", [128, 4, T], BF16)
    msk = sb("A_msk", [128, 896], BF16)
    q = sb("A_q", [128, 4, T], BF16)
    k = sb("A_k", [128, 4, T], BF16)
    v = sb("A_v", [128, 16, TT], BF16)
    pbuf = sb("A_p", [128, 2, TT], BF16)
    att = sb("A_att", [128, 4, TT], BF16)
    rrep = pbuf[:].bitcast(F32).rearrange("p a b -> p (a b)")

    peT = g["ins"]["peT"].rearrange("(c p) t -> p c t", p=128)
    P.op("gpsimd", lambda e: e.dma_start(out=pe[:], in_=peT), dma=True,
         writes=["A_pe"])
    P.op("gpsimd", lambda e: e.dma_start(out=msk[:], in_=g["ins"]["cmask"]),
         dma=True, writes=["A_msk"])

    wqkvT = g["ins"]["wqkvT"].rearrange("(c p) f -> p c f", p=128)
    woT = g["ins"]["woT"].rearrange("(c p) e2 -> p c e2", p=128)

    for h in range(H0):
        drain_bg(g, 2)
        base = h * 3 * D0
        P.op("gpsimd", lambda e, base=base: e.dma_start(
            out=wq[:], in_=wqkvT[:, :, base:base + 512]), dma=True,
            writes=["A_wq"])
        P.op("gpsimd", lambda e, base=base: e.dma_start(
            out=wk[:], in_=wqkvT[:, :, base + 512:base + 1024]), dma=True,
            writes=["A_wk"])
        P.op("gpsimd", lambda e, base=base: e.dma_start(
            out=wv[:], in_=wqkvT[:, :, base + 1024:base + 1536]), dma=True,
            writes=["A_wv"])
        P.op("gpsimd", lambda e, h=h: e.dma_start(
            out=wo[:], in_=woT[:, 4 * h:4 * h + 4, :]), dma=True,
            writes=["A_wo"])

        for d4 in range(4):
            for tt in range(NTT):
                s = slice(tt * TT, (tt + 1) * TT)
                for name, wt, dst, bank in (("q", wq, q, 2 + (d4 * NTT + tt) % 2),
                                            ("k", wk, k, 4 + (d4 * NTT + tt) % 2)):
                    for c in range(NCH):
                        P.op("tensor", lambda e, c=c, d4=d4, s=s, wt=wt, bank=bank:
                             e.matmul(ps[bank][:], wt[:, c, d4 * 128:(d4 + 1) * 128],
                                      xln[:, c, s], start=(c == 0),
                                      stop=(c == NCH - 1)),
                             reads=[f"A_w{name}"] + xln_reads(c, tt),
                             writes=[f"pb{bank}"])
                    P.op("vector", lambda e, d4=d4, s=s, dst=dst, bank=bank:
                         e.tensor_add(dst[:, d4, s], ps[bank][:], pe[:, d4, s]),
                         reads=[f"pb{bank}", "A_pe"],
                         writes=[f"A_{name}{d4}_{tt}"])

        for tc in range(16):
            bank = 6 + tc % 2
            for c in range(NCH):
                P.op("tensor", lambda e, c=c, tc=tc, bank=bank:
                     e.matmul(ps[bank][:], xln[:, c, tc * 128:(tc + 1) * 128],
                              wv[:, c, :], start=(c == 0), stop=(c == NCH - 1)),
                     reads=["A_wv"] + xln_reads(c, tc // 4),
                     writes=[f"pb{bank}"])
            P.op("vector", lambda e, tc=tc, bank=bank:
                 e.tensor_copy(v[:, tc, :], ps[bank][:]),
                 reads=[f"pb{bank}"], writes=[f"A_v{tc}"])

        def a_sexp(qj, ci):
            sqj = slice(qj * TT, (qj + 1) * TT)
            sbank = 4 + ci % 2
            pp = ci % 2
            for d4 in range(4):
                P.op("tensor", lambda e, ci=ci, d4=d4, sqj=sqj, sbank=sbank:
                     e.matmul(ps[sbank][:], k[:, d4, ci * 128:(ci + 1) * 128],
                              q[:, d4, sqj], start=(d4 == 0), stop=(d4 == 3)),
                     reads=[f"A_k{d4}_{ci // 4}", f"A_q{d4}_{qj}"],
                     writes=[f"pb{sbank}"])
            P.op("scalar", lambda e, pp=pp, sbank=sbank:
                 e.activation(pbuf[:, pp, :], ps[sbank][:], AF.Exp,
                              bias=g["cbias"][:, 0:1],
                              scale=1.0 / math.sqrt(D0)),
                 reads=[f"pb{sbank}", "consts"],
                 writes=[f"A_P{pp}", "A_rrep"])
            if ci >= 4 * qj:
                o = ci - 4 * qj
                ms = slice(384 - 128 * o, 896 - 128 * o)
                P.op("vector", lambda e, pp=pp, ms=ms:
                     e.tensor_mul(pbuf[:, pp, :], pbuf[:, pp, :], msk[:, ms]),
                     reads=["A_msk"], writes=[f"A_P{pp}"])

        def a_pv(qj, ci):
            nck = 4 * qj + 4
            pp = ci % 2
            for d4 in range(4):
                P.op("tensor", lambda e, ci=ci, d4=d4, pp=pp, nck=nck:
                     e.matmul(ps[d4][:], v[:, ci, d4 * 128:(d4 + 1) * 128],
                              pbuf[:, pp, :], start=(ci == 0),
                              stop=(ci == nck - 1)),
                     reads=[f"A_v{ci}", f"A_P{pp}"], writes=[f"pb{d4}"])
            P.op("tensor", lambda e, ci=ci, pp=pp, nck=nck:
                 e.matmul(ps[6][:], g["ln"]["ones128"][:], pbuf[:, pp, :],
                          start=(ci == 0), stop=(ci == nck - 1)),
                 reads=[f"A_P{pp}", "consts"], writes=["pb6"])

        for qj in range(NTT):
            sqj = slice(qj * TT, (qj + 1) * TT)
            nck = 4 * qj + 4
            a_sexp(qj, 0)
            for ci in range(1, nck):
                a_sexp(qj, ci)
                a_pv(qj, ci - 1)
            a_pv(qj, nck - 1)
            P.op("vector", lambda e: e.reciprocal(rrep[:], ps[6][:]),
                 reads=["pb6"], writes=["A_rrep"])
            for d4 in range(4):
                P.op("vector", lambda e, d4=d4:
                     e.tensor_mul(att[:, d4, :], ps[d4][:], rrep[:]),
                     reads=[f"pb{d4}", "A_rrep"], writes=[f"A_att{d4}"])
            for ec in range(NCH):
                bank = 6 + ec % 2
                for f4 in range(4):
                    P.op("tensor", lambda e, f4=f4, ec=ec, bank=bank:
                         e.matmul(ps[bank][:],
                                  wo[:, f4, ec * 128:(ec + 1) * 128],
                                  att[:, f4, :], start=(f4 == 0),
                                  stop=(f4 == 3)),
                         reads=["A_wo", f"A_att{f4}"], writes=[f"pb{bank}"])
                P.op("vector", lambda e, ec=ec, sqj=sqj, bank=bank:
                     e.tensor_add(x[:, ec, sqj], x[:, ec, sqj], ps[bank][:]),
                     reads=[f"pb{bank}"], writes=[f"rx{ec}_{qj}"])



def emit_phaseB(P, ctx, g):
    sb = g["ov"].sb
    x, xln, ps = g["x"], g["xln"], g["psum"]
    ins = g["ins"]

    w1sl = sb("B_w1", [128, 2, NCH, 1024], BF16)
    w2sl = sb("B_w2", [128, 2, NCH, 1024], BF16)
    hT = sb("B_h", [128, NCH, T], BF16)
    wg = sb("B_wg", [128, NCH, 8], F32)
    msum = g["bmsum"]
    gsc = sb("B_gsc", [1, 64], F32)
    bgt = sb("B_bg", [1, 8], F32)
    alv = sb("B_al", [1, 1], F32)
    b1m = sb("B_b1m", [128, 32], F32)
    b2m = sb("B_b2m", [128, 8], F32)
    b1x = sb("B_b1x", [128, 2, 32], F32)
    b2x = sb("B_b2x", [128, 2, 8], F32)
    bcomb = sb("B_bc", [128, 8], F32)
    crep = sb("B_crep", [128, 3], F32)
    selrep = sb("B_selrep", [128, 2], F32)
    iotaf, iotag = g["iotaf"], g["iotag"]
    idxf = sb("B_idxf", [128, 33], F32)
    w1idx = sb("B_w1idx", [128, 2, 8], U32)
    w2idx = sb("B_w2idx", [128, 2, 32], U32)
    bidxf = sb("B_bidxf", [32, 1], F32)
    onesr = sb("B_onesr", [1, 128], F32)
    bidx = sb("B_bidx", [32, 1], U32)
    btmp = sb("B_btmp", [32, 128], F32)

    P.op("vector", lambda e: e.memset(onesr[:], 1.0), writes=["B_onesr"])
    P.op("sync", lambda e: e.dma_start(out=wg[:], in_=ins["wgate"].rearrange(
        "(c p) e8 -> p c e8", p=128)), dma=True, writes=["B_wg"])
    P.op("sync", lambda e: e.dma_start(out=bgt[:], in_=ins["bgate"]), dma=True,
         writes=["B_bg"])
    P.op("sync", lambda e: e.dma_start(out=alv[:], in_=ins["alpha"]), dma=True,
         writes=["B_al"])
    P.op("sync", lambda e: e.dma_start(out=b1m[:], in_=ins["b1mcm"]), dma=True,
         writes=["B_b1m"])
    P.op("sync", lambda e: e.dma_start(out=b2m[:], in_=ins["b2mcm"]), dma=True,
         writes=["B_b2m"])

    W1mv = ins["W1m"].rearrange("(c p) hh -> p c hh", p=128)
    W2mv = ins["W2m"].rearrange("(gg p) ee -> p gg ee", p=128)
    W1ef = ins["W1e"].rearrange("e d h -> (e d) h")
    W2ef = ins["W2e"].rearrange("e h d -> (e h) d")

    def load_w1(path, s, par):
        if path == "m":
            P.op("gpsimd", lambda e, s=s, par=par: e.dma_start(
                out=w1sl[:, par, :, :],
                in_=W1mv[:, :, s * 1024:(s + 1) * 1024]), dma=True,
                writes=[f"B_w1b{par}c{c}" for c in range(NCH)])
        else:
            k = 0 if path == "a" else 1
            for c in range(NCH):
                P.op("gpsimd", lambda e, s=s, par=par, c=c, k=k:
                     e.indirect_dma_start(
                         out=w1sl[:, par, c, :], out_offset=None, in_=W1ef,
                         in_offset=IndirectOffsetOnAxis(ap=w1idx[:, k, :][:, c:c + 1],
                                                        axis=0),
                         element_offset=s * 1024), dma=True,
                     reads=[f"B_w1idx{k}"], writes=[f"B_w1b{par}c{c}"])

    def load_w2(path, s, par):
        if path == "m":
            P.op("gpsimd", lambda e, s=s, par=par: e.dma_start(
                out=w2sl[:, par, :, :],
                in_=W2mv[:, s * 8:(s + 1) * 8, :]), dma=True,
                writes=[f"B_w2b{par}c{c}" for c in range(NCH)])
        else:
            k = 0 if path == "a" else 1
            for gg in range(NCH):
                P.op("gpsimd", lambda e, s=s, par=par, gg=gg, k=k:
                     e.indirect_dma_start(
                         out=w2sl[:, par, gg, :], out_offset=None, in_=W2ef,
                         in_offset=IndirectOffsetOnAxis(
                             ap=w2idx[:, k, :][:, s * 8 + gg:s * 8 + gg + 1],
                             axis=0)), dma=True,
                     reads=[f"B_w2idx{k}"], writes=[f"B_w2b{par}c{gg}"])

    load_w1("m", 0, 0)
    load_w2("m", 0, 0)

    gs = lambda a, b: gsc[:, a:b]
    for c in range(NCH):
        P.op("tensor", lambda e, c=c: e.matmul(ps[0][0:1, 0:8], msum[:, c:c + 1],
                                               wg[:, c, :], start=(c == 0),
                                               stop=(c == NCH - 1)),
             reads=[f"Bgmsum{c}", "B_wg"], writes=["pb0"])
    P.op("vector", lambda e: e.tensor_scalar(gs(0, 8), ps[0][0:1, 0:8],
                                             1.0 / T, None, ALU.mult),
         reads=["pb0"], writes=["Bg_lg"])
    P.op("vector", lambda e: e.tensor_add(gs(0, 8), gs(0, 8), bgt[:]),
         reads=["B_bg"], writes=["Bg_lg"])
    P.op("vector", lambda e: e.reduce_max(gs(8, 9), gs(0, 8), axis=AX.X),
         reads=["Bg_lg"], writes=["Bg_mx"])
    P.op("vector", lambda e: e.tensor_scalar(gs(9, 10), gs(8, 9), -1.0, None,
                                             ALU.mult),
         reads=["Bg_mx"], writes=["Bg_nmx"])
    P.op("scalar", lambda e: e.activation(gs(16, 24), gs(0, 8), AF.Exp,
                                          bias=gs(9, 10)),
         reads=["Bg_lg", "Bg_nmx"], writes=["Bg_ex"])
    P.op("vector", lambda e: e.reduce_sum(gs(10, 11), gs(16, 24), axis=AX.X),
         reads=["Bg_ex"], writes=["Bg_sm"])
    P.op("vector", lambda e: e.reciprocal(gs(11, 12), gs(10, 11)),
         reads=["Bg_sm"], writes=["Bg_rs"])
    P.op("vector", lambda e: e.tensor_scalar(gs(24, 32), gs(16, 24), gs(11, 12),
                                             None, ALU.mult),
         reads=["Bg_ex", "Bg_rs"], writes=["Bg_p"])
    P.op("vector", lambda e: e.reduce_max(gs(12, 13), gs(24, 32), axis=AX.X),
         reads=["Bg_p"], writes=["Bg_m1"])
    P.op("vector", lambda e: e.tensor_scalar(gs(32, 40), gs(24, 32), gs(12, 13),
                                             None, ALU.is_ge),
         reads=["Bg_p", "Bg_m1"], writes=["Bg_is"])
    P.op("vector", lambda e: e.tensor_mul(gs(40, 48), gs(32, 40), iotaf[0:1, 0:8]),
         reads=["Bg_is", "consts"], writes=["Bg_t"])
    P.op("vector", lambda e: e.reduce_sum(gs(13, 14), gs(40, 48), axis=AX.X),
         reads=["Bg_t"], writes=["Bg_i1"])
    P.op("vector", lambda e: e.tensor_scalar(gs(13, 14), gs(13, 14), 1.0 / 128,
                                             None, ALU.mult),
         reads=["Bg_i1"], writes=["Bg_i1"])
    P.op("vector", lambda e: e.tensor_scalar(gs(32, 40), gs(32, 40), 2.0, None,
                                             ALU.mult),
         reads=["Bg_is"], writes=["Bg_is"])
    P.op("vector", lambda e: e.tensor_sub(gs(48, 56), gs(24, 32), gs(32, 40)),
         reads=["Bg_p", "Bg_is"], writes=["Bg_pw"])
    P.op("vector", lambda e: e.reduce_max(gs(14, 15), gs(48, 56), axis=AX.X),
         reads=["Bg_pw"], writes=["Bg_m2"])
    P.op("vector", lambda e: e.tensor_scalar(gs(32, 40), gs(48, 56), gs(14, 15),
                                             None, ALU.is_ge),
         reads=["Bg_pw", "Bg_m2"], writes=["Bg_is"])
    P.op("vector", lambda e: e.tensor_mul(gs(40, 48), gs(32, 40), iotaf[0:1, 0:8]),
         reads=["Bg_is", "consts"], writes=["Bg_t"])
    P.op("vector", lambda e: e.reduce_sum(gs(15, 16), gs(40, 48), axis=AX.X),
         reads=["Bg_t"], writes=["Bg_i2"])
    P.op("vector", lambda e: e.tensor_scalar(gs(15, 16), gs(15, 16), 1.0 / 128,
                                             None, ALU.mult),
         reads=["Bg_i2"], writes=["Bg_i2"])
    P.op("vector", lambda e: e.tensor_add(gs(56, 57), gs(12, 13), gs(14, 15)),
         reads=["Bg_m1", "Bg_m2"], writes=["Bg_den"])
    P.op("vector", lambda e: e.reciprocal(gs(57, 58), gs(56, 57)),
         reads=["Bg_den"], writes=["Bg_rden"])
    P.op("vector", lambda e: e.tensor_mul(gs(58, 59), gs(12, 13), gs(57, 58)),
         reads=["Bg_m1", "Bg_rden"], writes=["Bg_gA"])
    P.op("vector", lambda e: e.tensor_mul(gs(59, 60), gs(14, 15), gs(57, 58)),
         reads=["Bg_m2", "Bg_rden"], writes=["Bg_gB"])
    P.op("vector", lambda e: e.tensor_scalar(gs(60, 61), alv[:], 0.1, 1.0,
                                             ALU.max, ALU.min),
         reads=["B_al"], writes=["Bg_a"])
    P.op("vector", lambda e: e.tensor_scalar(gs(61, 62), gs(60, 61), -1.0, 1.0,
                                             ALU.mult, ALU.add),
         reads=["Bg_a"], writes=["Bg_oma"])
    P.op("vector", lambda e: e.tensor_mul(gs(62, 63), gs(61, 62), gs(58, 59)),
         reads=["Bg_oma", "Bg_gA"], writes=["Bg_cA"])
    P.op("vector", lambda e: e.tensor_mul(gs(63, 64), gs(61, 62), gs(59, 60)),
         reads=["Bg_oma", "Bg_gB"], writes=["Bg_cB"])
    for i, (a0, rr) in enumerate([(60, "Bg_a"), (62, "Bg_cA"), (63, "Bg_cB")]):
        P.op("tensor", lambda e, a0=a0, i=i: e.matmul(
            ps[1][:, i:i + 1], onesr[:], gsc[:, a0:a0 + 1],
            start=True, stop=True),
            reads=[rr, "B_onesr"], writes=["pb1"])
    P.op("vector", lambda e: e.tensor_copy(crep[:], ps[1][:, 0:3]),
         reads=["pb1"], writes=["B_crep"])
    for i, (a0, rr) in enumerate([(13, "Bg_i1"), (15, "Bg_i2")]):
        P.op("tensor", lambda e, a0=a0, i=i: e.matmul(
            ps[2][:, i:i + 1], onesr[:], gsc[:, a0:a0 + 1],
            start=True, stop=True),
            reads=[rr, "B_onesr"], writes=["pb2"])
    P.op("vector", lambda e: e.tensor_copy(selrep[:], ps[2][:, 0:2]),
         reads=["pb2"], writes=["B_selrep"])

    b1ef = ins["b1e"].rearrange("e (gg j) -> (e gg) j", j=128)
    b2ef = ins["b2e"].rearrange("e (gg j) -> (e gg) j", j=128)
    for k in range(2):
        sc = selrep[:, k:k + 1]
        P.op("vector", lambda e, sc=sc: e.tensor_scalar(
            idxf[:, 0:1], sc, 1024.0, None, ALU.mult),
            reads=["B_selrep"], writes=["B_idxt"])
        P.op("vector", lambda e, k=k: e.tensor_scalar(
            idxf[:, 1:9], iotaf[:, 0:8], idxf[:, 0:1], None, ALU.add),
            reads=["B_idxt", "consts"], writes=["B_idxf"])
        P.op("vector", lambda e, k=k: e.tensor_copy(w1idx[:, k, :], idxf[:, 1:9]),
             reads=["B_idxf"], writes=[f"B_w1idx{k}"])
        P.op("vector", lambda e, sc=sc: e.tensor_scalar(
            idxf[:, 0:1], sc, 4096.0, None, ALU.mult),
            reads=["B_selrep"], writes=["B_idxt"])
        P.op("vector", lambda e: e.tensor_scalar(
            idxf[:, 1:33], iotaf[:], idxf[:, 0:1], None, ALU.add),
            reads=["B_idxt", "consts"], writes=["B_idxf"])
        P.op("vector", lambda e, k=k: e.tensor_copy(w2idx[:, k, :], idxf[:, 1:33]),
             reads=["B_idxf"], writes=[f"B_w2idx{k}"])
        P.op("vector", lambda e, sc=sc: e.tensor_scalar(
            bidxf[:], sc[0:32, :], 32.0, None, ALU.mult),
            reads=["B_selrep"], writes=["B_bidxf"])
        P.op("vector", lambda e: e.tensor_scalar(
            bidxf[:], iotag[0:32, :], bidxf[:], None, ALU.add),
            reads=["B_bidxf", "consts"], writes=["B_bidxf"])
        P.op("vector", lambda e: e.tensor_copy(bidx[:], bidxf[:]),
             reads=["B_bidxf"], writes=["B_bidx"])
        P.op("gpsimd", lambda e: e.indirect_dma_start(
            out=btmp[:], out_offset=None, in_=b1ef,
            in_offset=IndirectOffsetOnAxis(ap=bidx[:], axis=0)), dma=True,
            reads=["B_bidx"], writes=["B_btmp"])
        P.op("tensor", lambda e: e.transpose(ps[3][:, 0:32], btmp[:],
                                             g["eyef"][0:32, 0:32]),
             reads=["B_btmp", "consts"], writes=["pb3"])
        P.op("vector", lambda e, k=k: e.tensor_copy(b1x[:, k, :], ps[3][:, 0:32]),
             reads=["pb3"], writes=[f"B_b1x{k}"])
        P.op("vector", lambda e, sc=sc: e.tensor_scalar(
            bidxf[0:8, :], sc[0:8, :], 8.0, None, ALU.mult),
            reads=["B_selrep"], writes=["B_bidxf"])
        P.op("vector", lambda e: e.tensor_scalar(
            bidxf[0:8, :], iotag[0:8, :], bidxf[0:8, :], None, ALU.add),
            reads=["B_bidxf", "consts"], writes=["B_bidxf"])
        P.op("vector", lambda e: e.tensor_copy(bidx[0:8, :], bidxf[0:8, :]),
             reads=["B_bidxf"], writes=["B_bidx"])
        P.op("gpsimd", lambda e: e.indirect_dma_start(
            out=btmp[0:8, :], out_offset=None, in_=b2ef,
            in_offset=IndirectOffsetOnAxis(ap=bidx[0:8, :], axis=0)), dma=True,
            reads=["B_bidx"], writes=["B_btmp"])
        P.op("tensor", lambda e: e.transpose(ps[3][:, 0:8], btmp[0:8, :],
                                             g["eyef"][0:8, 0:8]),
             reads=["B_btmp", "consts"], writes=["pb3"])
        P.op("vector", lambda e, k=k: e.tensor_copy(b2x[:, k, :], ps[3][:, 0:8]),
             reads=["pb3"], writes=[f"B_b2x{k}"])
    P.op("vector", lambda e: e.tensor_scalar(bcomb[:], b2m[:], crep[:, 0:1],
                                             None, ALU.mult),
         reads=["B_b2m", "B_crep"], writes=["B_bc"])
    for k in range(2):
        P.op("vector", lambda e, k=k: e.tensor_scalar(
            idxf[:, 0:8], b2x[:, k, :], crep[:, k + 1:k + 2], None, ALU.mult),
            reads=[f"B_b2x{k}", "B_crep"], writes=["B_idxf"])
        P.op("vector", lambda e: e.tensor_add(bcomb[:], bcomb[:], idxf[:, 0:8]),
             reads=["B_idxf"], writes=["B_bc"])

    paths = [("m", b1m, 0), ("a", b1x[:, 0, :], 1), ("b", b1x[:, 1, :], 2)]
    sl_ctr = 0
    for pi, (path, b1c, ci_) in enumerate(paths):
        for s in range(4):
            drain_bg(g, 2)
            par = sl_ctr % 2
            if s < 3:
                load_w1(path, s + 1, (sl_ctr + 1) % 2)
                load_w2(path, s + 1, (sl_ctr + 1) % 2)
            elif pi < 2:
                load_w1(paths[pi + 1][0], 0, (sl_ctr + 1) % 2)
                load_w2(paths[pi + 1][0], 0, (sl_ctr + 1) % 2)
            for hc in range(NCH):
                for tt in range(NTT):
                    bank = (hc * NTT + tt) % 4
                    st_ = slice(tt * TT, (tt + 1) * TT)
                    for c in range(NCH):
                        P.op("tensor", lambda e, c=c, hc=hc, st_=st_, par=par, bank=bank:
                             e.matmul(ps[bank][:],
                                      w1sl[:, par, c, hc * 128:(hc + 1) * 128],
                                      xln[:, c, st_], start=(c == 0),
                                      stop=(c == NCH - 1)),
                             reads=[f"B_w1b{par}c{c}"] + xln_reads(c, tt),
                             writes=[f"pb{bank}"])
                    gx = s * 8 + hc
                    P.op("scalar", lambda e, hc=hc, st_=st_, bank=bank, b1c=b1c, gx=gx:
                         e.activation(hT[:, hc, st_], ps[bank][:],
                                      AF.Gelu_apprx_tanh,
                                      bias=b1c[:, gx:gx + 1]),
                         reads=[f"pb{bank}", "B_b1m", "B_b1x0", "B_b1x1"],
                         writes=[f"B_h{hc}_{tt}"])
            for ec in range(NCH):
                for tt in range(NTT):
                    bank = 4 + (ec * NTT + tt) % 4
                    st_ = slice(tt * TT, (tt + 1) * TT)
                    for hc in range(NCH):
                        P.op("tensor", lambda e, hc=hc, ec=ec, st_=st_, par=par, bank=bank:
                             e.matmul(ps[bank][:],
                                      w2sl[:, par, hc, ec * 128:(ec + 1) * 128],
                                      hT[:, hc, st_], start=(hc == 0),
                                      stop=(hc == NCH - 1)),
                             reads=[f"B_w2b{par}c{hc}", f"B_h{hc}_{tt}"],
                             writes=[f"pb{bank}"])
                    P.op("vector", lambda e, ec=ec, st_=st_, bank=bank, ci_=ci_:
                         e.scalar_tensor_tensor(x[:, ec, st_], ps[bank][:],
                                                crep[:, ci_:ci_ + 1],
                                                x[:, ec, st_],
                                                ALU.mult, ALU.add),
                         reads=[f"pb{bank}", "B_crep"], writes=[f"rx{ec}_{tt}"])
            sl_ctr += 1
    for c in range(NCH):
        for tt in range(NTT):
            st_ = slice(tt * TT, (tt + 1) * TT)
            P.op("scalar", lambda e, c=c, st_=st_: e.add(
                x[:, c, st_], x[:, c, st_], bcomb[:, c:c + 1]),
                reads=["B_bc"], writes=[f"rx{c}_{tt}"])



def emit_phaseC(P, ctx, g):
    sb = g["ov"].sb
    x, xln, ps = g["x"], g["xln"], g["psum"]
    ins = g["ins"]

    wqp = sb("C_wq", [128, 2, NCH, 128], BF16)
    wkp = sb("C_wk", [128, 2, NCH, 128], BF16)
    wvp = sb("C_wv", [128, 2, NCH, 128], BF16)
    wop = sb("C_wo", [128, 2, 1024], BF16)
    qp = sb("C_qp", [128, 2, T], BF16)
    kp = sb("C_kp", [128, 2, T], BF16)
    vp = sb("C_vp", [128, 2, 16, 256], BF16)
    att = sb("C_att", [128, 2, NTT, TT], BF16)
    pbuf = sb("C_p", [128, 4, TT], BF16)
    rq = sb("C_rq", [128, TT], F32)

    Wq = ins["Wqc"].rearrange("(c p) f -> p c f", p=128)
    Wk = ins["Wkc"].rearrange("(c p) f -> p c f", p=128)
    Wv = ins["Wvc"].rearrange("(c p) f -> p c f", p=128)
    Wo = ins["Wo1"].rearrange("(q p) o -> p q o", p=128)

    P.op("vector", lambda e: e.memset(vp[:, :, :, 64:128], 1.0), writes=["C_vo"])
    P.op("vector", lambda e: e.memset(vp[:, :, :, 192:256], 1.0), writes=["C_vo"])

    def load_pair(pp, par):
        fs = slice(pp * 128, (pp + 1) * 128)
        P.op("gpsimd", lambda e, fs=fs, par=par: e.dma_start(
            out=wqp[:, par, :, :], in_=Wq[:, :, fs]), dma=True,
            writes=[f"C_wq{par}"])
        P.op("gpsimd", lambda e, fs=fs, par=par: e.dma_start(
            out=wkp[:, par, :, :], in_=Wk[:, :, fs]), dma=True,
            writes=[f"C_wk{par}"])
        P.op("gpsimd", lambda e, fs=fs, par=par: e.dma_start(
            out=wvp[:, par, :, :], in_=Wv[:, :, fs]), dma=True,
            writes=[f"C_wv{par}"])
        P.op("gpsimd", lambda e, pp=pp, par=par: e.dma_start(
            out=wop[:, par, :], in_=Wo[:, pp, :]), dma=True,
            writes=[f"C_wo{par}"])

    def proj_units(pp, par):
        units = []
        for tt in range(NTT):
            st_ = slice(tt * TT, (tt + 1) * TT)
            for name, wt, dst in (("q", wqp, qp), ("k", wkp, kp)):
                bank = (4 if name == "q" else 5) + 2 * (tt % 2)

                def u(name=name, wt=wt, dst=dst, bank=bank, st_=st_, tt=tt):
                    for c in range(NCH):
                        P.op("tensor", lambda e, c=c: e.matmul(
                            ps[bank][:], wt[:, par, c, :], xln[:, c, st_],
                            start=(c == 0), stop=(c == NCH - 1)),
                            reads=[f"C_w{name}{par}"] + xln_reads(c, tt),
                            writes=[f"pb{bank}"])
                    P.op("vector", lambda e: e.tensor_copy(dst[:, par, st_],
                                                           ps[bank][:]),
                         reads=[f"pb{bank}"], writes=[f"C_{name}{par}_{tt}"])
                units.append(u)
        for tc in range(16):
            bank = 2 + tc % 2

            def u(tc=tc, bank=bank):
                for c in range(NCH):
                    P.op("tensor", lambda e, c=c: e.matmul(
                        ps[bank][:, 0:128],
                        xln[:, c, tc * 128:(tc + 1) * 128],
                        wvp[:, par, c, :], start=(c == 0),
                        stop=(c == NCH - 1)),
                        reads=[f"C_wv{par}"] + xln_reads(c, tc // 4),
                        writes=[f"pb{bank}"])
                P.op("vector", lambda e: e.tensor_copy(vp[:, par, tc, 0:64],
                                                       ps[bank][:, 0:64]),
                     reads=[f"pb{bank}"], writes=[f"C_v{par}_{tc}"])
                P.op("vector", lambda e: e.tensor_copy(vp[:, par, tc, 128:192],
                                                       ps[bank][:, 64:128]),
                     reads=[f"pb{bank}"], writes=[f"C_v{par}_{tc}"])
            units.append(u)
        return units

    def attn_pair(pp, par):
        def c_sexp(qj, ci):
            sqj = slice(qj * TT, (qj + 1) * TT)
            for hh in range(2):
                hs = slice(hh * 64, (hh + 1) * 64)
                sbank = 4 + hh + 2 * (ci % 2)
                P.op("tensor", lambda e, ci=ci, hs=hs, sqj=sqj, sbank=sbank:
                     e.matmul(ps[sbank][:], kp[hs, par, ci * 128:(ci + 1) * 128],
                              qp[hs, par, sqj], start=True, stop=True),
                     reads=[f"C_k{par}_{ci // 4}", f"C_q{par}_{qj}"],
                     writes=[f"pb{sbank}"])
            for hh in range(2):
                sbank = 4 + hh + 2 * (ci % 2)
                ppx = 2 * (ci % 2) + hh
                P.op("scalar", lambda e, ppx=ppx, sbank=sbank:
                     e.activation(pbuf[:, ppx, :], ps[sbank][:], AF.Exp,
                                  bias=g["cbias"][:, 1:2], scale=1.0 / 8.0),
                     reads=[f"pb{sbank}", "consts"], writes=[f"C_P{ppx}"])

        def c_pv(qj, ci):
            pvb = 2 * (qj % 2)
            for hh in range(2):
                ppx = 2 * (ci % 2) + hh
                P.op("tensor", lambda e, ci=ci, ppx=ppx, hh=hh, pvb=pvb:
                     e.matmul(ps[pvb + hh][:], vp[:, par, ci,
                                                  hh * 128:(hh + 1) * 128],
                              pbuf[:, ppx, :], start=(ci == 0), stop=(ci == 15)),
                     reads=[f"C_v{par}_{ci}", f"C_P{ppx}", "C_vo"],
                     writes=[f"pb{pvb + hh}"])

        def c_evac(qj):
            pvb = 2 * (qj % 2)
            for hh in range(2):
                hs = slice(hh * 64, (hh + 1) * 64)
                rqh = rq[hh * 64:(hh + 1) * 64, :]
                P.op("vector", lambda e, hh=hh, rqh=rqh, pvb=pvb: e.reciprocal(
                    rqh, ps[pvb + hh][64:128, :]),
                    reads=[f"pb{pvb + hh}"], writes=[f"C_rq{hh}"])
                P.op("vector", lambda e, hh=hh, hs=hs, qj=qj, pvb=pvb, rqh=rqh:
                     e.tensor_mul(att[hs, par, qj, :], ps[pvb + hh][0:64, :],
                                  rqh),
                     reads=[f"pb{pvb + hh}", f"C_rq{hh}"],
                     writes=[f"C_att{par}_{qj}"])

        c_sexp(0, 0)
        for u in range(1, 64):
            qj, ci = divmod(u, 16)
            c_sexp(qj, ci)
            pq, pc = divmod(u - 1, 16)
            c_pv(pq, pc)
            if pc == 15:
                c_evac(pq)
        c_pv(3, 15)
        c_evac(3)

    def outproj_units(pp, par):
        units = []
        for oc in range(NCH):
            for tt in range(NTT):
                bank = (oc * NTT + tt) % 2
                st_ = slice(tt * TT, (tt + 1) * TT)

                def u(oc=oc, tt=tt, bank=bank, st_=st_):
                    P.op("tensor", lambda e: e.matmul(
                        ps[bank][:], wop[:, par, oc * 128:(oc + 1) * 128],
                        att[:, par, tt, :], start=True, stop=True),
                        reads=[f"C_wo{par}", f"C_att{par}_{tt}"],
                        writes=[f"pb{bank}"])
                    P.op("vector", lambda e: e.tensor_add(
                        x[:, oc, st_], x[:, oc, st_], ps[bank][:]),
                        reads=[f"pb{bank}"], writes=[f"rx{oc}_{tt}"])
                units.append(u)
        return units

    load_pair(0, 0)
    for u in proj_units(0, 0):
        u()
    load_pair(1, 1)
    for pp in range(8):
        drain_bg(g, 2)
        par = pp % 2
        attn_pair(pp, par)
        pu = proj_units(pp + 1, (pp + 1) % 2) if pp < 7 else []
        ou = outproj_units(pp, par)
        n = max(len(pu), len(ou))
        for i in range(n):
            if i < len(pu):
                pu[i]()
            if i < len(ou):
                ou[i]()
        if pp < 6:
            load_pair(pp + 2, par)



def emit_phaseD(P, ctx, g, out_ap):
    nc = P.nc
    x, xln, ps = g["x"], g["xln"], g["psum"]
    ins = g["ins"]
    ov = g["ov"]
    jtok = g["jtok"]
    NEX, CAP, W = 8, 512, 1040

    gwT = ov.sb("D_gw", [128, NCH, 8], BF16)
    dsm = ov.sb("D_sm", [128, 48], F32)
    ptok = ov.sb("D_pt", [128, 16, 8], F32)
    ptok2 = ov.sb("D_pt2", [128, 8, 16], F32)
    cmpb = ov.sb("D_cmpb", [128, 8, 16], F32)
    bis = ov.sb("D_bis", [128, 6, 8], F32)
    cnt8 = ov.sb("D_cnt8", [128, 8], BF16)
    dsc = ov.sb("D_sc", [8, T], F32)
    cat = ov.sb("D_cat", [128, 16, W], BF16)
    w1sl = ov.sb("D_w1", [128, 2, NCH, 512], BF16)
    w2c = ov.sb("D_w2", [128, 2, 4, E], BF16)
    stage = ov.sb("D_stage", [128, 2, E], F32)
    xin = ov.sb("D_xin", [128, 4, W], BF16)
    eo = ov.sb("D_eo", [128, 4, E], BF16)
    tidu = ov.sb("D_tid", [128, 2, 4], U32)
    tscr = ov.sb("D_tscr", [128, 2], F32)
    gf = ov.sb("D_gf", [128, 2, 4], F32)
    xinTb = [nc.alloc_sbuf_tensor_at("D_xinT0", [128, NCH, CAP], BF16,
                                     offset=nc.lookup_mloc(stage).addr),
             nc.alloc_sbuf_tensor_at("D_xinT1", [128, NCH, CAP], BF16,
                                     offset=nc.lookup_mloc(dsc).addr)]
    hT = nc.alloc_sbuf_tensor_at("D_hT", [128, 32, CAP], BF16,
                                 offset=nc.lookup_mloc(xln).addr)
    xbc = xln[:].rearrange("p c t -> p (c t)").bitcast(F32)
    bufB = xbc[0:8, 0:2048]
    bufD = xbc[0:8, 2048:4096]
    alB = [f"xln{c}_{t}" for c in (0, 1) for t in range(NTT)]
    alD = [f"xln{c}_{t}" for c in (2, 3) for t in range(NTT)]
    bufC = dsc[0:8, :]

    xin_d = [nc.dram_tensor(f"D_xin{e}", [CAP, W], BF16, kind="Internal").ap()
             for e in range(NEX)]
    bndreg = {}

    def bnd(e):
        if "r" not in bndreg:
            bndreg["r"] = e.to_reg(CAP - 1)
        return bndreg["r"]

    W1B = g["w1b"].rearrange("ee (c p) h -> ee p c h", p=128)
    W2B = g["w2b"].rearrange("ee (hc p) d -> ee p hc d", p=128)

    def load_w1(ex, s, par):
        P.op("sync", lambda e, ex=ex, s=s, par=par: e.dma_start(
            out=w1sl[:, par, :, :],
            in_=W1B[ex, :, :, s * 512:(s + 1) * 512]), dma=True,
            reads=[f"L1s1{ex}"], writes=[f"D_w1{par}"])

    def load_w2(ex, g4, par):
        P.op("scalar", lambda e, ex=ex, g4=g4, par=par: e.dma_start(
            out=w2c[:, par, :, :], in_=W2B[ex, :, g4 * 4:(g4 + 1) * 4, :]),
            dma=True, reads=[f"L1s2{ex}"], writes=[f"D_w2{par}"])

    load_w1(0, 0, 0)
    load_w2(0, 0, 0)
    P.op("gpsimd", lambda e: e.dma_start(out=gwT[:], in_=ins["gwT"].rearrange(
        "(c p) e8 -> p c e8", p=128)), dma=True, writes=["D_gw"])
    P.op("gpsimd", lambda e: e.dma_start(out=cat[:, :, 1032:1034],
                                         in_=ins["tidhl"]), dma=True,
         writes=[f"D_cat{tc}" for tc in range(16)])
    drain_bg(g, 99)

    for tt in range(NTT):
        st_ = slice(tt * TT, (tt + 1) * TT)
        bank = tt % 2
        for c in range(NCH):
            P.op("tensor", lambda e, c=c, st_=st_, bank=bank:
                 e.matmul(ps[bank][0:8, :], gwT[:, c, :], xln[:, c, st_],
                          start=(c == 0), stop=(c == NCH - 1)),
                 reads=["D_gw"] + xln_reads(c, tt), writes=[f"pb{bank}"])
        P.op("vector", lambda e, st_=st_, bank=bank:
             e.tensor_copy(dsc[:, st_], ps[bank][0:8, :]),
             reads=[f"pb{bank}"], writes=["D_dsc"])

    fill_units = []

    def fill_copy(dst, src_ps, bank, wr, idx):
        if idx % 3 == 2:
            P.op("vector", lambda e: e.tensor_copy(dst, src_ps),
                 reads=[f"pb{bank}"], writes=wr)
        else:
            P.op("scalar", lambda e: e.activation(dst, src_ps, AF.Copy),
                 reads=[f"pb{bank}"], writes=wr)

    for tc in range(16):
        for ec in range(NCH):
            idx = tc * NCH + ec
            bank = 4 + idx % 4

            def u(tc=tc, ec=ec, bank=bank, idx=idx):
                psb = ps[bank][:].bitcast(BF16)
                P.op("tensor", lambda e, psb=psb: e.transpose(
                    psb[:, 0:128], xln[:, ec, tc * 128:(tc + 1) * 128],
                    g["eyeb"][:]),
                    reads=xln_reads(ec, tc // 4) + ["consts"],
                    writes=[f"pb{bank}"])
                fill_copy(cat[:, tc, ec * 128:(ec + 1) * 128], psb[:, 0:128],
                          bank, [f"D_cat{tc}"], idx)
            fill_units.append(u)
    for tc in range(16):
        spar = tc % 2
        for ec in range(NCH):
            idx = tc * NCH + ec
            bank = 4 + idx % 4

            def u(tc=tc, ec=ec, bank=bank, spar=spar, idx=idx):
                P.op("tensor", lambda e: e.transpose(
                    ps[bank][:, 0:128], x[:, ec, tc * 128:(tc + 1) * 128],
                    g["eyef"][:]),
                    reads=[f"rx{ec}_{tc // 4}", "consts"], writes=[f"pb{bank}"])
                fill_copy(stage[:, spar, ec * 128:(ec + 1) * 128],
                          ps[bank][:, 0:128], bank, [f"D_st{spar}"], idx + 1)
            fill_units.append(u)

        def u(tc=tc, spar=spar):
            qeng = "sync" if tc % 2 == 0 else "scalar"
            P.op(qeng, lambda e: e.dma_start(
                out=out_ap[tc * 128:(tc + 1) * 128, :], in_=stage[:, spar, :]),
                dma=True, reads=[f"D_st{spar}"], writes=["outdram"])
        fill_units.append(u)

    def fdrain(k):
        for _ in range(k):
            if fill_units:
                fill_units.pop(0)()

    PT = [f"D_ptok{t}" for t in range(16)]
    for tc in range(16):
        bank = 2 + tc % 2
        P.op("tensor", lambda e, tc=tc, bank=bank: e.transpose(
            ps[bank][:, 0:8], dsc[0:8, tc * 128:(tc + 1) * 128],
            g["eyef"][0:8, 0:8]),
            reads=["D_dsc", "consts"], writes=[f"pb{bank}"])
        P.op("vector", lambda e, tc=tc, bank=bank: e.tensor_copy(
            ptok[:, tc, :], ps[bank][:, 0:8]),
            reads=[f"pb{bank}"], writes=[f"D_ptok{tc}"])
    p3 = ptok[:]
    mxb = dsm[:, 0:16].unsqueeze(2).broadcast_to([128, 16, 8])
    smb = dsm[:, 16:32].unsqueeze(2).broadcast_to([128, 16, 8])
    P.op("vector", lambda e: e.reduce_max(dsm[:, 0:16], p3, axis=AX.X),
         reads=PT, writes=["D_mx"])
    fdrain(8)
    P.op("vector", lambda e: e.tensor_sub(p3, p3, mxb),
         reads=PT + ["D_mx"], writes=PT)
    fdrain(8)
    P.op("scalar", lambda e: e.activation(
        ptok[:].rearrange("p a b -> p (a b)"),
        ptok[:].rearrange("p a b -> p (a b)"), AF.Exp),
        reads=PT, writes=PT)
    P.op("vector", lambda e: e.reduce_sum(dsm[:, 16:32], p3, axis=AX.X),
         reads=PT, writes=["D_sme"])
    fdrain(8)
    P.op("vector", lambda e: e.reciprocal(dsm[:, 16:32], dsm[:, 16:32]),
         reads=["D_sme"], writes=["D_sme"])
    P.op("vector", lambda e: e.tensor_mul(p3, p3, smb),
         reads=PT + ["D_sme"], writes=PT)
    for tc in range(16):
        P.op("vector", lambda e, tc=tc: e.tensor_copy(
            ptok2[:, :, tc:tc + 1], ptok[:, tc, :].unsqueeze(2)),
            reads=[f"D_ptok{tc}"], writes=["D_pt2"])
        P.op("vector", lambda e, tc=tc: e.tensor_copy(
            cat[:, tc, 1024:1032], ptok[:, tc, :]),
            reads=[f"D_ptok{tc}"], writes=[f"D_cat{tc}"])

    for tc in range(16):
        P.op("tensor", lambda e, tc=tc: e.transpose(
            ps[3][0:8, 0:128], ptok[:, tc, :], g["eyef"][:]),
            reads=[f"D_ptok{tc}", "consts"], writes=["pb3"])
        P.op("vector", lambda e, tc=tc: e.tensor_copy(
            dsc[:, tc * 128:(tc + 1) * 128], ps[3][0:8, 0:128]),
            reads=["pb3"], writes=["D_dsc"])

    blo = bis[:, 0, :]
    bhi = bis[:, 1, :]
    bmid = bis[:, 2, :]
    bsel = bis[:, 3, :]
    bt1 = bis[:, 4, :]
    bt2 = bis[:, 5, :]
    midb = bmid.unsqueeze(2).broadcast_to([128, 8, 16])
    P.op("vector", lambda e: e.memset(blo, 0.0), writes=["D_lo"])
    P.op("vector", lambda e: e.memset(bhi, 1.0), writes=["D_hi"])
    for it in range(28):
        P.op("vector", lambda e: e.tensor_add(bmid, blo, bhi),
             reads=["D_lo", "D_hi"], writes=["D_mid"])
        P.op("vector", lambda e: e.tensor_scalar(bmid, bmid, 0.5, None,
                                                 ALU.mult),
             reads=["D_mid"], writes=["D_mid"])
        P.op("vector", lambda e: e.tensor_tensor(cmpb[:], ptok2[:], midb,
                                                 ALU.is_ge),
             reads=["D_pt2", "D_mid"], writes=["D_cmpb"])
        P.op("vector", lambda e: e.reduce_sum(dsm[:, 32:40], cmpb[:],
                                              axis=AX.X),
             reads=["D_cmpb"], writes=["D_cntf"])
        P.op("vector", lambda e: e.tensor_copy(cnt8[:], dsm[:, 32:40]),
             reads=["D_cntf"], writes=["D_cnt8"])
        P.op("tensor", lambda e: e.matmul(ps[2][:, 0:8], g["ln"]["ones128"][:],
                                          cnt8[:], start=True, stop=True),
             reads=["D_cnt8", "consts"], writes=["pb2"])
        fdrain(9)
        P.op("vector", lambda e: e.tensor_scalar(bsel, ps[2][:, 0:8],
                                                 float(CAP), None, ALU.is_ge),
             reads=["pb2"], writes=["D_sel"])
        P.op("vector", lambda e: e.tensor_sub(bt1, bmid, blo),
             reads=["D_mid", "D_lo"], writes=["D_t1"])
        P.op("vector", lambda e: e.tensor_mul(bt1, bt1, bsel),
             reads=["D_t1", "D_sel"], writes=["D_t1"])
        P.op("vector", lambda e: e.tensor_add(blo, blo, bt1),
             reads=["D_lo", "D_t1"], writes=["D_lo"])
        P.op("vector", lambda e: e.tensor_sub(bt2, bhi, bmid),
             reads=["D_hi", "D_mid"], writes=["D_t2"])
        P.op("vector", lambda e: e.tensor_mul(bt2, bt2, bsel),
             reads=["D_t2", "D_sel"], writes=["D_t2"])
        P.op("vector", lambda e: e.tensor_add(bhi, bmid, bt2),
             reads=["D_mid", "D_t2"], writes=["D_hi"])
    lo8 = dsm[0:8, 40:41]
    P.op("vector", lambda e: e.tensor_mul(dsm[0:8, 32:40], blo[0:8, :],
                                          g["eyef"][0:8, 0:8]),
         reads=["D_lo", "consts"], writes=["D_l8t"])
    P.op("vector", lambda e: e.reduce_sum(lo8, dsm[0:8, 32:40], axis=AX.X),
         reads=["D_l8t"], writes=["D_lo8"])

    P.op("vector", lambda e: e.tensor_scalar(bufB[:], dsc[0:8, :],
                                             lo8, None, ALU.is_ge),
         reads=["D_dsc", "D_lo8"], writes=["D_bufB"] + alB)
    P.op("vector", lambda e: e.memset(bufD[:], 0.0),
         writes=["D_bufD"] + alD)
    P.op("vector", lambda e: e.tensor_tensor_scan(bufC[:], bufB[:], bufD[:],
                                                  0.0, ALU.add, ALU.add),
         reads=["D_bufB", "D_bufD", "D_dsc"] + alB + alD,
         writes=["D_bufC", "D_dsc"])
    P.op("vector", lambda e: e.tensor_sub(bufC[:], bufC[:], bufB[:]),
         reads=["D_bufC", "D_bufB"] + alB, writes=["D_bufC"])
    P.op("gpsimd", lambda e: e.iota(bufD[:], [[1, 2048]], base=4096,
                                    channel_multiplier=0,
                                    allow_small_or_imprecise_dtypes=True),
         reads=["D_bufD"], writes=["D_bufD"] + alD)
    P.op("vector", lambda e: e.tensor_mul(bufC[:], bufC[:], bufB[:]),
         reads=["D_bufB", "D_bufC"] + alB, writes=["D_bufC"])
    P.op("vector", lambda e: e.tensor_scalar(bufB[:], bufB[:], -1.0,
                                             1.0, ALU.mult, ALU.add),
         reads=["D_bufB"], writes=["D_bufB"] + alB)
    P.op("vector", lambda e: e.tensor_mul(bufD[:], bufD[:], bufB[:]),
         reads=["D_bufB", "D_bufD"] + alB, writes=["D_bufD"] + alD)
    P.op("vector", lambda e: e.tensor_add(bufC[:], bufC[:], bufD[:]),
         reads=["D_bufD", "D_bufC"] + alD, writes=["D_bufC"])
    for tc in range(16):
        P.op("tensor", lambda e, tc=tc: e.transpose(
            ps[2][0:128, 0:8], bufC[:, tc * 128:(tc + 1) * 128],
            g["eyef"][0:8, 0:8]),
            reads=["D_bufC", "consts"], writes=["pb2"])
        P.op("vector", lambda e, tc=tc: e.tensor_copy(jtok[:, tc, :],
                                                      ps[2][:, 0:8]),
             reads=["pb2"], writes=[f"D_jtok{tc}"])

    def scatter(ex):
        for tc in range(16):
            P.op("gpsimd", lambda e, ex=ex, tc=tc: e.indirect_dma_start(
                out=xin_d[ex], out_offset=IndirectOffsetOnAxis(
                    ap=jtok[:, tc, ex:ex + 1], axis=0),
                in_=cat[:, tc, :], in_offset=None,
                bounds_check=bnd(e), oob_is_err=False), dma=True,
                reads=[f"D_jtok{tc}", f"D_cat{tc}"], writes=[f"D_xind{ex}"])

    scatter(0)
    fdrain(999)

    def gather(ex):
        P.op("sync", lambda e, ex=ex: e.dma_start(
            out=xin[:], in_=xin_d[ex].rearrange("(jc p) w -> p jc w", p=128)),
            dma=True, reads=[f"D_xind{ex}"], writes=["D_xinb"])

    def transpose_units(ex):
        par = ex % 2
        xinT = xinTb[par]
        units = []
        for dc in range(NCH):
            for jc in range(4):
                bank = 2 + (dc * 4 + jc) % 2
                psb = ps[bank][:].bitcast(BF16)
                first = (dc == 0 and jc == 0)
                alias = (["D_st0", "D_st1"] if par == 0 else
                         ["D_dsc", "D_bufC"]) if ex < 2 and first else []

                def u(jc=jc, dc=dc, psb=psb, bank=bank, par=par, xinT=xinT,
                      alias=alias):
                    P.op("tensor", lambda e: e.transpose(
                        psb[:, 0:128], xin[:, jc, dc * 128:(dc + 1) * 128],
                        g["eyeb"][:]),
                        reads=["D_xinb", "consts"], writes=[f"pb{bank}"])
                    P.op("vector", lambda e: e.tensor_copy(
                        xinT[:, dc, jc * 128:(jc + 1) * 128], psb[:, 0:128]),
                        reads=[f"pb{bank}"],
                        writes=[f"D_xinT{par}_{dc}"] + alias)
                units.append(u)
        return units

    def extract(ex, par):
        for jc in range(4):
            P.op("vector", lambda e, jc=jc, ex=ex, par=par: e.tensor_copy(
                gf[:, par, jc:jc + 1], xin[:, jc, 1024 + ex:1025 + ex]),
                reads=["D_xinb"], writes=[f"D_gf{par}"])
            P.op("vector", lambda e, jc=jc: e.tensor_copy(
                tscr[:, 0:1], xin[:, jc, 1032:1033]),
                reads=["D_xinb"], writes=["D_tscr"])
            P.op("vector", lambda e, jc=jc: e.tensor_copy(
                tscr[:, 1:2], xin[:, jc, 1033:1034]),
                reads=["D_xinb"], writes=["D_tscr"])
            P.op("vector", lambda e: e.tensor_scalar(
                tscr[:, 0:1], tscr[:, 0:1], 256.0, None, ALU.mult),
                reads=["D_tscr"], writes=["D_tscr"])
            P.op("vector", lambda e: e.tensor_add(
                tscr[:, 0:1], tscr[:, 0:1], tscr[:, 1:2]),
                reads=["D_tscr"], writes=["D_tscr"])
            P.op("vector", lambda e, jc=jc, par=par: e.tensor_copy(
                tidu[:, par, jc:jc + 1], tscr[:, 0:1]),
                reads=["D_tscr"], writes=[f"D_tid{par}"])

    gather(0)
    for u in transpose_units(0):
        u()
    extract(0, 0)
    scatter(1)

    w1ctr = 0
    for ex in range(NEX):
        par = ex % 2
        if ex + 1 < NEX:
            gather(ex + 1)
        if ex + 2 < NEX:
            scatter(ex + 2)
        tun = transpose_units(ex + 1) if ex + 1 < NEX else []
        xinT = xinTb[par]
        for s in range(8):
            wpar = w1ctr % 2
            if s < 7:
                load_w1(ex, s + 1, (w1ctr + 1) % 2)
            elif ex < NEX - 1:
                load_w1(ex + 1, 0, (w1ctr + 1) % 2)
            if s >= 2:
                for _ in range(6):
                    if tun:
                        tun.pop(0)()
            for hc in range(4):
                gh = s * 4 + hc
                bank = gh % 2
                for dc in range(NCH):
                    P.op("tensor", lambda e, dc=dc, hc=hc, wpar=wpar, bank=bank,
                         xinT=xinT, par=par:
                         e.matmul(ps[bank][:],
                                  w1sl[:, wpar, dc, hc * 128:(hc + 1) * 128],
                                  xinT[:, dc, :], start=(dc == 0),
                                  stop=(dc == NCH - 1)),
                         reads=[f"D_w1{wpar}", f"D_xinT{par}_{dc}"],
                         writes=[f"pb{bank}"])
                P.op("scalar", lambda e, gh=gh, bank=bank: e.activation(
                    hT[:, gh, :], ps[bank][:], AF.Gelu),
                    reads=[f"pb{bank}"],
                    writes=[f"D_h{gh}", f"xln{gh // 4}_{gh % 4}"])
            w1ctr += 1
        for g4 in range(8):
            wpar2 = g4 % 2
            if g4 < 7:
                load_w2(ex, g4 + 1, (g4 + 1) % 2)
            elif ex < NEX - 1:
                load_w2(ex + 1, 0, (g4 + 1) % 2)
            if g4 == 6 and ex + 1 < NEX:
                extract(ex + 1, (ex + 1) % 2)
            for ghl in range(4):
                gh = g4 * 4 + ghl
                for jc in range(4):
                    for dh in range(2):
                        bank = jc * 2 + dh
                        P.op("tensor", lambda e, gh=gh, ghl=ghl, jc=jc, dh=dh,
                             wpar2=wpar2, bank=bank:
                             e.matmul(ps[bank][:],
                                      hT[:, gh, jc * 128:(jc + 1) * 128],
                                      w2c[:, wpar2, ghl, dh * 512:(dh + 1) * 512],
                                      start=(gh == 0), stop=(gh == 31)),
                             reads=[f"D_w2{wpar2}", f"D_h{gh}"],
                             writes=[f"pb{bank}"])
        for jc in range(4):
            for dh in range(2):
                bank = jc * 2 + dh
                P.op("vector", lambda e, jc=jc, dh=dh, bank=bank, par=par:
                     e.tensor_scalar(eo[:, jc, dh * 512:(dh + 1) * 512],
                                     ps[bank][:], gf[:, par, jc:jc + 1], None,
                                     ALU.mult),
                     reads=[f"pb{bank}", f"D_gf{par}"], writes=[f"D_eo{jc}"])
        for jc in range(4):
            P.op("gpsimd", lambda e, jc=jc, par=par: e.indirect_dma_start(
                out=out_ap, out_offset=IndirectOffsetOnAxis(
                    ap=tidu[:, par, jc:jc + 1], axis=0),
                in_=eo[:, jc, :], in_offset=None,
                oob_is_err=True, compute_op=ALU.add), dma=True,
                reads=[f"D_eo{jc}", f"D_tid{par}", "outdram"],
                writes=["outdram"])



def _build(phases):
    nc = bass.Bass("TRN2", target_bir_lowering=False,
                   dynamic_dma_scratch_size=16384)
    ins = {}

    def din(name, shape):
        ins[name] = nc.dram_tensor(name, shape, F32, kind="ExternalInput").ap()

    din("xT", [E, T])
    din("peT", [D0, T])
    din("cmask", [128, 896])
    din("eye", [128, 128])
    if "A" in phases:
        din("wqkvT", [E, 3 * H0 * D0])
        din("woT", [H0 * D0, E])
    if "B" in phases:
        din("wgate", [E, 8])
        din("bgate", [1, 8])
        din("alpha", [1, 1])
        din("W1m", [E, 4096])
        din("W2m", [4096, E])
        din("b1mcm", [128, 32])
        din("b2mcm", [128, 8])
        din("W1e", [8, E, 4096])
        din("W2e", [8, 4096, E])
        din("b1e", [8, 4096])
        din("b2e", [8, E])
    if "C" in phases:
        din("Wqc", [E, E])
        din("Wkc", [E, E])
        din("Wvc", [E, E])
        din("Wo1", [E, E])
    if "D" in phases:
        din("gwT", [E, 8])
        din("tidhl", [128, 16, 2])
        din("Ew1", [8, E, 4096])
        din("Ew2", [8, 4096, E])
    if "B" in phases or "D" in phases:
        din("iotachf", [128, 32])
        din("iotagf", [128, 1])
    last = phases[-1]
    if last == "D":
        out_ap = nc.dram_tensor("out", [T, E], F32, kind="ExternalOutput").ap()
    else:
        out_ap = nc.dram_tensor("xTo", [E, T], F32, kind="ExternalOutput").ap()

    with ExitStack() as ctx:
        def sb(name, shape, dt, side="left"):
            return ctx.enter_context(nc.sbuf_tensor(name, shape, dt, side=side))

        P = Prog(nc, ctx)
        g = {"ins": ins, "sb": sb}

        x = sb("x", [128, NCH, T], F32, side="right")
        g["x"] = x
        ones128 = sb("ones128", [128, 128], BF16)
        g["eyeb"] = sb("eyeb", [128, 128], BF16)
        g["eyef"] = sb("eyef", [128, 128], F32)
        g["cbias"] = sb("cbias", [128, 2], F32)
        xln = sb("xln", [128, NCH, T], BF16)
        g["xln"] = xln
        ln_stt = sb("ln_stt", [128, TT], F32)
        g["ln"] = {
            "ones128": ones128,
            "sq2": ln_stt[:].bitcast(BF16),
            "stm": sb("ln_stm", [128, TT], F32),
            "stv": sb("ln_stv", [128, TT], F32),
            "sttmp": ln_stt,
        }
        g["bmsum"] = sb("bmsum", [128, NCH], F32)
        g["jtok"] = sb("jtok", [128, 16, 8], U32)
        g["psum"] = [ctx.enter_context(nc.psum_tensor(f"pb{i}", [128, TT], F32))
                     for i in range(8)]
        if "B" in phases or "D" in phases:
            g["iotaf"] = sb("iotaf", [128, 32], F32)
            g["iotag"] = sb("iotag", [128, 1], F32)
            P.op("sync", lambda e: e.dma_start(out=g["iotaf"][:],
                                               in_=ins["iotachf"]), dma=True,
                 writes=["consts"])
            P.op("sync", lambda e: e.dma_start(out=g["iotag"][:],
                                               in_=ins["iotagf"]), dma=True,
                 writes=["consts"])
        g["bg"] = []
        if "D" in phases:
            w1b = nc.dram_tensor("L1w1b", [8, E, 4096], BF16,
                                 kind="Internal").ap()
            w2b = nc.dram_tensor("L1w2b", [8, 4096, E], BF16,
                                 kind="Internal").ap()
            g["w1b"], g["w2b"] = w1b, w2b
            for ex in range(8):
                for half in range(2):
                    hs = slice(half * 512, (half + 1) * 512)
                    hs2 = slice(half * 2048, (half + 1) * 2048)
                    g["bg"].append(lambda P=P, ex=ex, hs=hs: P.op(
                        "gpsimd", lambda e: e.dma_start(
                            out=w1b[ex, hs, :], in_=ins["Ew1"][ex, hs, :]),
                        dma=True, writes=[f"L1s1{ex}"]))
                    g["bg"].append(lambda P=P, ex=ex, hs2=hs2: P.op(
                        "gpsimd", lambda e: e.dma_start(
                            out=w2b[ex, hs2, :], in_=ins["Ew2"][ex, hs2, :]),
                        dma=True, writes=[f"L1s2{ex}"]))
        arena_bytes = (nc.sbuf_bytes_remaining - 64) // 32 * 32
        arena = sb("arena", [128, arena_bytes], mybir.dt.int8)
        g["ov"] = Overlay(nc, arena)

        P.op("vector", lambda e: e.memset(ones128[:], 1.0), writes=["consts"])
        P.op("vector", lambda e: e.memset(g["cbias"][:, 0:1], -C_L0),
             writes=["consts"])
        P.op("vector", lambda e: e.memset(g["cbias"][:, 1:2], -C_L1),
             writes=["consts"])
        P.op("gpsimd", lambda e: e.dma_start(out=g["eyeb"][:], in_=ins["eye"]),
             dma=True, writes=["consts"])
        P.op("sync", lambda e: e.dma_start(out=g["eyef"][:], in_=ins["eye"]),
             dma=True, writes=["consts"])
        xTr = ins["xT"].rearrange("(c p) t -> p c t", p=128)
        for tt in range(NTT):
            s = slice(tt * TT, (tt + 1) * TT)
            P.op("sync", lambda e, s=s: e.dma_start(out=x[:, :, s],
                                                    in_=xTr[:, :, s]), dma=True,
                 writes=[f"rx{c}_{tt}" for c in range(NCH)])

        for i, ph in enumerate(phases):
            if ph == "A":
                emit_ln(P, g)
                g["ov"].reset()
                emit_phaseA(P, ctx, g)
            elif ph == "B":
                emit_ln(P, g, gate_msum=g["bmsum"], gtag="B")
                P.barrier()
                g["ov"].reset()
                emit_phaseB(P, ctx, g)
            elif ph == "C":
                emit_ln(P, g)
                P.barrier()
                g["ov"].reset()
                emit_phaseC(P, ctx, g)
            elif ph == "D":
                emit_ln(P, g)
                P.barrier()
                g["ov"].reset()
                emit_phaseD(P, ctx, g, out_ap)
            else:
                raise NotImplementedError(ph)

        if last != "D":
            P.barrier()
            outr = out_ap.rearrange("(c p) t -> p c t", p=128)
            P.op("sync", lambda e: e.dma_start(out=outr, in_=x[:]), dma=True,
                 reads=[f"rx{c}_{t}" for c in range(NCH) for t in range(NTT)],
                 writes=["out"])

        block = ctx.enter_context(nc.Block())
        P.emit(block)
    return nc



def _host_consts():
    pos = np.arange(T, dtype=np.float32)[:, None]
    div = np.exp(np.arange(0, D0, 2, dtype=np.float32) * (-np.log(10000.0) / D0))
    ang = pos * div
    pe = np.stack([np.sin(ang), np.cos(ang)], axis=-1).reshape(T, D0)
    peT = np.ascontiguousarray(pe.T.astype(np.float32))
    r = np.arange(128)[:, None]
    w = np.arange(896)[None, :]
    cmask = ((w - 384) >= r).astype(np.float32)
    eye = np.eye(128, dtype=np.float32)
    return {"peT": peT, "cmask": cmask, "eye": eye}


def _host_shared(inputs, phases):
    sh = dict(_host_consts())
    f32 = lambda a: np.ascontiguousarray(np.asarray(a, np.float32))
    if "A" in phases:
        sh["wqkvT"] = f32(np.asarray(inputs["L0_attn_Wqkv"]).T)
        sh["woT"] = f32(np.asarray(inputs["L0_attn_Wo"]).T)
    if "B" in phases:
        sh["wgate"] = f32(inputs["L0_ffn_w_gate"])
        sh["bgate"] = f32(inputs["L0_ffn_b_gate"]).reshape(1, 8)
        sh["alpha"] = f32(inputs["L0_ffn_alpha"]).reshape(1, 1)
        sh["W1m"] = f32(inputs["L0_ffn_W1m"])
        sh["W2m"] = f32(inputs["L0_ffn_W2m"])
        sh["b1mcm"] = f32(np.asarray(inputs["L0_ffn_b1m"]).reshape(32, 128).T)
        sh["b2mcm"] = f32(np.asarray(inputs["L0_ffn_b2m"]).reshape(8, 128).T)
        sh["W1e"] = f32(inputs["L0_ffn_W1e"])
        sh["W2e"] = f32(inputs["L0_ffn_W2e"])
        sh["b1e"] = f32(inputs["L0_ffn_b1e"])
        sh["b2e"] = f32(inputs["L0_ffn_b2e"])
    if "C" in phases:
        cat = lambda w: f32(np.asarray(w).transpose(1, 0, 2).reshape(E, E))
        sh["Wqc"] = cat(inputs["L1_attn_W_q"])
        sh["Wkc"] = cat(inputs["L1_attn_W_k"])
        sh["Wvc"] = cat(inputs["L1_attn_W_v"])
        sh["Wo1"] = f32(inputs["L1_attn_W_o"])
    if "D" in phases:
        tid = (np.arange(16)[None, :] * 128 + np.arange(128)[:, None])
        sh["tidhl"] = np.ascontiguousarray(
            np.stack([tid // 256, tid % 256], axis=-1).astype(np.float32))
        sh["gwT"] = f32(np.asarray(inputs["L1_ffn_gate_w"]).T)
        sh["Ew1"] = f32(inputs["L1_ffn_w1"])
        sh["Ew2"] = f32(inputs["L1_ffn_w2"])
    if "B" in phases or "D" in phases:
        p = np.arange(128, dtype=np.float32)[:, None]
        gg = np.arange(32, dtype=np.float32)[None, :]
        sh["iotachf"] = np.ascontiguousarray(gg * 128 + p)
        sh["iotagf"] = np.ascontiguousarray(p)
    return sh


def kernel(**inputs):
    x = np.asarray(inputs["L0_attn_x"], np.float32)
    phase_groups = [_PHASES] if _FUSED else list(_PHASES)
    xcur = [np.ascontiguousarray(x[c].T) for c in range(8)]
    out = None
    for ph in phase_groups:
        sh = _host_shared(inputs, ph)
        nc = _build(ph)
        in_maps = [dict(sh, xT=xcur[c]) for c in range(8)]
        res = run_bass_kernel_spmd(nc, in_maps, core_ids=list(range(8)))
        if ph[-1] == "D":
            out = np.stack([np.asarray(res.results[c]["out"]) for c in range(8)])
        else:
            xcur = [np.ascontiguousarray(np.asarray(res.results[c]["xTo"]))
                    for c in range(8)]
    if out is None:
        out = np.stack([xcur[c].T for c in range(8)])
    return out


# version 14
# speedup vs baseline: 1.1967x; 1.1967x over previous
import math
from contextlib import ExitStack

import numpy as np

import ml_dtypes

import concourse.bass as bass
import concourse.mybir as mybir
from concourse.bass_core_rust import IndirectOffsetOnAxis
from concourse.bass_utils import run_bass_kernel_spmd

F32 = mybir.dt.float32
BF16 = mybir.dt.bfloat16
U32 = mybir.dt.uint32
AF = mybir.ActivationFunctionType
ALU = mybir.AluOpType
AX = mybir.AxisListType

T = 2048
E = 1024
NCH = 8
NTT = 4
TT = 512
H0, D0 = 8, 512
C_L0 = 24.0
C_L1 = 4.0

_FUSED = True
_PHASES = "ABCD"


_NQSEM = 12
_QUEUES = ["qsync", "qscalar", "qgp"]
_STREAMS = (["tensor", "vector", "scalar", "gpsimd", "sync"]
            + [f"{q}{i}" for q in _QUEUES for i in range(_NQSEM)])
_QUEUE_OF = {"sync": "qsync", "scalar": "qscalar", "gpsimd": "qgp"}


class Prog:
    def __init__(self, nc, ctx):
        self.nc = nc
        self.sem = {n: ctx.enter_context(nc.semaphore("sem_" + n))
                    for n in _STREAMS}
        self.cnt = {n: 0 for n in _STREAMS}
        self.qrot = {q: 0 for q in _QUEUES}
        self.hwm = {}
        self.writers = {}
        self.readers = {}
        self.items = []

    def op(self, engine, emit, reads=(), writes=(), dma=False):
        waits = {}

        def need(d):
            for s, v in (d or {}).items():
                if waits.get(s, 0) < v:
                    waits[s] = v

        for r in reads:
            need(self.writers.get(r))
        for w in writes:
            need(self.writers.get(w))
            need(self.readers.get(w))
        if dma:
            q = _QUEUE_OF[engine]
            i = self.qrot[q]
            self.qrot[q] = (i + 1) % _NQSEM
            stream, amt = f"{q}{i}", 16
            if self.cnt[stream] and waits.get(stream, 0) < self.cnt[stream]:
                waits[stream] = self.cnt[stream]
        else:
            stream, amt = engine, 1
        real = {}
        for s, v in waits.items():
            if engine == "tensor" and s == "tensor":
                continue
            k = (engine, s)
            if self.hwm.get(k, 0) >= v:
                continue
            self.hwm[k] = v
            real[s] = v
        self.cnt[stream] += amt
        sig = (stream, self.cnt[stream])
        for w in writes:
            d = self.writers.setdefault(w, {})
            d[sig[0]] = sig[1]
            self.readers.pop(w, None)
        for r in reads:
            d = self.readers.setdefault(r, {})
            if d.get(sig[0], 0) < sig[1]:
                d[sig[0]] = sig[1]
        self.items.append((engine, real, emit, stream, amt))

    def barrier(self):
        snapshot = dict(self.cnt)
        for eng in ["tensor", "vector", "scalar", "gpsimd", "sync"]:
            real = {}
            for s, v in snapshot.items():
                if v == 0 or eng == s:
                    continue
                k = (eng, s)
                if self.hwm.get(k, 0) >= v:
                    continue
                self.hwm[k] = v
                real[s] = v
            if real:
                self.items.append((eng, real, None, None, 0))

    def emit(self, block):
        final = dict(self.cnt)

        def body_for(name):
            def body(e):
                for engine, waits, emitf, stream, amt in self.items:
                    if engine != name:
                        continue
                    for s in sorted(waits):
                        e.wait_ge(self.sem[s], waits[s])
                    if emitf is not None:
                        emitf(e).then_inc(self.sem[stream], amt)
                if name == "gpsimd":
                    for s, v in final.items():
                        if v and s != "gpsimd":
                            e.wait_ge(self.sem[s], v)
            return body

        block.tensor(body_for("tensor"))
        block.vector(body_for("vector"))
        block.scalar(body_for("scalar"))
        block.gpsimd(body_for("gpsimd"))
        block.sync(body_for("sync"))


class Overlay:

    def __init__(self, nc, arena):
        self.nc = nc
        self.base = nc.lookup_mloc(arena).addr
        self.size = int(np.prod(arena.shape[1:]))
        self.off = 0
        self.peak = 0

    def reset(self):
        self.off = 0

    def mark(self):
        return self.off

    def rollback(self, m):
        self.off = m

    def sb(self, name, shape, dt):
        nb = int(np.prod(shape[1:])) * mybir.dt.size(dt)
        nb = (nb + 31) // 32 * 32
        assert self.off + nb <= self.size, (
            f"overlay overflow allocating {name}: {self.off}+{nb} > {self.size}")
        t = self.nc.alloc_sbuf_tensor_at(name, shape, dt,
                                         offset=self.base + self.off)
        self.off += nb
        self.peak = max(self.peak, self.off)
        return t



def emit_ln(P, g, gate_msum=None, gtag=""):
    x, xln = g["x"], g["xln"]
    ln = g["ln"]
    ones = ln["ones128"]
    sq2, stm, stv, stt = ln["sq2"], ln["stm"], ln["stv"], ln["sttmp"]
    ps = g["psum"]
    for tt in range(NTT):
        s = slice(tt * TT, (tt + 1) * TT)
        for c in range(NCH):
            P.op("vector",
                 lambda e, c=c, s=s: e.tensor_copy(xln[:, c, s], x[:, c, s]),
                 reads=[f"rx{c}_{tt}"], writes=[f"xb{c}_{tt}", f"xln{c}_{tt}"])
    for tt in range(NTT):
        s = slice(tt * TT, (tt + 1) * TT)
        for c in range(NCH):
            sq = sq2[:, (c % 2) * TT:(c % 2 + 1) * TT]
            P.op("scalar",
                 lambda e, c=c, s=s, sq=sq: e.activation(sq, xln[:, c, s],
                                                         AF.Square),
                 reads=[f"xb{c}_{tt}"], writes=[f"lnsq{c % 2}"])
            P.op("tensor",
                 lambda e, c=c, s=s: e.matmul(ps[0][:], ones[:], xln[:, c, s],
                                              start=(c == 0), stop=(c == NCH - 1)),
                 reads=[f"xb{c}_{tt}", "consts"], writes=["pb0"])
            P.op("tensor",
                 lambda e, c=c, sq=sq: e.matmul(ps[1][:], ones[:], sq,
                                                start=(c == 0), stop=(c == NCH - 1)),
                 reads=[f"lnsq{c % 2}", "consts"], writes=["pb1"])
        P.op("vector", lambda e: e.tensor_scalar(stm[:], ps[0][:], 1.0 / E, None,
                                                 ALU.mult),
             reads=["pb0"], writes=["lnstm"])
        P.op("vector", lambda e: e.tensor_scalar(stv[:], ps[1][:], 1.0 / E, 1e-5,
                                                 ALU.mult, ALU.add),
             reads=["pb1"], writes=["lnstv"])
        P.op("vector", lambda e: e.tensor_mul(stt[:], stm[:], stm[:]),
             reads=["lnstm", "pb1"], writes=["lnstt", "lnsq0", "lnsq1"])
        P.op("vector", lambda e: e.tensor_sub(stv[:], stv[:], stt[:]),
             reads=["lnstv", "lnstt"], writes=["lnstv"])
        P.op("scalar", lambda e: e.activation(stt[:], stv[:], AF.Ln),
             reads=["lnstv"], writes=["lnstt"])
        P.op("scalar", lambda e: e.activation(stv[:], stt[:], AF.Exp, scale=-0.5),
             reads=["lnstt"], writes=["lnstv"])
        P.op("vector", lambda e: e.tensor_mul(stm[:], stm[:], stv[:]),
             reads=["lnstm", "lnstv"], writes=["lnstm"])
        for c in range(NCH):
            P.op("vector",
                 lambda e, c=c, s=s: e.tensor_mul(xln[:, c, s], x[:, c, s], stv[:]),
                 reads=[f"rx{c}_{tt}", "lnstv"],
                 writes=[f"xln{c}_{tt}", f"xb{c}_{tt}"])
            P.op("vector",
                 lambda e, c=c, s=s: e.tensor_sub(xln[:, c, s], xln[:, c, s], stm[:]),
                 reads=["lnstm"],
                 writes=[f"xln{c}_{tt}"])
    if gate_msum is not None:
        for c in range(NCH):
            P.op("vector",
                 lambda e, c=c: e.reduce_sum(gate_msum[:, c:c + 1], xln[:, c, :],
                                             axis=AX.X),
                 reads=[f"xln{c}_{t}" for t in range(NTT)],
                 writes=[f"{gtag}gmsum{c}"])


def drain_bg(g, k):
    for _ in range(k):
        if g["bg"]:
            g["bg"].pop(0)()


def xln_reads(c, tt=None):
    if tt is None:
        return [f"xln{c}_{t}" for t in range(NTT)]
    return [f"xln{c}_{tt}"]


def rx_w(c, tt=None):
    if tt is None:
        return [f"rx{c}_{t}" for t in range(NTT)]
    return [f"rx{c}_{tt}"]



def emit_phaseA(P, ctx, g):
    sb = g["ov"].sb
    x, xln, ps = g["x"], g["xln"], g["psum"]

    wq = sb("A_wq", [128, NCH, 512], BF16)
    wk = sb("A_wk", [128, NCH, 512], BF16)
    wv = sb("A_wv", [128, NCH, 512], BF16)
    wo = sb("A_wo", [128, 4, 1024], BF16)
    pe = sb("A_pe", [128, 4, T], BF16)
    msk = sb("A_msk", [128, 896], BF16)
    q = sb("A_q", [128, 4, T], BF16)
    k = sb("A_k", [128, 4, T], BF16)
    v = sb("A_v", [128, 16, TT], BF16)
    pbuf = sb("A_p", [128, 2, TT], BF16)
    att = sb("A_att", [128, 4, TT], BF16)
    rrep = pbuf[:].bitcast(F32).rearrange("p a b -> p (a b)")

    peT = g["ins"]["peT"].rearrange("(c p) t -> p c t", p=128)
    P.op("scalar", lambda e: e.dma_start(out=pe[:], in_=peT), dma=True,
         writes=["A_pe"])
    P.op("scalar", lambda e: e.dma_start(out=msk[:], in_=g["ins"]["cmask"]),
         dma=True, writes=["A_msk"])

    wqkvT = g["ins"]["wqkvT"].rearrange("(c p) f -> p c f", p=128)
    woT = g["ins"]["woT"].rearrange("(c p) e2 -> p c e2", p=128)

    for h in range(H0):
        drain_bg(g, 2)
        base = h * 3 * D0
        P.op("gpsimd", lambda e, base=base: e.dma_start(
            out=wq[:], in_=wqkvT[:, :, base:base + 512]), dma=True,
            writes=["A_wq"])
        P.op("gpsimd", lambda e, base=base: e.dma_start(
            out=wk[:], in_=wqkvT[:, :, base + 512:base + 1024]), dma=True,
            writes=["A_wk"])
        P.op("gpsimd", lambda e, base=base: e.dma_start(
            out=wv[:], in_=wqkvT[:, :, base + 1024:base + 1536]), dma=True,
            writes=["A_wv"])
        P.op("gpsimd", lambda e, h=h: e.dma_start(
            out=wo[:], in_=woT[:, 4 * h:4 * h + 4, :]), dma=True,
            writes=["A_wo"])

        for d4 in range(4):
            for tt in range(NTT):
                s = slice(tt * TT, (tt + 1) * TT)
                for name, wt, dst, bank in (("q", wq, q, 2 + (d4 * NTT + tt) % 2),
                                            ("k", wk, k, 4 + (d4 * NTT + tt) % 2)):
                    for c in range(NCH):
                        P.op("tensor", lambda e, c=c, d4=d4, s=s, wt=wt, bank=bank:
                             e.matmul(ps[bank][:], wt[:, c, d4 * 128:(d4 + 1) * 128],
                                      xln[:, c, s], start=(c == 0),
                                      stop=(c == NCH - 1)),
                             reads=[f"A_w{name}"] + xln_reads(c, tt),
                             writes=[f"pb{bank}"])
                    P.op("vector", lambda e, d4=d4, s=s, dst=dst, bank=bank:
                         e.tensor_add(dst[:, d4, s], ps[bank][:], pe[:, d4, s]),
                         reads=[f"pb{bank}", "A_pe"],
                         writes=[f"A_{name}{d4}_{tt}"])

        for tc in range(16):
            bank = 6 + tc % 2
            for c in range(NCH):
                P.op("tensor", lambda e, c=c, tc=tc, bank=bank:
                     e.matmul(ps[bank][:], xln[:, c, tc * 128:(tc + 1) * 128],
                              wv[:, c, :], start=(c == 0), stop=(c == NCH - 1)),
                     reads=["A_wv"] + xln_reads(c, tc // 4),
                     writes=[f"pb{bank}"])
            P.op("vector", lambda e, tc=tc, bank=bank:
                 e.tensor_copy(v[:, tc, :], ps[bank][:]),
                 reads=[f"pb{bank}"], writes=[f"A_v{tc}"])

        def a_sexp(qj, ci):
            sqj = slice(qj * TT, (qj + 1) * TT)
            sbank = 4 + ci % 2
            pp = ci % 2
            for d4 in range(4):
                P.op("tensor", lambda e, ci=ci, d4=d4, sqj=sqj, sbank=sbank:
                     e.matmul(ps[sbank][:], k[:, d4, ci * 128:(ci + 1) * 128],
                              q[:, d4, sqj], start=(d4 == 0), stop=(d4 == 3)),
                     reads=[f"A_k{d4}_{ci // 4}", f"A_q{d4}_{qj}"],
                     writes=[f"pb{sbank}"])
            P.op("scalar", lambda e, pp=pp, sbank=sbank:
                 e.activation(pbuf[:, pp, :], ps[sbank][:], AF.Exp,
                              bias=g["cbias"][:, 0:1],
                              scale=1.0 / math.sqrt(D0)),
                 reads=[f"pb{sbank}", "consts"],
                 writes=[f"A_P{pp}", "A_rrep"])
            if ci >= 4 * qj:
                o = ci - 4 * qj
                ms = slice(384 - 128 * o, 896 - 128 * o)
                P.op("vector", lambda e, pp=pp, ms=ms:
                     e.tensor_mul(pbuf[:, pp, :], pbuf[:, pp, :], msk[:, ms]),
                     reads=["A_msk"], writes=[f"A_P{pp}"])

        def a_pv(qj, ci):
            nck = 4 * qj + 4
            pp = ci % 2
            for d4 in range(4):
                P.op("tensor", lambda e, ci=ci, d4=d4, pp=pp, nck=nck:
                     e.matmul(ps[d4][:], v[:, ci, d4 * 128:(d4 + 1) * 128],
                              pbuf[:, pp, :], start=(ci == 0),
                              stop=(ci == nck - 1)),
                     reads=[f"A_v{ci}", f"A_P{pp}"], writes=[f"pb{d4}"])
            P.op("tensor", lambda e, ci=ci, pp=pp, nck=nck:
                 e.matmul(ps[6][:], g["ln"]["ones128"][:], pbuf[:, pp, :],
                          start=(ci == 0), stop=(ci == nck - 1)),
                 reads=[f"A_P{pp}", "consts"], writes=["pb6"])

        for qj in range(NTT):
            sqj = slice(qj * TT, (qj + 1) * TT)
            nck = 4 * qj + 4
            a_sexp(qj, 0)
            for ci in range(1, nck):
                a_sexp(qj, ci)
                a_pv(qj, ci - 1)
            a_pv(qj, nck - 1)
            P.op("vector", lambda e: e.reciprocal(rrep[:], ps[6][:]),
                 reads=["pb6"], writes=["A_rrep"])
            for d4 in range(4):
                P.op("vector", lambda e, d4=d4:
                     e.tensor_mul(att[:, d4, :], ps[d4][:], rrep[:]),
                     reads=[f"pb{d4}", "A_rrep"], writes=[f"A_att{d4}"])
            for ec in range(NCH):
                bank = 6 + ec % 2
                for f4 in range(4):
                    P.op("tensor", lambda e, f4=f4, ec=ec, bank=bank:
                         e.matmul(ps[bank][:],
                                  wo[:, f4, ec * 128:(ec + 1) * 128],
                                  att[:, f4, :], start=(f4 == 0),
                                  stop=(f4 == 3)),
                         reads=["A_wo", f"A_att{f4}"], writes=[f"pb{bank}"])
                P.op("vector", lambda e, ec=ec, sqj=sqj, bank=bank:
                     e.tensor_add(x[:, ec, sqj], x[:, ec, sqj], ps[bank][:]),
                     reads=[f"pb{bank}"], writes=[f"rx{ec}_{qj}"])



def emit_phaseB(P, ctx, g):
    sb = g["ov"].sb
    x, xln, ps = g["x"], g["xln"], g["psum"]
    ins = g["ins"]

    w1sl = sb("B_w1", [128, 2, NCH, 1024], BF16)
    w2sl = sb("B_w2", [128, 2, NCH, 1024], BF16)
    hT = sb("B_h", [128, NCH, T], BF16)
    wg = sb("B_wg", [128, NCH, 8], F32)
    msum = g["bmsum"]
    gsc = sb("B_gsc", [1, 64], F32)
    bgt = sb("B_bg", [1, 8], F32)
    alv = sb("B_al", [1, 1], F32)
    b1m = sb("B_b1m", [128, 32], F32)
    b2m = sb("B_b2m", [128, 8], F32)
    b1x = sb("B_b1x", [128, 2, 32], F32)
    b2x = sb("B_b2x", [128, 2, 8], F32)
    bcomb = sb("B_bc", [128, 8], F32)
    crep = sb("B_crep", [128, 3], F32)
    selrep = sb("B_selrep", [128, 2], F32)
    iotaf, iotag = g["iotaf"], g["iotag"]
    idxf = sb("B_idxf", [128, 33], F32)
    w1idx = sb("B_w1idx", [128, 2, 8], U32)
    w2idx = sb("B_w2idx", [128, 2, 32], U32)
    bidxf = sb("B_bidxf", [32, 1], F32)
    onesr = sb("B_onesr", [1, 128], F32)
    bidx = sb("B_bidx", [32, 1], U32)
    btmp = sb("B_btmp", [32, 128], F32)

    P.op("vector", lambda e: e.memset(onesr[:], 1.0), writes=["B_onesr"])
    P.op("sync", lambda e: e.dma_start(out=wg[:], in_=ins["wgate"].rearrange(
        "(c p) e8 -> p c e8", p=128)), dma=True, writes=["B_wg"])
    P.op("sync", lambda e: e.dma_start(out=bgt[:], in_=ins["bgate"]), dma=True,
         writes=["B_bg"])
    P.op("sync", lambda e: e.dma_start(out=alv[:], in_=ins["alpha"]), dma=True,
         writes=["B_al"])
    P.op("sync", lambda e: e.dma_start(out=b1m[:], in_=ins["b1mcm"]), dma=True,
         writes=["B_b1m"])
    P.op("sync", lambda e: e.dma_start(out=b2m[:], in_=ins["b2mcm"]), dma=True,
         writes=["B_b2m"])

    W1mv = ins["W1m"].rearrange("(c p) hh -> p c hh", p=128)
    W2mv = ins["W2m"].rearrange("(gg p) ee -> p gg ee", p=128)
    W1ef = ins["W1e"].rearrange("e d h -> (e d) h")
    W2ef = ins["W2e"].rearrange("e h d -> (e h) d")

    def load_w1(path, s, par):
        if path == "m":
            P.op("gpsimd", lambda e, s=s, par=par: e.dma_start(
                out=w1sl[:, par, :, :],
                in_=W1mv[:, :, s * 1024:(s + 1) * 1024]), dma=True,
                writes=[f"B_w1b{par}c{c}" for c in range(NCH)])
        else:
            k = 0 if path == "a" else 1
            for c in range(NCH):
                P.op("gpsimd", lambda e, s=s, par=par, c=c, k=k:
                     e.indirect_dma_start(
                         out=w1sl[:, par, c, :], out_offset=None, in_=W1ef,
                         in_offset=IndirectOffsetOnAxis(ap=w1idx[:, k, :][:, c:c + 1],
                                                        axis=0),
                         element_offset=s * 1024), dma=True,
                     reads=[f"B_w1idx{k}"], writes=[f"B_w1b{par}c{c}"])

    def load_w2(path, s, par):
        if path == "m":
            P.op("gpsimd", lambda e, s=s, par=par: e.dma_start(
                out=w2sl[:, par, :, :],
                in_=W2mv[:, s * 8:(s + 1) * 8, :]), dma=True,
                writes=[f"B_w2b{par}c{c}" for c in range(NCH)])
        else:
            k = 0 if path == "a" else 1
            for gg in range(NCH):
                P.op("gpsimd", lambda e, s=s, par=par, gg=gg, k=k:
                     e.indirect_dma_start(
                         out=w2sl[:, par, gg, :], out_offset=None, in_=W2ef,
                         in_offset=IndirectOffsetOnAxis(
                             ap=w2idx[:, k, :][:, s * 8 + gg:s * 8 + gg + 1],
                             axis=0)), dma=True,
                     reads=[f"B_w2idx{k}"], writes=[f"B_w2b{par}c{gg}"])

    load_w1("m", 0, 0)
    load_w2("m", 0, 0)

    gs = lambda a, b: gsc[:, a:b]
    for c in range(NCH):
        P.op("tensor", lambda e, c=c: e.matmul(ps[0][0:1, 0:8], msum[:, c:c + 1],
                                               wg[:, c, :], start=(c == 0),
                                               stop=(c == NCH - 1)),
             reads=[f"Bgmsum{c}", "B_wg"], writes=["pb0"])
    P.op("vector", lambda e: e.tensor_scalar(gs(0, 8), ps[0][0:1, 0:8],
                                             1.0 / T, None, ALU.mult),
         reads=["pb0"], writes=["Bg_lg"])
    P.op("vector", lambda e: e.tensor_add(gs(0, 8), gs(0, 8), bgt[:]),
         reads=["B_bg"], writes=["Bg_lg"])
    P.op("vector", lambda e: e.reduce_max(gs(8, 9), gs(0, 8), axis=AX.X),
         reads=["Bg_lg"], writes=["Bg_mx"])
    P.op("vector", lambda e: e.tensor_scalar(gs(9, 10), gs(8, 9), -1.0, None,
                                             ALU.mult),
         reads=["Bg_mx"], writes=["Bg_nmx"])
    P.op("scalar", lambda e: e.activation(gs(16, 24), gs(0, 8), AF.Exp,
                                          bias=gs(9, 10)),
         reads=["Bg_lg", "Bg_nmx"], writes=["Bg_ex"])
    P.op("vector", lambda e: e.reduce_sum(gs(10, 11), gs(16, 24), axis=AX.X),
         reads=["Bg_ex"], writes=["Bg_sm"])
    P.op("vector", lambda e: e.reciprocal(gs(11, 12), gs(10, 11)),
         reads=["Bg_sm"], writes=["Bg_rs"])
    P.op("vector", lambda e: e.tensor_scalar(gs(24, 32), gs(16, 24), gs(11, 12),
                                             None, ALU.mult),
         reads=["Bg_ex", "Bg_rs"], writes=["Bg_p"])
    P.op("vector", lambda e: e.reduce_max(gs(12, 13), gs(24, 32), axis=AX.X),
         reads=["Bg_p"], writes=["Bg_m1"])
    P.op("vector", lambda e: e.tensor_scalar(gs(32, 40), gs(24, 32), gs(12, 13),
                                             None, ALU.is_ge),
         reads=["Bg_p", "Bg_m1"], writes=["Bg_is"])
    P.op("vector", lambda e: e.tensor_mul(gs(40, 48), gs(32, 40), iotaf[0:1, 0:8]),
         reads=["Bg_is", "consts"], writes=["Bg_t"])
    P.op("vector", lambda e: e.reduce_sum(gs(13, 14), gs(40, 48), axis=AX.X),
         reads=["Bg_t"], writes=["Bg_i1"])
    P.op("vector", lambda e: e.tensor_scalar(gs(13, 14), gs(13, 14), 1.0 / 128,
                                             None, ALU.mult),
         reads=["Bg_i1"], writes=["Bg_i1"])
    P.op("vector", lambda e: e.tensor_scalar(gs(32, 40), gs(32, 40), 2.0, None,
                                             ALU.mult),
         reads=["Bg_is"], writes=["Bg_is"])
    P.op("vector", lambda e: e.tensor_sub(gs(48, 56), gs(24, 32), gs(32, 40)),
         reads=["Bg_p", "Bg_is"], writes=["Bg_pw"])
    P.op("vector", lambda e: e.reduce_max(gs(14, 15), gs(48, 56), axis=AX.X),
         reads=["Bg_pw"], writes=["Bg_m2"])
    P.op("vector", lambda e: e.tensor_scalar(gs(32, 40), gs(48, 56), gs(14, 15),
                                             None, ALU.is_ge),
         reads=["Bg_pw", "Bg_m2"], writes=["Bg_is"])
    P.op("vector", lambda e: e.tensor_mul(gs(40, 48), gs(32, 40), iotaf[0:1, 0:8]),
         reads=["Bg_is", "consts"], writes=["Bg_t"])
    P.op("vector", lambda e: e.reduce_sum(gs(15, 16), gs(40, 48), axis=AX.X),
         reads=["Bg_t"], writes=["Bg_i2"])
    P.op("vector", lambda e: e.tensor_scalar(gs(15, 16), gs(15, 16), 1.0 / 128,
                                             None, ALU.mult),
         reads=["Bg_i2"], writes=["Bg_i2"])
    P.op("vector", lambda e: e.tensor_add(gs(56, 57), gs(12, 13), gs(14, 15)),
         reads=["Bg_m1", "Bg_m2"], writes=["Bg_den"])
    P.op("vector", lambda e: e.reciprocal(gs(57, 58), gs(56, 57)),
         reads=["Bg_den"], writes=["Bg_rden"])
    P.op("vector", lambda e: e.tensor_mul(gs(58, 59), gs(12, 13), gs(57, 58)),
         reads=["Bg_m1", "Bg_rden"], writes=["Bg_gA"])
    P.op("vector", lambda e: e.tensor_mul(gs(59, 60), gs(14, 15), gs(57, 58)),
         reads=["Bg_m2", "Bg_rden"], writes=["Bg_gB"])
    P.op("vector", lambda e: e.tensor_scalar(gs(60, 61), alv[:], 0.1, 1.0,
                                             ALU.max, ALU.min),
         reads=["B_al"], writes=["Bg_a"])
    P.op("vector", lambda e: e.tensor_scalar(gs(61, 62), gs(60, 61), -1.0, 1.0,
                                             ALU.mult, ALU.add),
         reads=["Bg_a"], writes=["Bg_oma"])
    P.op("vector", lambda e: e.tensor_mul(gs(62, 63), gs(61, 62), gs(58, 59)),
         reads=["Bg_oma", "Bg_gA"], writes=["Bg_cA"])
    P.op("vector", lambda e: e.tensor_mul(gs(63, 64), gs(61, 62), gs(59, 60)),
         reads=["Bg_oma", "Bg_gB"], writes=["Bg_cB"])
    for i, (a0, rr) in enumerate([(60, "Bg_a"), (62, "Bg_cA"), (63, "Bg_cB")]):
        P.op("tensor", lambda e, a0=a0, i=i: e.matmul(
            ps[1][:, i:i + 1], onesr[:], gsc[:, a0:a0 + 1],
            start=True, stop=True),
            reads=[rr, "B_onesr"], writes=["pb1"])
    P.op("vector", lambda e: e.tensor_copy(crep[:], ps[1][:, 0:3]),
         reads=["pb1"], writes=["B_crep"])
    for i, (a0, rr) in enumerate([(13, "Bg_i1"), (15, "Bg_i2")]):
        P.op("tensor", lambda e, a0=a0, i=i: e.matmul(
            ps[2][:, i:i + 1], onesr[:], gsc[:, a0:a0 + 1],
            start=True, stop=True),
            reads=[rr, "B_onesr"], writes=["pb2"])
    P.op("vector", lambda e: e.tensor_copy(selrep[:], ps[2][:, 0:2]),
         reads=["pb2"], writes=["B_selrep"])

    b1ef = ins["b1e"].rearrange("e (gg j) -> (e gg) j", j=128)
    b2ef = ins["b2e"].rearrange("e (gg j) -> (e gg) j", j=128)
    for k in range(2):
        sc = selrep[:, k:k + 1]
        P.op("vector", lambda e, sc=sc: e.tensor_scalar(
            idxf[:, 0:1], sc, 1024.0, None, ALU.mult),
            reads=["B_selrep"], writes=["B_idxt"])
        P.op("vector", lambda e, k=k: e.tensor_scalar(
            idxf[:, 1:9], iotaf[:, 0:8], idxf[:, 0:1], None, ALU.add),
            reads=["B_idxt", "consts"], writes=["B_idxf"])
        P.op("vector", lambda e, k=k: e.tensor_copy(w1idx[:, k, :], idxf[:, 1:9]),
             reads=["B_idxf"], writes=[f"B_w1idx{k}"])
        P.op("vector", lambda e, sc=sc: e.tensor_scalar(
            idxf[:, 0:1], sc, 4096.0, None, ALU.mult),
            reads=["B_selrep"], writes=["B_idxt"])
        P.op("vector", lambda e: e.tensor_scalar(
            idxf[:, 1:33], iotaf[:], idxf[:, 0:1], None, ALU.add),
            reads=["B_idxt", "consts"], writes=["B_idxf"])
        P.op("vector", lambda e, k=k: e.tensor_copy(w2idx[:, k, :], idxf[:, 1:33]),
             reads=["B_idxf"], writes=[f"B_w2idx{k}"])
        P.op("vector", lambda e, sc=sc: e.tensor_scalar(
            bidxf[:], sc[0:32, :], 32.0, None, ALU.mult),
            reads=["B_selrep"], writes=["B_bidxf"])
        P.op("vector", lambda e: e.tensor_scalar(
            bidxf[:], iotag[0:32, :], bidxf[:], None, ALU.add),
            reads=["B_bidxf", "consts"], writes=["B_bidxf"])
        P.op("vector", lambda e: e.tensor_copy(bidx[:], bidxf[:]),
             reads=["B_bidxf"], writes=["B_bidx"])
        P.op("gpsimd", lambda e: e.indirect_dma_start(
            out=btmp[:], out_offset=None, in_=b1ef,
            in_offset=IndirectOffsetOnAxis(ap=bidx[:], axis=0)), dma=True,
            reads=["B_bidx"], writes=["B_btmp"])
        P.op("tensor", lambda e: e.transpose(ps[3][:, 0:32], btmp[:],
                                             g["eyef"][0:32, 0:32]),
             reads=["B_btmp", "consts"], writes=["pb3"])
        P.op("vector", lambda e, k=k: e.tensor_copy(b1x[:, k, :], ps[3][:, 0:32]),
             reads=["pb3"], writes=[f"B_b1x{k}"])
        P.op("vector", lambda e, sc=sc: e.tensor_scalar(
            bidxf[0:8, :], sc[0:8, :], 8.0, None, ALU.mult),
            reads=["B_selrep"], writes=["B_bidxf"])
        P.op("vector", lambda e: e.tensor_scalar(
            bidxf[0:8, :], iotag[0:8, :], bidxf[0:8, :], None, ALU.add),
            reads=["B_bidxf", "consts"], writes=["B_bidxf"])
        P.op("vector", lambda e: e.tensor_copy(bidx[0:8, :], bidxf[0:8, :]),
             reads=["B_bidxf"], writes=["B_bidx"])
        P.op("gpsimd", lambda e: e.indirect_dma_start(
            out=btmp[0:8, :], out_offset=None, in_=b2ef,
            in_offset=IndirectOffsetOnAxis(ap=bidx[0:8, :], axis=0)), dma=True,
            reads=["B_bidx"], writes=["B_btmp"])
        P.op("tensor", lambda e: e.transpose(ps[3][:, 0:8], btmp[0:8, :],
                                             g["eyef"][0:8, 0:8]),
             reads=["B_btmp", "consts"], writes=["pb3"])
        P.op("vector", lambda e, k=k: e.tensor_copy(b2x[:, k, :], ps[3][:, 0:8]),
             reads=["pb3"], writes=[f"B_b2x{k}"])
    P.op("vector", lambda e: e.tensor_scalar(bcomb[:], b2m[:], crep[:, 0:1],
                                             None, ALU.mult),
         reads=["B_b2m", "B_crep"], writes=["B_bc"])
    for k in range(2):
        P.op("vector", lambda e, k=k: e.tensor_scalar(
            idxf[:, 0:8], b2x[:, k, :], crep[:, k + 1:k + 2], None, ALU.mult),
            reads=[f"B_b2x{k}", "B_crep"], writes=["B_idxf"])
        P.op("vector", lambda e: e.tensor_add(bcomb[:], bcomb[:], idxf[:, 0:8]),
             reads=["B_idxf"], writes=["B_bc"])

    paths = [("m", b1m, 0), ("a", b1x[:, 0, :], 1), ("b", b1x[:, 1, :], 2)]
    sl_ctr = 0
    for pi, (path, b1c, ci_) in enumerate(paths):
        for s in range(4):
            drain_bg(g, 2)
            par = sl_ctr % 2
            if s < 3:
                load_w1(path, s + 1, (sl_ctr + 1) % 2)
                load_w2(path, s + 1, (sl_ctr + 1) % 2)
            elif pi < 2:
                load_w1(paths[pi + 1][0], 0, (sl_ctr + 1) % 2)
                load_w2(paths[pi + 1][0], 0, (sl_ctr + 1) % 2)
            for tt in range(NTT):
                for hc in range(NCH):
                    bank = (tt * NCH + hc) % 4
                    st_ = slice(tt * TT, (tt + 1) * TT)
                    for c in range(NCH):
                        P.op("tensor", lambda e, c=c, hc=hc, st_=st_, par=par, bank=bank:
                             e.matmul(ps[bank][:],
                                      w1sl[:, par, c, hc * 128:(hc + 1) * 128],
                                      xln[:, c, st_], start=(c == 0),
                                      stop=(c == NCH - 1)),
                             reads=[f"B_w1b{par}c{c}"] + xln_reads(c, tt),
                             writes=[f"pb{bank}"])
                    gx = s * 8 + hc
                    P.op("scalar", lambda e, hc=hc, st_=st_, bank=bank, b1c=b1c, gx=gx:
                         e.activation(hT[:, hc, st_], ps[bank][:],
                                      AF.Gelu_apprx_tanh,
                                      bias=b1c[:, gx:gx + 1]),
                         reads=[f"pb{bank}", "B_b1m", "B_b1x0", "B_b1x1"],
                         writes=[f"B_h{hc}_{tt}"])
            for tt in range(NTT):
                for ec in range(NCH):
                    bank = 4 + (tt * NCH + ec) % 4
                    st_ = slice(tt * TT, (tt + 1) * TT)
                    for hc in range(NCH):
                        P.op("tensor", lambda e, hc=hc, ec=ec, st_=st_, par=par, bank=bank:
                             e.matmul(ps[bank][:],
                                      w2sl[:, par, hc, ec * 128:(ec + 1) * 128],
                                      hT[:, hc, st_], start=(hc == 0),
                                      stop=(hc == NCH - 1)),
                             reads=[f"B_w2b{par}c{hc}", f"B_h{hc}_{tt}"],
                             writes=[f"pb{bank}"])
                    P.op("vector", lambda e, ec=ec, st_=st_, bank=bank, ci_=ci_:
                         e.scalar_tensor_tensor(x[:, ec, st_], ps[bank][:],
                                                crep[:, ci_:ci_ + 1],
                                                x[:, ec, st_],
                                                ALU.mult, ALU.add),
                         reads=[f"pb{bank}", "B_crep"], writes=[f"rx{ec}_{tt}"])
            sl_ctr += 1
    for c in range(NCH):
        for tt in range(NTT):
            st_ = slice(tt * TT, (tt + 1) * TT)
            P.op("scalar", lambda e, c=c, st_=st_: e.add(
                x[:, c, st_], x[:, c, st_], bcomb[:, c:c + 1]),
                reads=["B_bc"], writes=[f"rx{c}_{tt}"])



def emit_phaseC(P, ctx, g):
    sb = g["ov"].sb
    x, xln, ps = g["x"], g["xln"], g["psum"]
    ins = g["ins"]

    wqp = sb("C_wq", [128, 2, NCH, 128], BF16)
    wkp = sb("C_wk", [128, 2, NCH, 128], BF16)
    wvp = sb("C_wv", [128, 2, NCH, 128], BF16)
    wop = sb("C_wo", [128, 2, 1024], BF16)
    qp = sb("C_qp", [128, 2, T], BF16)
    kp = sb("C_kp", [128, 2, T], BF16)
    vp = sb("C_vp", [128, 2, 16, 256], BF16)
    att = sb("C_att", [128, 2, NTT, TT], BF16)
    pbuf = sb("C_p", [128, 4, TT], BF16)
    rq = sb("C_rq", [128, TT], F32)

    Wq = ins["Wqc"].rearrange("(c p) f -> p c f", p=128)
    Wk = ins["Wkc"].rearrange("(c p) f -> p c f", p=128)
    Wv = ins["Wvc"].rearrange("(c p) f -> p c f", p=128)
    Wo = ins["Wo1"].rearrange("(q p) o -> p q o", p=128)

    P.op("vector", lambda e: e.memset(vp[:, :, :, 64:128], 1.0), writes=["C_vo"])
    P.op("vector", lambda e: e.memset(vp[:, :, :, 192:256], 1.0), writes=["C_vo"])

    def load_pair(pp, par):
        fs = slice(pp * 128, (pp + 1) * 128)
        P.op("gpsimd", lambda e, fs=fs, par=par: e.dma_start(
            out=wqp[:, par, :, :], in_=Wq[:, :, fs]), dma=True,
            writes=[f"C_wq{par}"])
        P.op("gpsimd", lambda e, fs=fs, par=par: e.dma_start(
            out=wkp[:, par, :, :], in_=Wk[:, :, fs]), dma=True,
            writes=[f"C_wk{par}"])
        P.op("gpsimd", lambda e, fs=fs, par=par: e.dma_start(
            out=wvp[:, par, :, :], in_=Wv[:, :, fs]), dma=True,
            writes=[f"C_wv{par}"])
        P.op("gpsimd", lambda e, pp=pp, par=par: e.dma_start(
            out=wop[:, par, :], in_=Wo[:, pp, :]), dma=True,
            writes=[f"C_wo{par}"])

    def proj_units(pp, par):
        units = []
        for tt in range(NTT):
            st_ = slice(tt * TT, (tt + 1) * TT)
            for name, wt, dst in (("q", wqp, qp), ("k", wkp, kp)):
                bank = (4 if name == "q" else 5) + 2 * (tt % 2)

                def u(name=name, wt=wt, dst=dst, bank=bank, st_=st_, tt=tt):
                    for c in range(NCH):
                        P.op("tensor", lambda e, c=c: e.matmul(
                            ps[bank][:], wt[:, par, c, :], xln[:, c, st_],
                            start=(c == 0), stop=(c == NCH - 1)),
                            reads=[f"C_w{name}{par}"] + xln_reads(c, tt),
                            writes=[f"pb{bank}"])
                    P.op("vector", lambda e: e.tensor_copy(dst[:, par, st_],
                                                           ps[bank][:]),
                         reads=[f"pb{bank}"], writes=[f"C_{name}{par}_{tt}"])
                units.append(u)
        for tc in range(16):
            bank = 2 + tc % 2

            def u(tc=tc, bank=bank):
                for c in range(NCH):
                    P.op("tensor", lambda e, c=c: e.matmul(
                        ps[bank][:, 0:128],
                        xln[:, c, tc * 128:(tc + 1) * 128],
                        wvp[:, par, c, :], start=(c == 0),
                        stop=(c == NCH - 1)),
                        reads=[f"C_wv{par}"] + xln_reads(c, tc // 4),
                        writes=[f"pb{bank}"])
                P.op("vector", lambda e: e.tensor_copy(vp[:, par, tc, 0:64],
                                                       ps[bank][:, 0:64]),
                     reads=[f"pb{bank}"], writes=[f"C_v{par}_{tc}"])
                P.op("vector", lambda e: e.tensor_copy(vp[:, par, tc, 128:192],
                                                       ps[bank][:, 64:128]),
                     reads=[f"pb{bank}"], writes=[f"C_v{par}_{tc}"])
            units.append(u)
        return units

    def attn_pair(pp, par):
        def c_sexp(qj, ci):
            sqj = slice(qj * TT, (qj + 1) * TT)
            for hh in range(2):
                hs = slice(hh * 64, (hh + 1) * 64)
                sbank = 4 + hh + 2 * (ci % 2)
                P.op("tensor", lambda e, ci=ci, hs=hs, sqj=sqj, sbank=sbank:
                     e.matmul(ps[sbank][:], kp[hs, par, ci * 128:(ci + 1) * 128],
                              qp[hs, par, sqj], start=True, stop=True),
                     reads=[f"C_k{par}_{ci // 4}", f"C_q{par}_{qj}"],
                     writes=[f"pb{sbank}"])
            for hh in range(2):
                sbank = 4 + hh + 2 * (ci % 2)
                ppx = 2 * (ci % 2) + hh
                P.op("scalar", lambda e, ppx=ppx, sbank=sbank:
                     e.activation(pbuf[:, ppx, :], ps[sbank][:], AF.Exp,
                                  bias=g["cbias"][:, 1:2], scale=1.0 / 8.0),
                     reads=[f"pb{sbank}", "consts"], writes=[f"C_P{ppx}"])

        def c_pv(qj, ci):
            pvb = 2 * (qj % 2)
            for hh in range(2):
                ppx = 2 * (ci % 2) + hh
                P.op("tensor", lambda e, ci=ci, ppx=ppx, hh=hh, pvb=pvb:
                     e.matmul(ps[pvb + hh][:], vp[:, par, ci,
                                                  hh * 128:(hh + 1) * 128],
                              pbuf[:, ppx, :], start=(ci == 0), stop=(ci == 15)),
                     reads=[f"C_v{par}_{ci}", f"C_P{ppx}", "C_vo"],
                     writes=[f"pb{pvb + hh}"])

        def c_evac(qj):
            pvb = 2 * (qj % 2)
            for hh in range(2):
                hs = slice(hh * 64, (hh + 1) * 64)
                rqh = rq[hh * 64:(hh + 1) * 64, :]
                P.op("vector", lambda e, hh=hh, rqh=rqh, pvb=pvb: e.reciprocal(
                    rqh, ps[pvb + hh][64:128, :]),
                    reads=[f"pb{pvb + hh}"], writes=[f"C_rq{hh}"])
                P.op("vector", lambda e, hh=hh, hs=hs, qj=qj, pvb=pvb, rqh=rqh:
                     e.tensor_mul(att[hs, par, qj, :], ps[pvb + hh][0:64, :],
                                  rqh),
                     reads=[f"pb{pvb + hh}", f"C_rq{hh}"],
                     writes=[f"C_att{par}_{qj}"])

        c_sexp(0, 0)
        for u in range(1, 64):
            qj, ci = divmod(u, 16)
            c_sexp(qj, ci)
            pq, pc = divmod(u - 1, 16)
            c_pv(pq, pc)
            if pc == 15:
                c_evac(pq)
        c_pv(3, 15)
        c_evac(3)

    def outproj_units(pp, par):
        units = []
        for oc in range(NCH):
            for tt in range(NTT):
                bank = (oc * NTT + tt) % 2
                st_ = slice(tt * TT, (tt + 1) * TT)

                def u(oc=oc, tt=tt, bank=bank, st_=st_):
                    P.op("tensor", lambda e: e.matmul(
                        ps[bank][:], wop[:, par, oc * 128:(oc + 1) * 128],
                        att[:, par, tt, :], start=True, stop=True),
                        reads=[f"C_wo{par}", f"C_att{par}_{tt}"],
                        writes=[f"pb{bank}"])
                    P.op("vector", lambda e: e.tensor_add(
                        x[:, oc, st_], x[:, oc, st_], ps[bank][:]),
                        reads=[f"pb{bank}"], writes=[f"rx{oc}_{tt}"])
                units.append(u)
        return units

    load_pair(0, 0)
    for u in proj_units(0, 0):
        u()
    load_pair(1, 1)
    for pp in range(8):
        drain_bg(g, 2)
        par = pp % 2
        attn_pair(pp, par)
        pu = proj_units(pp + 1, (pp + 1) % 2) if pp < 7 else []
        ou = outproj_units(pp, par)
        n = max(len(pu), len(ou))
        for i in range(n):
            if i < len(pu):
                pu[i]()
            if i < len(ou):
                ou[i]()
        if pp < 6:
            load_pair(pp + 2, par)



def emit_phaseD(P, ctx, g, out_ap):
    nc = P.nc
    x, xln, ps = g["x"], g["xln"], g["psum"]
    ins = g["ins"]
    ov = g["ov"]
    jtok = g["jtok"]
    NEX, CAP, W = 8, 512, 1040

    gwT = ov.sb("D_gw", [128, NCH, 8], BF16)
    dsm = ov.sb("D_sm", [128, 48], F32)
    ptok = ov.sb("D_pt", [128, 16, 8], F32)
    ptok2 = ov.sb("D_pt2", [128, 8, 16], F32)
    cmpb = ov.sb("D_cmpb", [128, 8, 16], F32)
    bis = ov.sb("D_bis", [128, 6, 8], F32)
    cnt8 = ov.sb("D_cnt8", [128, 8], BF16)
    dsc = ov.sb("D_sc", [8, T], F32)
    cat = ov.sb("D_cat", [128, 16, W], BF16)
    w1sl = ov.sb("D_w1", [128, 2, NCH, 512], BF16)
    w2c = ov.sb("D_w2", [128, 2, 4, E], BF16)
    stage = ov.sb("D_stage", [128, 2, E], F32)
    xin = ov.sb("D_xin", [128, 4, W], BF16)
    eo = ov.sb("D_eo", [128, 4, E], BF16)
    tidu = ov.sb("D_tid", [128, 2, 4], U32)
    tscr = ov.sb("D_tscr", [128, 2], F32)
    gf = ov.sb("D_gf", [128, 2, 4], F32)
    xinTb = [nc.alloc_sbuf_tensor_at("D_xinT0", [128, NCH, CAP], BF16,
                                     offset=nc.lookup_mloc(stage).addr),
             nc.alloc_sbuf_tensor_at("D_xinT1", [128, NCH, CAP], BF16,
                                     offset=nc.lookup_mloc(dsc).addr)]
    hT = nc.alloc_sbuf_tensor_at("D_hT", [128, 32, CAP], BF16,
                                 offset=nc.lookup_mloc(xln).addr)
    xbc = xln[:].rearrange("p c t -> p (c t)").bitcast(F32)
    bufB = xbc[0:8, 0:2048]
    bufD = xbc[0:8, 2048:4096]
    alB = [f"xln{c}_{t}" for c in (0, 1) for t in range(NTT)]
    alD = [f"xln{c}_{t}" for c in (2, 3) for t in range(NTT)]
    bufC = dsc[0:8, :]

    xin_d = [nc.dram_tensor(f"D_xin{e}", [CAP, W], BF16, kind="Internal").ap()
             for e in range(NEX)]
    bndreg = {}

    def bnd(e):
        if "r" not in bndreg:
            bndreg["r"] = e.to_reg(CAP - 1)
        return bndreg["r"]

    W1B = g["w1b"].rearrange("ee (c p) h -> ee p c h", p=128)
    W2B = g["w2b"].rearrange("ee (hc p) d -> ee p hc d", p=128)

    def load_w1(ex, s, par):
        P.op("sync", lambda e, ex=ex, s=s, par=par: e.dma_start(
            out=w1sl[:, par, :, :],
            in_=W1B[ex, :, :, s * 512:(s + 1) * 512]), dma=True,
            reads=[f"L1s1{ex}"], writes=[f"D_w1{par}"])

    def load_w2(ex, g4, par):
        P.op("scalar", lambda e, ex=ex, g4=g4, par=par: e.dma_start(
            out=w2c[:, par, :, :], in_=W2B[ex, :, g4 * 4:(g4 + 1) * 4, :]),
            dma=True, reads=[f"L1s2{ex}"], writes=[f"D_w2{par}"])

    load_w1(0, 0, 0)
    load_w2(0, 0, 0)
    P.op("gpsimd", lambda e: e.dma_start(out=gwT[:], in_=ins["gwT"].rearrange(
        "(c p) e8 -> p c e8", p=128)), dma=True, writes=["D_gw"])
    P.op("gpsimd", lambda e: e.dma_start(out=cat[:, :, 1032:1034],
                                         in_=ins["tidhl"]), dma=True,
         writes=[f"D_cat{tc}" for tc in range(16)])
    drain_bg(g, 99)

    for tt in range(NTT):
        st_ = slice(tt * TT, (tt + 1) * TT)
        bank = tt % 2
        for c in range(NCH):
            P.op("tensor", lambda e, c=c, st_=st_, bank=bank:
                 e.matmul(ps[bank][0:8, :], gwT[:, c, :], xln[:, c, st_],
                          start=(c == 0), stop=(c == NCH - 1)),
                 reads=["D_gw"] + xln_reads(c, tt), writes=[f"pb{bank}"])
        P.op("vector", lambda e, st_=st_, bank=bank:
             e.tensor_copy(dsc[:, st_], ps[bank][0:8, :]),
             reads=[f"pb{bank}"], writes=["D_dsc"])

    fill_units = []

    def fill_copy(dst, src_ps, bank, wr, idx):
        if idx % 3 == 2:
            P.op("vector", lambda e: e.tensor_copy(dst, src_ps),
                 reads=[f"pb{bank}"], writes=wr)
        else:
            P.op("scalar", lambda e: e.activation(dst, src_ps, AF.Copy),
                 reads=[f"pb{bank}"], writes=wr)

    for tc in range(16):
        for ec in range(NCH):
            idx = tc * NCH + ec
            bank = 4 + idx % 4

            def u(tc=tc, ec=ec, bank=bank, idx=idx):
                psb = ps[bank][:].bitcast(BF16)
                P.op("tensor", lambda e, psb=psb: e.transpose(
                    psb[:, 0:128], xln[:, ec, tc * 128:(tc + 1) * 128],
                    g["eyeb"][:]),
                    reads=xln_reads(ec, tc // 4) + ["consts"],
                    writes=[f"pb{bank}"])
                fill_copy(cat[:, tc, ec * 128:(ec + 1) * 128], psb[:, 0:128],
                          bank, [f"D_cat{tc}"], idx)
            fill_units.append(u)
    for tc in range(16):
        spar = tc % 2
        for ec in range(NCH):
            idx = tc * NCH + ec
            bank = 4 + idx % 4

            def u(tc=tc, ec=ec, bank=bank, spar=spar, idx=idx):
                P.op("tensor", lambda e: e.transpose(
                    ps[bank][:, 0:128], x[:, ec, tc * 128:(tc + 1) * 128],
                    g["eyef"][:]),
                    reads=[f"rx{ec}_{tc // 4}", "consts"], writes=[f"pb{bank}"])
                fill_copy(stage[:, spar, ec * 128:(ec + 1) * 128],
                          ps[bank][:, 0:128], bank, [f"D_st{spar}"], idx + 1)
            fill_units.append(u)

        def u(tc=tc, spar=spar):
            qeng = "sync" if tc % 2 == 0 else "scalar"
            P.op(qeng, lambda e: e.dma_start(
                out=out_ap[tc * 128:(tc + 1) * 128, :], in_=stage[:, spar, :]),
                dma=True, reads=[f"D_st{spar}"], writes=["outdram"])
        fill_units.append(u)

    def fdrain(k):
        for _ in range(k):
            if fill_units:
                fill_units.pop(0)()

    PT = [f"D_ptok{t}" for t in range(16)]
    for tc in range(16):
        bank = 2 + tc % 2
        P.op("tensor", lambda e, tc=tc, bank=bank: e.transpose(
            ps[bank][:, 0:8], dsc[0:8, tc * 128:(tc + 1) * 128],
            g["eyef"][0:8, 0:8]),
            reads=["D_dsc", "consts"], writes=[f"pb{bank}"])
        P.op("vector", lambda e, tc=tc, bank=bank: e.tensor_copy(
            ptok[:, tc, :], ps[bank][:, 0:8]),
            reads=[f"pb{bank}"], writes=[f"D_ptok{tc}"])
    p3 = ptok[:]
    mxb = dsm[:, 0:16].unsqueeze(2).broadcast_to([128, 16, 8])
    smb = dsm[:, 16:32].unsqueeze(2).broadcast_to([128, 16, 8])
    P.op("vector", lambda e: e.reduce_max(dsm[:, 0:16], p3, axis=AX.X),
         reads=PT, writes=["D_mx"])
    fdrain(8)
    P.op("vector", lambda e: e.tensor_sub(p3, p3, mxb),
         reads=PT + ["D_mx"], writes=PT)
    fdrain(8)
    P.op("scalar", lambda e: e.activation(
        ptok[:].rearrange("p a b -> p (a b)"),
        ptok[:].rearrange("p a b -> p (a b)"), AF.Exp),
        reads=PT, writes=PT)
    P.op("vector", lambda e: e.reduce_sum(dsm[:, 16:32], p3, axis=AX.X),
         reads=PT, writes=["D_sme"])
    fdrain(8)
    P.op("vector", lambda e: e.reciprocal(dsm[:, 16:32], dsm[:, 16:32]),
         reads=["D_sme"], writes=["D_sme"])
    P.op("vector", lambda e: e.tensor_mul(p3, p3, smb),
         reads=PT + ["D_sme"], writes=PT)
    for tc in range(16):
        P.op("vector", lambda e, tc=tc: e.tensor_copy(
            ptok2[:, :, tc:tc + 1], ptok[:, tc, :].unsqueeze(2)),
            reads=[f"D_ptok{tc}"], writes=["D_pt2"])
        P.op("vector", lambda e, tc=tc: e.tensor_copy(
            cat[:, tc, 1024:1032], ptok[:, tc, :]),
            reads=[f"D_ptok{tc}"], writes=[f"D_cat{tc}"])

    for tc in range(16):
        P.op("tensor", lambda e, tc=tc: e.transpose(
            ps[3][0:8, 0:128], ptok[:, tc, :], g["eyef"][:]),
            reads=[f"D_ptok{tc}", "consts"], writes=["pb3"])
        P.op("vector", lambda e, tc=tc: e.tensor_copy(
            dsc[:, tc * 128:(tc + 1) * 128], ps[3][0:8, 0:128]),
            reads=["pb3"], writes=["D_dsc"])

    blo = bis[:, 0, :]
    bhi = bis[:, 1, :]
    bmid = bis[:, 2, :]
    bsel = bis[:, 3, :]
    bt1 = bis[:, 4, :]
    bt2 = bis[:, 5, :]
    midb = bmid.unsqueeze(2).broadcast_to([128, 8, 16])
    P.op("vector", lambda e: e.memset(blo, 0.0), writes=["D_lo"])
    P.op("vector", lambda e: e.memset(bhi, 1.0), writes=["D_hi"])
    for it in range(28):
        P.op("vector", lambda e: e.tensor_add(bmid, blo, bhi),
             reads=["D_lo", "D_hi"], writes=["D_mid"])
        P.op("vector", lambda e: e.tensor_scalar(bmid, bmid, 0.5, None,
                                                 ALU.mult),
             reads=["D_mid"], writes=["D_mid"])
        P.op("vector", lambda e: e.tensor_tensor(cmpb[:], ptok2[:], midb,
                                                 ALU.is_ge),
             reads=["D_pt2", "D_mid"], writes=["D_cmpb"])
        P.op("vector", lambda e: e.reduce_sum(dsm[:, 32:40], cmpb[:],
                                              axis=AX.X),
             reads=["D_cmpb"], writes=["D_cntf"])
        P.op("vector", lambda e: e.tensor_copy(cnt8[:], dsm[:, 32:40]),
             reads=["D_cntf"], writes=["D_cnt8"])
        P.op("tensor", lambda e: e.matmul(ps[2][:, 0:8], g["ln"]["ones128"][:],
                                          cnt8[:], start=True, stop=True),
             reads=["D_cnt8", "consts"], writes=["pb2"])
        fdrain(4)
        P.op("vector", lambda e: e.tensor_scalar(bsel, ps[2][:, 0:8],
                                                 float(CAP), None, ALU.is_ge),
             reads=["pb2"], writes=["D_sel"])
        P.op("vector", lambda e: e.tensor_sub(bt1, bmid, blo),
             reads=["D_mid", "D_lo"], writes=["D_t1"])
        P.op("vector", lambda e: e.tensor_mul(bt1, bt1, bsel),
             reads=["D_t1", "D_sel"], writes=["D_t1"])
        P.op("vector", lambda e: e.tensor_add(blo, blo, bt1),
             reads=["D_lo", "D_t1"], writes=["D_lo"])
        P.op("vector", lambda e: e.tensor_sub(bt2, bhi, bmid),
             reads=["D_hi", "D_mid"], writes=["D_t2"])
        P.op("vector", lambda e: e.tensor_mul(bt2, bt2, bsel),
             reads=["D_t2", "D_sel"], writes=["D_t2"])
        P.op("vector", lambda e: e.tensor_add(bhi, bmid, bt2),
             reads=["D_mid", "D_t2"], writes=["D_hi"])
    lo8 = dsm[0:8, 40:41]
    P.op("vector", lambda e: e.tensor_mul(dsm[0:8, 32:40], blo[0:8, :],
                                          g["eyef"][0:8, 0:8]),
         reads=["D_lo", "consts"], writes=["D_l8t"])
    P.op("vector", lambda e: e.reduce_sum(lo8, dsm[0:8, 32:40], axis=AX.X),
         reads=["D_l8t"], writes=["D_lo8"])

    P.op("vector", lambda e: e.tensor_scalar(bufB[:], dsc[0:8, :],
                                             lo8, None, ALU.is_ge),
         reads=["D_dsc", "D_lo8"], writes=["D_bufB"] + alB)
    P.op("vector", lambda e: e.memset(bufD[:], 0.0),
         writes=["D_bufD"] + alD)
    P.op("vector", lambda e: e.tensor_tensor_scan(bufC[:], bufB[:], bufD[:],
                                                  0.0, ALU.add, ALU.add),
         reads=["D_bufB", "D_bufD", "D_dsc"] + alB + alD,
         writes=["D_bufC", "D_dsc"])
    P.op("vector", lambda e: e.tensor_sub(bufC[:], bufC[:], bufB[:]),
         reads=["D_bufC", "D_bufB"] + alB, writes=["D_bufC"])
    P.op("gpsimd", lambda e: e.iota(bufD[:], [[1, 2048]], base=4096,
                                    channel_multiplier=0,
                                    allow_small_or_imprecise_dtypes=True),
         reads=["D_bufD"], writes=["D_bufD"] + alD)
    P.op("vector", lambda e: e.tensor_mul(bufC[:], bufC[:], bufB[:]),
         reads=["D_bufB", "D_bufC"] + alB, writes=["D_bufC"])
    P.op("vector", lambda e: e.tensor_scalar(bufB[:], bufB[:], -1.0,
                                             1.0, ALU.mult, ALU.add),
         reads=["D_bufB"], writes=["D_bufB"] + alB)
    P.op("vector", lambda e: e.tensor_mul(bufD[:], bufD[:], bufB[:]),
         reads=["D_bufB", "D_bufD"] + alB, writes=["D_bufD"] + alD)
    P.op("vector", lambda e: e.tensor_add(bufC[:], bufC[:], bufD[:]),
         reads=["D_bufD", "D_bufC"] + alD, writes=["D_bufC"])
    for tc in range(16):
        P.op("tensor", lambda e, tc=tc: e.transpose(
            ps[2][0:128, 0:8], bufC[:, tc * 128:(tc + 1) * 128],
            g["eyef"][0:8, 0:8]),
            reads=["D_bufC", "consts"], writes=["pb2"])
        P.op("vector", lambda e, tc=tc: e.tensor_copy(jtok[:, tc, :],
                                                      ps[2][:, 0:8]),
             reads=["pb2"], writes=[f"D_jtok{tc}"])

    def scatter(ex):
        for tc in range(16):
            P.op("gpsimd", lambda e, ex=ex, tc=tc: e.indirect_dma_start(
                out=xin_d[ex], out_offset=IndirectOffsetOnAxis(
                    ap=jtok[:, tc, ex:ex + 1], axis=0),
                in_=cat[:, tc, :], in_offset=None,
                bounds_check=bnd(e), oob_is_err=False), dma=True,
                reads=[f"D_jtok{tc}", f"D_cat{tc}"], writes=[f"D_xind{ex}"])

    scatter(0)
    fdrain(999)

    def gather(ex):
        P.op("sync", lambda e, ex=ex: e.dma_start(
            out=xin[:], in_=xin_d[ex].rearrange("(jc p) w -> p jc w", p=128)),
            dma=True, reads=[f"D_xind{ex}"], writes=["D_xinb"])

    def transpose_units(ex):
        par = ex % 2
        xinT = xinTb[par]
        units = []
        for dc in range(NCH):
            for jc in range(4):
                bank = 2 + (dc * 4 + jc) % 2
                psb = ps[bank][:].bitcast(BF16)
                first = (dc == 0 and jc == 0)
                alias = (["D_st0", "D_st1"] if par == 0 else
                         ["D_dsc", "D_bufC"]) if ex < 2 and first else []

                def u(jc=jc, dc=dc, psb=psb, bank=bank, par=par, xinT=xinT,
                      alias=alias):
                    P.op("tensor", lambda e: e.transpose(
                        psb[:, 0:128], xin[:, jc, dc * 128:(dc + 1) * 128],
                        g["eyeb"][:]),
                        reads=["D_xinb", "consts"], writes=[f"pb{bank}"])
                    P.op("vector", lambda e: e.tensor_copy(
                        xinT[:, dc, jc * 128:(jc + 1) * 128], psb[:, 0:128]),
                        reads=[f"pb{bank}"],
                        writes=[f"D_xinT{par}_{dc}"] + alias)
                units.append(u)
        return units

    def extract(ex, par):
        for jc in range(4):
            P.op("vector", lambda e, jc=jc, ex=ex, par=par: e.tensor_copy(
                gf[:, par, jc:jc + 1], xin[:, jc, 1024 + ex:1025 + ex]),
                reads=["D_xinb"], writes=[f"D_gf{par}"])
            P.op("vector", lambda e, jc=jc: e.tensor_copy(
                tscr[:, 0:1], xin[:, jc, 1032:1033]),
                reads=["D_xinb"], writes=["D_tscr"])
            P.op("vector", lambda e, jc=jc: e.tensor_copy(
                tscr[:, 1:2], xin[:, jc, 1033:1034]),
                reads=["D_xinb"], writes=["D_tscr"])
            P.op("vector", lambda e: e.tensor_scalar(
                tscr[:, 0:1], tscr[:, 0:1], 256.0, None, ALU.mult),
                reads=["D_tscr"], writes=["D_tscr"])
            P.op("vector", lambda e: e.tensor_add(
                tscr[:, 0:1], tscr[:, 0:1], tscr[:, 1:2]),
                reads=["D_tscr"], writes=["D_tscr"])
            P.op("vector", lambda e, jc=jc, par=par: e.tensor_copy(
                tidu[:, par, jc:jc + 1], tscr[:, 0:1]),
                reads=["D_tscr"], writes=[f"D_tid{par}"])

    gather(0)
    for u in transpose_units(0):
        u()
    extract(0, 0)
    scatter(1)

    w1ctr = 0
    for ex in range(NEX):
        par = ex % 2
        if ex + 1 < NEX:
            gather(ex + 1)
        if ex + 2 < NEX:
            scatter(ex + 2)
        tun = transpose_units(ex + 1) if ex + 1 < NEX else []
        xinT = xinTb[par]
        for s in range(8):
            wpar = w1ctr % 2
            if s < 7:
                load_w1(ex, s + 1, (w1ctr + 1) % 2)
            elif ex < NEX - 1:
                load_w1(ex + 1, 0, (w1ctr + 1) % 2)
            if s >= 2:
                for _ in range(6):
                    if tun:
                        tun.pop(0)()
            for hc in range(4):
                gh = s * 4 + hc
                bank = gh % 2
                for dc in range(NCH):
                    P.op("tensor", lambda e, dc=dc, hc=hc, wpar=wpar, bank=bank,
                         xinT=xinT, par=par:
                         e.matmul(ps[bank][:],
                                  w1sl[:, wpar, dc, hc * 128:(hc + 1) * 128],
                                  xinT[:, dc, :], start=(dc == 0),
                                  stop=(dc == NCH - 1)),
                         reads=[f"D_w1{wpar}", f"D_xinT{par}_{dc}"],
                         writes=[f"pb{bank}"])
                P.op("scalar", lambda e, gh=gh, bank=bank: e.activation(
                    hT[:, gh, :], ps[bank][:], AF.Gelu),
                    reads=[f"pb{bank}"],
                    writes=[f"D_h{gh}", f"xln{gh // 4}_{gh % 4}"])
            w1ctr += 1
        for g4 in range(8):
            wpar2 = g4 % 2
            if g4 < 7:
                load_w2(ex, g4 + 1, (g4 + 1) % 2)
            elif ex < NEX - 1:
                load_w2(ex + 1, 0, (g4 + 1) % 2)
            if g4 == 6 and ex + 1 < NEX:
                extract(ex + 1, (ex + 1) % 2)
            for ghl in range(4):
                gh = g4 * 4 + ghl
                for jc in range(4):
                    for dh in range(2):
                        bank = jc * 2 + dh
                        P.op("tensor", lambda e, gh=gh, ghl=ghl, jc=jc, dh=dh,
                             wpar2=wpar2, bank=bank:
                             e.matmul(ps[bank][:],
                                      hT[:, gh, jc * 128:(jc + 1) * 128],
                                      w2c[:, wpar2, ghl, dh * 512:(dh + 1) * 512],
                                      start=(gh == 0), stop=(gh == 31)),
                             reads=[f"D_w2{wpar2}", f"D_h{gh}"],
                             writes=[f"pb{bank}"])
        for jc in range(4):
            for dh in range(2):
                bank = jc * 2 + dh
                P.op("vector", lambda e, jc=jc, dh=dh, bank=bank, par=par:
                     e.tensor_scalar(eo[:, jc, dh * 512:(dh + 1) * 512],
                                     ps[bank][:], gf[:, par, jc:jc + 1], None,
                                     ALU.mult),
                     reads=[f"pb{bank}", f"D_gf{par}"], writes=[f"D_eo{jc}"])
        for jc in range(4):
            P.op("gpsimd", lambda e, jc=jc, par=par: e.indirect_dma_start(
                out=out_ap, out_offset=IndirectOffsetOnAxis(
                    ap=tidu[:, par, jc:jc + 1], axis=0),
                in_=eo[:, jc, :], in_offset=None,
                oob_is_err=True, compute_op=ALU.add), dma=True,
                reads=[f"D_eo{jc}", f"D_tid{par}", "outdram"],
                writes=["outdram"])



def _build(phases):
    nc = bass.Bass("TRN2", target_bir_lowering=False,
                   dynamic_dma_scratch_size=16384)
    ins = {}

    def din(name, shape):
        ins[name] = nc.dram_tensor(name, shape, F32, kind="ExternalInput").ap()

    din("xT", [E, T])
    ins["peT"] = nc.dram_tensor("peT", [D0, T], BF16,
                                kind="ExternalInput").ap()
    ins["cmask"] = nc.dram_tensor("cmask", [128, 896], BF16,
                                  kind="ExternalInput").ap()
    din("eye", [128, 128])
    if "A" in phases:
        din("wqkvT", [E, 3 * H0 * D0])
        din("woT", [H0 * D0, E])
    if "B" in phases:
        din("wgate", [E, 8])
        din("bgate", [1, 8])
        din("alpha", [1, 1])
        din("W1m", [E, 4096])
        din("W2m", [4096, E])
        din("b1mcm", [128, 32])
        din("b2mcm", [128, 8])
        din("W1e", [8, E, 4096])
        din("W2e", [8, 4096, E])
        din("b1e", [8, 4096])
        din("b2e", [8, E])
    if "C" in phases:
        din("Wqc", [E, E])
        din("Wkc", [E, E])
        din("Wvc", [E, E])
        din("Wo1", [E, E])
    if "D" in phases:
        din("gwT", [E, 8])
        din("tidhl", [128, 16, 2])
        din("Ew1", [8, E, 4096])
        din("Ew2", [8, 4096, E])
    if "B" in phases or "D" in phases:
        din("iotachf", [128, 32])
        din("iotagf", [128, 1])
    last = phases[-1]
    if last == "D":
        out_ap = nc.dram_tensor("out", [T, E], F32, kind="ExternalOutput").ap()
    else:
        out_ap = nc.dram_tensor("xTo", [E, T], F32, kind="ExternalOutput").ap()

    with ExitStack() as ctx:
        def sb(name, shape, dt, side="left"):
            return ctx.enter_context(nc.sbuf_tensor(name, shape, dt, side=side))

        P = Prog(nc, ctx)
        g = {"ins": ins, "sb": sb}

        x = sb("x", [128, NCH, T], F32, side="right")
        g["x"] = x
        ones128 = sb("ones128", [128, 128], BF16)
        g["eyeb"] = sb("eyeb", [128, 128], BF16)
        g["eyef"] = sb("eyef", [128, 128], F32)
        g["cbias"] = sb("cbias", [128, 2], F32)
        xln = sb("xln", [128, NCH, T], BF16)
        g["xln"] = xln
        ln_stt = sb("ln_stt", [128, TT], F32)
        g["ln"] = {
            "ones128": ones128,
            "sq2": ln_stt[:].bitcast(BF16),
            "stm": sb("ln_stm", [128, TT], F32),
            "stv": sb("ln_stv", [128, TT], F32),
            "sttmp": ln_stt,
        }
        g["bmsum"] = sb("bmsum", [128, NCH], F32)
        g["jtok"] = sb("jtok", [128, 16, 8], U32)
        g["psum"] = [ctx.enter_context(nc.psum_tensor(f"pb{i}", [128, TT], F32))
                     for i in range(8)]
        if "B" in phases or "D" in phases:
            g["iotaf"] = sb("iotaf", [128, 32], F32)
            g["iotag"] = sb("iotag", [128, 1], F32)
            P.op("sync", lambda e: e.dma_start(out=g["iotaf"][:],
                                               in_=ins["iotachf"]), dma=True,
                 writes=["consts"])
            P.op("sync", lambda e: e.dma_start(out=g["iotag"][:],
                                               in_=ins["iotagf"]), dma=True,
                 writes=["consts"])
        g["bg"] = []
        if "D" in phases:
            w1b = nc.dram_tensor("L1w1b", [8, E, 4096], BF16,
                                 kind="Internal").ap()
            w2b = nc.dram_tensor("L1w2b", [8, 4096, E], BF16,
                                 kind="Internal").ap()
            g["w1b"], g["w2b"] = w1b, w2b
            for ex in range(8):
                for half in range(2):
                    hs = slice(half * 512, (half + 1) * 512)
                    hs2 = slice(half * 2048, (half + 1) * 2048)
                    g["bg"].append(lambda P=P, ex=ex, hs=hs: P.op(
                        "gpsimd", lambda e: e.dma_start(
                            out=w1b[ex, hs, :], in_=ins["Ew1"][ex, hs, :]),
                        dma=True, writes=[f"L1s1{ex}"]))
                    g["bg"].append(lambda P=P, ex=ex, hs2=hs2: P.op(
                        "gpsimd", lambda e: e.dma_start(
                            out=w2b[ex, hs2, :], in_=ins["Ew2"][ex, hs2, :]),
                        dma=True, writes=[f"L1s2{ex}"]))
        arena_bytes = (nc.sbuf_bytes_remaining - 64) // 32 * 32
        arena = sb("arena", [128, arena_bytes], mybir.dt.int8)
        g["ov"] = Overlay(nc, arena)

        P.op("vector", lambda e: e.memset(ones128[:], 1.0), writes=["consts"])
        P.op("vector", lambda e: e.memset(g["cbias"][:, 0:1], -C_L0),
             writes=["consts"])
        P.op("vector", lambda e: e.memset(g["cbias"][:, 1:2], -C_L1),
             writes=["consts"])
        P.op("gpsimd", lambda e: e.dma_start(out=g["eyeb"][:], in_=ins["eye"]),
             dma=True, writes=["consts"])
        P.op("sync", lambda e: e.dma_start(out=g["eyef"][:], in_=ins["eye"]),
             dma=True, writes=["consts"])
        xTr = ins["xT"].rearrange("(c p) t -> p c t", p=128)
        for tt in range(NTT):
            s = slice(tt * TT, (tt + 1) * TT)
            P.op("sync", lambda e, s=s: e.dma_start(out=x[:, :, s],
                                                    in_=xTr[:, :, s]), dma=True,
                 writes=[f"rx{c}_{tt}" for c in range(NCH)])

        for i, ph in enumerate(phases):
            if ph == "A":
                emit_ln(P, g)
                g["ov"].reset()
                emit_phaseA(P, ctx, g)
            elif ph == "B":
                emit_ln(P, g, gate_msum=g["bmsum"], gtag="B")
                P.barrier()
                g["ov"].reset()
                emit_phaseB(P, ctx, g)
            elif ph == "C":
                emit_ln(P, g)
                P.barrier()
                g["ov"].reset()
                emit_phaseC(P, ctx, g)
            elif ph == "D":
                emit_ln(P, g)
                P.barrier()
                g["ov"].reset()
                emit_phaseD(P, ctx, g, out_ap)
            else:
                raise NotImplementedError(ph)

        if last != "D":
            P.barrier()
            outr = out_ap.rearrange("(c p) t -> p c t", p=128)
            P.op("sync", lambda e: e.dma_start(out=outr, in_=x[:]), dma=True,
                 reads=[f"rx{c}_{t}" for c in range(NCH) for t in range(NTT)],
                 writes=["out"])

        block = ctx.enter_context(nc.Block())
        P.emit(block)
    return nc



def _host_consts():
    pos = np.arange(T, dtype=np.float32)[:, None]
    div = np.exp(np.arange(0, D0, 2, dtype=np.float32) * (-np.log(10000.0) / D0))
    ang = pos * div
    pe = np.stack([np.sin(ang), np.cos(ang)], axis=-1).reshape(T, D0)
    peT = np.ascontiguousarray(pe.T.astype(ml_dtypes.bfloat16))
    r = np.arange(128)[:, None]
    w = np.arange(896)[None, :]
    cmask = ((w - 384) >= r).astype(ml_dtypes.bfloat16)
    eye = np.eye(128, dtype=np.float32)
    return {"peT": peT, "cmask": cmask, "eye": eye}


def _host_shared(inputs, phases):
    sh = dict(_host_consts())
    f32 = lambda a: np.ascontiguousarray(np.asarray(a, np.float32))
    if "A" in phases:
        sh["wqkvT"] = f32(np.asarray(inputs["L0_attn_Wqkv"]).T)
        sh["woT"] = f32(np.asarray(inputs["L0_attn_Wo"]).T)
    if "B" in phases:
        sh["wgate"] = f32(inputs["L0_ffn_w_gate"])
        sh["bgate"] = f32(inputs["L0_ffn_b_gate"]).reshape(1, 8)
        sh["alpha"] = f32(inputs["L0_ffn_alpha"]).reshape(1, 1)
        sh["W1m"] = f32(inputs["L0_ffn_W1m"])
        sh["W2m"] = f32(inputs["L0_ffn_W2m"])
        sh["b1mcm"] = f32(np.asarray(inputs["L0_ffn_b1m"]).reshape(32, 128).T)
        sh["b2mcm"] = f32(np.asarray(inputs["L0_ffn_b2m"]).reshape(8, 128).T)
        sh["W1e"] = f32(inputs["L0_ffn_W1e"])
        sh["W2e"] = f32(inputs["L0_ffn_W2e"])
        sh["b1e"] = f32(inputs["L0_ffn_b1e"])
        sh["b2e"] = f32(inputs["L0_ffn_b2e"])
    if "C" in phases:
        cat = lambda w: f32(np.asarray(w).transpose(1, 0, 2).reshape(E, E))
        sh["Wqc"] = cat(inputs["L1_attn_W_q"])
        sh["Wkc"] = cat(inputs["L1_attn_W_k"])
        sh["Wvc"] = cat(inputs["L1_attn_W_v"])
        sh["Wo1"] = f32(inputs["L1_attn_W_o"])
    if "D" in phases:
        tid = (np.arange(16)[None, :] * 128 + np.arange(128)[:, None])
        sh["tidhl"] = np.ascontiguousarray(
            np.stack([tid // 256, tid % 256], axis=-1).astype(np.float32))
        sh["gwT"] = f32(np.asarray(inputs["L1_ffn_gate_w"]).T)
        sh["Ew1"] = f32(inputs["L1_ffn_w1"])
        sh["Ew2"] = f32(inputs["L1_ffn_w2"])
    if "B" in phases or "D" in phases:
        p = np.arange(128, dtype=np.float32)[:, None]
        gg = np.arange(32, dtype=np.float32)[None, :]
        sh["iotachf"] = np.ascontiguousarray(gg * 128 + p)
        sh["iotagf"] = np.ascontiguousarray(p)
    return sh


def kernel(**inputs):
    x = np.asarray(inputs["L0_attn_x"], np.float32)
    phase_groups = [_PHASES] if _FUSED else list(_PHASES)
    xcur = [np.ascontiguousarray(x[c].T) for c in range(8)]
    out = None
    for ph in phase_groups:
        sh = _host_shared(inputs, ph)
        nc = _build(ph)
        in_maps = [dict(sh, xT=xcur[c]) for c in range(8)]
        res = run_bass_kernel_spmd(nc, in_maps, core_ids=list(range(8)))
        if ph[-1] == "D":
            out = np.stack([np.asarray(res.results[c]["out"]) for c in range(8)])
        else:
            xcur = [np.ascontiguousarray(np.asarray(res.results[c]["xTo"]))
                    for c in range(8)]
    if out is None:
        out = np.stack([xcur[c].T for c in range(8)])
    return out


# version 15
# speedup vs baseline: 1.2033x; 1.0055x over previous
import math
from contextlib import ExitStack

import numpy as np

import ml_dtypes

import concourse.bass as bass
import concourse.mybir as mybir
from concourse.bass_core_rust import IndirectOffsetOnAxis
from concourse.bass_utils import run_bass_kernel_spmd

F32 = mybir.dt.float32
BF16 = mybir.dt.bfloat16
U32 = mybir.dt.uint32
AF = mybir.ActivationFunctionType
ALU = mybir.AluOpType
AX = mybir.AxisListType

T = 2048
E = 1024
NCH = 8
NTT = 4
TT = 512
H0, D0 = 8, 512
C_L0 = 24.0
C_L1 = 4.0

_FUSED = True
_PHASES = "ABCD"


_NQSEM = 12
_QUEUES = ["qsync", "qscalar", "qgp"]
_STREAMS = (["tensor", "vector", "scalar", "gpsimd", "sync"]
            + [f"{q}{i}" for q in _QUEUES for i in range(_NQSEM)])
_QUEUE_OF = {"sync": "qsync", "scalar": "qscalar", "gpsimd": "qgp"}


class Prog:
    def __init__(self, nc, ctx):
        self.nc = nc
        self.sem = {n: ctx.enter_context(nc.semaphore("sem_" + n))
                    for n in _STREAMS}
        self.cnt = {n: 0 for n in _STREAMS}
        self.qrot = {q: 0 for q in _QUEUES}
        self.hwm = {}
        self.writers = {}
        self.readers = {}
        self.items = []

    def op(self, engine, emit, reads=(), writes=(), dma=False):
        waits = {}

        def need(d):
            for s, v in (d or {}).items():
                if waits.get(s, 0) < v:
                    waits[s] = v

        for r in reads:
            need(self.writers.get(r))
        for w in writes:
            need(self.writers.get(w))
            need(self.readers.get(w))
        if dma:
            q = _QUEUE_OF[engine]
            i = self.qrot[q]
            self.qrot[q] = (i + 1) % _NQSEM
            stream, amt = f"{q}{i}", 16
            if self.cnt[stream] and waits.get(stream, 0) < self.cnt[stream]:
                waits[stream] = self.cnt[stream]
        else:
            stream, amt = engine, 1
        real = {}
        for s, v in waits.items():
            if engine == "tensor" and s == "tensor":
                continue
            k = (engine, s)
            if self.hwm.get(k, 0) >= v:
                continue
            self.hwm[k] = v
            real[s] = v
        self.cnt[stream] += amt
        sig = (stream, self.cnt[stream])
        for w in writes:
            d = self.writers.setdefault(w, {})
            d[sig[0]] = sig[1]
            self.readers.pop(w, None)
        for r in reads:
            d = self.readers.setdefault(r, {})
            if d.get(sig[0], 0) < sig[1]:
                d[sig[0]] = sig[1]
        self.items.append((engine, real, emit, stream, amt))

    def barrier(self):
        snapshot = dict(self.cnt)
        for eng in ["tensor", "vector", "scalar", "gpsimd", "sync"]:
            real = {}
            for s, v in snapshot.items():
                if v == 0 or eng == s:
                    continue
                k = (eng, s)
                if self.hwm.get(k, 0) >= v:
                    continue
                self.hwm[k] = v
                real[s] = v
            if real:
                self.items.append((eng, real, None, None, 0))

    def emit(self, block):
        final = dict(self.cnt)

        def body_for(name):
            def body(e):
                for engine, waits, emitf, stream, amt in self.items:
                    if engine != name:
                        continue
                    for s in sorted(waits):
                        e.wait_ge(self.sem[s], waits[s])
                    if emitf is not None:
                        emitf(e).then_inc(self.sem[stream], amt)
                if name == "gpsimd":
                    for s, v in final.items():
                        if v and s != "gpsimd":
                            e.wait_ge(self.sem[s], v)
            return body

        block.tensor(body_for("tensor"))
        block.vector(body_for("vector"))
        block.scalar(body_for("scalar"))
        block.gpsimd(body_for("gpsimd"))
        block.sync(body_for("sync"))


class Overlay:

    def __init__(self, nc, arena):
        self.nc = nc
        self.base = nc.lookup_mloc(arena).addr
        self.size = int(np.prod(arena.shape[1:]))
        self.off = 0
        self.peak = 0

    def reset(self):
        self.off = 0

    def mark(self):
        return self.off

    def rollback(self, m):
        self.off = m

    def sb(self, name, shape, dt):
        nb = int(np.prod(shape[1:])) * mybir.dt.size(dt)
        nb = (nb + 31) // 32 * 32
        assert self.off + nb <= self.size, (
            f"overlay overflow allocating {name}: {self.off}+{nb} > {self.size}")
        t = self.nc.alloc_sbuf_tensor_at(name, shape, dt,
                                         offset=self.base + self.off)
        self.off += nb
        self.peak = max(self.peak, self.off)
        return t



def emit_ln(P, g, gate_msum=None, gtag=""):
    x, xln = g["x"], g["xln"]
    ln = g["ln"]
    ones = ln["ones128"]
    sq2, stm, stv, stt = ln["sq2"], ln["stm"], ln["stv"], ln["sttmp"]
    ps = g["psum"]
    for tt in range(NTT):
        s = slice(tt * TT, (tt + 1) * TT)
        for c in range(NCH):
            P.op("vector",
                 lambda e, c=c, s=s: e.tensor_copy(xln[:, c, s], x[:, c, s]),
                 reads=[f"rx{c}_{tt}"], writes=[f"xb{c}_{tt}", f"xln{c}_{tt}"])
    for tt in range(NTT):
        s = slice(tt * TT, (tt + 1) * TT)
        for c in range(NCH):
            sq = sq2[:, (c % 2) * TT:(c % 2 + 1) * TT]
            P.op("scalar",
                 lambda e, c=c, s=s, sq=sq: e.activation(sq, xln[:, c, s],
                                                         AF.Square),
                 reads=[f"xb{c}_{tt}"], writes=[f"lnsq{c % 2}"])
            P.op("tensor",
                 lambda e, c=c, s=s: e.matmul(ps[0][:], ones[:], xln[:, c, s],
                                              start=(c == 0), stop=(c == NCH - 1)),
                 reads=[f"xb{c}_{tt}", "consts"], writes=["pb0"])
            P.op("tensor",
                 lambda e, c=c, sq=sq: e.matmul(ps[1][:], ones[:], sq,
                                                start=(c == 0), stop=(c == NCH - 1)),
                 reads=[f"lnsq{c % 2}", "consts"], writes=["pb1"])
        P.op("vector", lambda e: e.tensor_scalar(stm[:], ps[0][:], 1.0 / E, None,
                                                 ALU.mult),
             reads=["pb0"], writes=["lnstm"])
        P.op("vector", lambda e: e.tensor_scalar(stv[:], ps[1][:], 1.0 / E, 1e-5,
                                                 ALU.mult, ALU.add),
             reads=["pb1"], writes=["lnstv"])
        P.op("vector", lambda e: e.tensor_mul(stt[:], stm[:], stm[:]),
             reads=["lnstm", "pb1"], writes=["lnstt", "lnsq0", "lnsq1"])
        P.op("vector", lambda e: e.tensor_sub(stv[:], stv[:], stt[:]),
             reads=["lnstv", "lnstt"], writes=["lnstv"])
        P.op("scalar", lambda e: e.activation(stt[:], stv[:], AF.Ln),
             reads=["lnstv"], writes=["lnstt"])
        P.op("scalar", lambda e: e.activation(stv[:], stt[:], AF.Exp, scale=-0.5),
             reads=["lnstt"], writes=["lnstv"])
        P.op("vector", lambda e: e.tensor_mul(stm[:], stm[:], stv[:]),
             reads=["lnstm", "lnstv"], writes=["lnstm"])
        for c in range(NCH):
            P.op("vector",
                 lambda e, c=c, s=s: e.tensor_mul(xln[:, c, s], x[:, c, s], stv[:]),
                 reads=[f"rx{c}_{tt}", "lnstv"],
                 writes=[f"xln{c}_{tt}", f"xb{c}_{tt}"])
            P.op("vector",
                 lambda e, c=c, s=s: e.tensor_sub(xln[:, c, s], xln[:, c, s], stm[:]),
                 reads=["lnstm"],
                 writes=[f"xln{c}_{tt}"])
    if gate_msum is not None:
        for c in range(NCH):
            P.op("vector",
                 lambda e, c=c: e.reduce_sum(gate_msum[:, c:c + 1], xln[:, c, :],
                                             axis=AX.X),
                 reads=[f"xln{c}_{t}" for t in range(NTT)],
                 writes=[f"{gtag}gmsum{c}"])


def drain_bg(g, k):
    for _ in range(k):
        if g["bg"]:
            g["bg"].pop(0)()


def xln_reads(c, tt=None):
    if tt is None:
        return [f"xln{c}_{t}" for t in range(NTT)]
    return [f"xln{c}_{tt}"]


def rx_w(c, tt=None):
    if tt is None:
        return [f"rx{c}_{t}" for t in range(NTT)]
    return [f"rx{c}_{tt}"]



def emit_phaseA(P, ctx, g):
    sb = g["ov"].sb
    x, xln, ps = g["x"], g["xln"], g["psum"]

    wq = sb("A_wq", [128, NCH, 512], BF16)
    wk = sb("A_wk", [128, NCH, 512], BF16)
    wv = sb("A_wv", [128, NCH, 512], BF16)
    wo = sb("A_wo", [128, 4, 1024], BF16)
    pe = sb("A_pe", [128, 4, T], BF16)
    msk = sb("A_msk", [128, 896], BF16)
    q = sb("A_q", [128, 4, T], BF16)
    k = sb("A_k", [128, 4, T], BF16)
    v = sb("A_v", [128, 16, TT], BF16)
    pbuf = sb("A_p", [128, 2, TT], BF16)
    att = sb("A_att", [128, 4, TT], BF16)
    rrep = pbuf[:].bitcast(F32).rearrange("p a b -> p (a b)")

    peT = g["ins"]["peT"].rearrange("(c p) t -> p c t", p=128)
    P.op("scalar", lambda e: e.dma_start(out=pe[:], in_=peT), dma=True,
         writes=["A_pe"])
    P.op("scalar", lambda e: e.dma_start(out=msk[:], in_=g["ins"]["cmask"]),
         dma=True, writes=["A_msk"])

    wqkvT = g["ins"]["wqkvT"].rearrange("(c p) f -> p c f", p=128)
    woT = g["ins"]["woT"].rearrange("(c p) e2 -> p c e2", p=128)

    for h in range(H0):
        drain_bg(g, 2)
        base = h * 3 * D0
        P.op("gpsimd", lambda e, base=base: e.dma_start(
            out=wq[:], in_=wqkvT[:, :, base:base + 512]), dma=True,
            writes=["A_wq"])
        P.op("gpsimd", lambda e, base=base: e.dma_start(
            out=wk[:], in_=wqkvT[:, :, base + 512:base + 1024]), dma=True,
            writes=["A_wk"])
        P.op("gpsimd", lambda e, base=base: e.dma_start(
            out=wv[:], in_=wqkvT[:, :, base + 1024:base + 1536]), dma=True,
            writes=["A_wv"])
        P.op("gpsimd", lambda e, h=h: e.dma_start(
            out=wo[:], in_=woT[:, 4 * h:4 * h + 4, :]), dma=True,
            writes=["A_wo"])

        for d4 in range(4):
            for tt in range(NTT):
                s = slice(tt * TT, (tt + 1) * TT)
                for name, wt, dst, bank in (("q", wq, q, 2 + (d4 * NTT + tt) % 2),
                                            ("k", wk, k, 4 + (d4 * NTT + tt) % 2)):
                    for c in range(NCH):
                        P.op("tensor", lambda e, c=c, d4=d4, s=s, wt=wt, bank=bank:
                             e.matmul(ps[bank][:], wt[:, c, d4 * 128:(d4 + 1) * 128],
                                      xln[:, c, s], start=(c == 0),
                                      stop=(c == NCH - 1)),
                             reads=[f"A_w{name}"] + xln_reads(c, tt),
                             writes=[f"pb{bank}"])
                    P.op("vector", lambda e, d4=d4, s=s, dst=dst, bank=bank:
                         e.tensor_add(dst[:, d4, s], ps[bank][:], pe[:, d4, s]),
                         reads=[f"pb{bank}", "A_pe"],
                         writes=[f"A_{name}{d4}_{tt}"])

        for tc in range(16):
            bank = 6 + tc % 2
            for c in range(NCH):
                P.op("tensor", lambda e, c=c, tc=tc, bank=bank:
                     e.matmul(ps[bank][:], xln[:, c, tc * 128:(tc + 1) * 128],
                              wv[:, c, :], start=(c == 0), stop=(c == NCH - 1)),
                     reads=["A_wv"] + xln_reads(c, tc // 4),
                     writes=[f"pb{bank}"])
            P.op("vector", lambda e, tc=tc, bank=bank:
                 e.tensor_copy(v[:, tc, :], ps[bank][:]),
                 reads=[f"pb{bank}"], writes=[f"A_v{tc}"])

        def a_sexp(qj, ci):
            sqj = slice(qj * TT, (qj + 1) * TT)
            sbank = 4 + ci % 2
            pp = ci % 2
            for d4 in range(4):
                P.op("tensor", lambda e, ci=ci, d4=d4, sqj=sqj, sbank=sbank:
                     e.matmul(ps[sbank][:], k[:, d4, ci * 128:(ci + 1) * 128],
                              q[:, d4, sqj], start=(d4 == 0), stop=(d4 == 3)),
                     reads=[f"A_k{d4}_{ci // 4}", f"A_q{d4}_{qj}"],
                     writes=[f"pb{sbank}"])
            P.op("scalar", lambda e, pp=pp, sbank=sbank:
                 e.activation(pbuf[:, pp, :], ps[sbank][:], AF.Exp,
                              bias=g["cbias"][:, 0:1],
                              scale=1.0 / math.sqrt(D0)),
                 reads=[f"pb{sbank}", "consts"],
                 writes=[f"A_P{pp}", "A_rrep"])
            if ci >= 4 * qj:
                o = ci - 4 * qj
                ms = slice(384 - 128 * o, 896 - 128 * o)
                P.op("vector", lambda e, pp=pp, ms=ms:
                     e.tensor_mul(pbuf[:, pp, :], pbuf[:, pp, :], msk[:, ms]),
                     reads=["A_msk"], writes=[f"A_P{pp}"])

        def a_pv(qj, ci):
            nck = 4 * qj + 4
            pp = ci % 2
            for d4 in range(4):
                P.op("tensor", lambda e, ci=ci, d4=d4, pp=pp, nck=nck:
                     e.matmul(ps[d4][:], v[:, ci, d4 * 128:(d4 + 1) * 128],
                              pbuf[:, pp, :], start=(ci == 0),
                              stop=(ci == nck - 1)),
                     reads=[f"A_v{ci}", f"A_P{pp}"], writes=[f"pb{d4}"])
            P.op("tensor", lambda e, ci=ci, pp=pp, nck=nck:
                 e.matmul(ps[6][:], g["ln"]["ones128"][:], pbuf[:, pp, :],
                          start=(ci == 0), stop=(ci == nck - 1)),
                 reads=[f"A_P{pp}", "consts"], writes=["pb6"])

        for qj in range(NTT):
            sqj = slice(qj * TT, (qj + 1) * TT)
            nck = 4 * qj + 4
            a_sexp(qj, 0)
            for ci in range(1, nck):
                a_sexp(qj, ci)
                a_pv(qj, ci - 1)
            a_pv(qj, nck - 1)
            P.op("scalar", lambda e: e.activation(ps[6][:], ps[6][:], AF.Ln),
                 reads=["pb6"], writes=["pb6"])
            P.op("scalar", lambda e: e.activation(rrep[:], ps[6][:], AF.Exp,
                                                  scale=-1.0),
                 reads=["pb6"], writes=["A_rrep"])
            for d4 in range(4):
                P.op("vector", lambda e, d4=d4:
                     e.tensor_mul(att[:, d4, :], ps[d4][:], rrep[:]),
                     reads=[f"pb{d4}", "A_rrep"], writes=[f"A_att{d4}"])
            for ec in range(NCH):
                bank = 6 + ec % 2
                for f4 in range(4):
                    P.op("tensor", lambda e, f4=f4, ec=ec, bank=bank:
                         e.matmul(ps[bank][:],
                                  wo[:, f4, ec * 128:(ec + 1) * 128],
                                  att[:, f4, :], start=(f4 == 0),
                                  stop=(f4 == 3)),
                         reads=["A_wo", f"A_att{f4}"], writes=[f"pb{bank}"])
                P.op("vector", lambda e, ec=ec, sqj=sqj, bank=bank:
                     e.tensor_add(x[:, ec, sqj], x[:, ec, sqj], ps[bank][:]),
                     reads=[f"pb{bank}"], writes=[f"rx{ec}_{qj}"])



def emit_phaseB(P, ctx, g):
    sb = g["ov"].sb
    x, xln, ps = g["x"], g["xln"], g["psum"]
    ins = g["ins"]

    w1sl = sb("B_w1", [128, 2, NCH, 1024], BF16)
    w2sl = sb("B_w2", [128, 2, NCH, 1024], BF16)
    hT = sb("B_h", [128, NCH, T], BF16)
    wg = sb("B_wg", [128, NCH, 8], F32)
    msum = g["bmsum"]
    gsc = sb("B_gsc", [1, 64], F32)
    bgt = sb("B_bg", [1, 8], F32)
    alv = sb("B_al", [1, 1], F32)
    b1m = sb("B_b1m", [128, 32], F32)
    b2m = sb("B_b2m", [128, 8], F32)
    b1x = sb("B_b1x", [128, 2, 32], F32)
    b2x = sb("B_b2x", [128, 2, 8], F32)
    bcomb = sb("B_bc", [128, 8], F32)
    crep = sb("B_crep", [128, 3], F32)
    selrep = sb("B_selrep", [128, 2], F32)
    iotaf, iotag = g["iotaf"], g["iotag"]
    idxf = sb("B_idxf", [128, 33], F32)
    w1idx = sb("B_w1idx", [128, 2, 8], U32)
    w2idx = sb("B_w2idx", [128, 2, 32], U32)
    bidxf = sb("B_bidxf", [32, 1], F32)
    onesr = sb("B_onesr", [1, 128], F32)
    bidx = sb("B_bidx", [32, 1], U32)
    btmp = sb("B_btmp", [32, 128], F32)

    P.op("vector", lambda e: e.memset(onesr[:], 1.0), writes=["B_onesr"])
    P.op("sync", lambda e: e.dma_start(out=wg[:], in_=ins["wgate"].rearrange(
        "(c p) e8 -> p c e8", p=128)), dma=True, writes=["B_wg"])
    P.op("sync", lambda e: e.dma_start(out=bgt[:], in_=ins["bgate"]), dma=True,
         writes=["B_bg"])
    P.op("sync", lambda e: e.dma_start(out=alv[:], in_=ins["alpha"]), dma=True,
         writes=["B_al"])
    P.op("sync", lambda e: e.dma_start(out=b1m[:], in_=ins["b1mcm"]), dma=True,
         writes=["B_b1m"])
    P.op("sync", lambda e: e.dma_start(out=b2m[:], in_=ins["b2mcm"]), dma=True,
         writes=["B_b2m"])

    W1mv = ins["W1m"].rearrange("(c p) hh -> p c hh", p=128)
    W2mv = ins["W2m"].rearrange("(gg p) ee -> p gg ee", p=128)
    W1ef = ins["W1e"].rearrange("e d h -> (e d) h")
    W2ef = ins["W2e"].rearrange("e h d -> (e h) d")

    def load_w1(path, s, par):
        if path == "m":
            P.op("gpsimd", lambda e, s=s, par=par: e.dma_start(
                out=w1sl[:, par, :, :],
                in_=W1mv[:, :, s * 1024:(s + 1) * 1024]), dma=True,
                writes=[f"B_w1b{par}c{c}" for c in range(NCH)])
        else:
            k = 0 if path == "a" else 1
            for c in range(NCH):
                P.op("gpsimd", lambda e, s=s, par=par, c=c, k=k:
                     e.indirect_dma_start(
                         out=w1sl[:, par, c, :], out_offset=None, in_=W1ef,
                         in_offset=IndirectOffsetOnAxis(ap=w1idx[:, k, :][:, c:c + 1],
                                                        axis=0),
                         element_offset=s * 1024), dma=True,
                     reads=[f"B_w1idx{k}"], writes=[f"B_w1b{par}c{c}"])

    def load_w2(path, s, par):
        if path == "m":
            P.op("gpsimd", lambda e, s=s, par=par: e.dma_start(
                out=w2sl[:, par, :, :],
                in_=W2mv[:, s * 8:(s + 1) * 8, :]), dma=True,
                writes=[f"B_w2b{par}c{c}" for c in range(NCH)])
        else:
            k = 0 if path == "a" else 1
            for gg in range(NCH):
                P.op("gpsimd", lambda e, s=s, par=par, gg=gg, k=k:
                     e.indirect_dma_start(
                         out=w2sl[:, par, gg, :], out_offset=None, in_=W2ef,
                         in_offset=IndirectOffsetOnAxis(
                             ap=w2idx[:, k, :][:, s * 8 + gg:s * 8 + gg + 1],
                             axis=0)), dma=True,
                     reads=[f"B_w2idx{k}"], writes=[f"B_w2b{par}c{gg}"])

    load_w1("m", 0, 0)
    load_w2("m", 0, 0)

    gs = lambda a, b: gsc[:, a:b]
    for c in range(NCH):
        P.op("tensor", lambda e, c=c: e.matmul(ps[0][0:1, 0:8], msum[:, c:c + 1],
                                               wg[:, c, :], start=(c == 0),
                                               stop=(c == NCH - 1)),
             reads=[f"Bgmsum{c}", "B_wg"], writes=["pb0"])
    P.op("vector", lambda e: e.tensor_scalar(gs(0, 8), ps[0][0:1, 0:8],
                                             1.0 / T, None, ALU.mult),
         reads=["pb0"], writes=["Bg_lg"])
    P.op("vector", lambda e: e.tensor_add(gs(0, 8), gs(0, 8), bgt[:]),
         reads=["B_bg"], writes=["Bg_lg"])
    P.op("vector", lambda e: e.reduce_max(gs(8, 9), gs(0, 8), axis=AX.X),
         reads=["Bg_lg"], writes=["Bg_mx"])
    P.op("vector", lambda e: e.tensor_scalar(gs(9, 10), gs(8, 9), -1.0, None,
                                             ALU.mult),
         reads=["Bg_mx"], writes=["Bg_nmx"])
    P.op("scalar", lambda e: e.activation(gs(16, 24), gs(0, 8), AF.Exp,
                                          bias=gs(9, 10)),
         reads=["Bg_lg", "Bg_nmx"], writes=["Bg_ex"])
    P.op("vector", lambda e: e.reduce_sum(gs(10, 11), gs(16, 24), axis=AX.X),
         reads=["Bg_ex"], writes=["Bg_sm"])
    P.op("vector", lambda e: e.reciprocal(gs(11, 12), gs(10, 11)),
         reads=["Bg_sm"], writes=["Bg_rs"])
    P.op("vector", lambda e: e.tensor_scalar(gs(24, 32), gs(16, 24), gs(11, 12),
                                             None, ALU.mult),
         reads=["Bg_ex", "Bg_rs"], writes=["Bg_p"])
    P.op("vector", lambda e: e.reduce_max(gs(12, 13), gs(24, 32), axis=AX.X),
         reads=["Bg_p"], writes=["Bg_m1"])
    P.op("vector", lambda e: e.tensor_scalar(gs(32, 40), gs(24, 32), gs(12, 13),
                                             None, ALU.is_ge),
         reads=["Bg_p", "Bg_m1"], writes=["Bg_is"])
    P.op("vector", lambda e: e.tensor_mul(gs(40, 48), gs(32, 40), iotaf[0:1, 0:8]),
         reads=["Bg_is", "consts"], writes=["Bg_t"])
    P.op("vector", lambda e: e.reduce_sum(gs(13, 14), gs(40, 48), axis=AX.X),
         reads=["Bg_t"], writes=["Bg_i1"])
    P.op("vector", lambda e: e.tensor_scalar(gs(13, 14), gs(13, 14), 1.0 / 128,
                                             None, ALU.mult),
         reads=["Bg_i1"], writes=["Bg_i1"])
    P.op("vector", lambda e: e.tensor_scalar(gs(32, 40), gs(32, 40), 2.0, None,
                                             ALU.mult),
         reads=["Bg_is"], writes=["Bg_is"])
    P.op("vector", lambda e: e.tensor_sub(gs(48, 56), gs(24, 32), gs(32, 40)),
         reads=["Bg_p", "Bg_is"], writes=["Bg_pw"])
    P.op("vector", lambda e: e.reduce_max(gs(14, 15), gs(48, 56), axis=AX.X),
         reads=["Bg_pw"], writes=["Bg_m2"])
    P.op("vector", lambda e: e.tensor_scalar(gs(32, 40), gs(48, 56), gs(14, 15),
                                             None, ALU.is_ge),
         reads=["Bg_pw", "Bg_m2"], writes=["Bg_is"])
    P.op("vector", lambda e: e.tensor_mul(gs(40, 48), gs(32, 40), iotaf[0:1, 0:8]),
         reads=["Bg_is", "consts"], writes=["Bg_t"])
    P.op("vector", lambda e: e.reduce_sum(gs(15, 16), gs(40, 48), axis=AX.X),
         reads=["Bg_t"], writes=["Bg_i2"])
    P.op("vector", lambda e: e.tensor_scalar(gs(15, 16), gs(15, 16), 1.0 / 128,
                                             None, ALU.mult),
         reads=["Bg_i2"], writes=["Bg_i2"])
    P.op("vector", lambda e: e.tensor_add(gs(56, 57), gs(12, 13), gs(14, 15)),
         reads=["Bg_m1", "Bg_m2"], writes=["Bg_den"])
    P.op("vector", lambda e: e.reciprocal(gs(57, 58), gs(56, 57)),
         reads=["Bg_den"], writes=["Bg_rden"])
    P.op("vector", lambda e: e.tensor_mul(gs(58, 59), gs(12, 13), gs(57, 58)),
         reads=["Bg_m1", "Bg_rden"], writes=["Bg_gA"])
    P.op("vector", lambda e: e.tensor_mul(gs(59, 60), gs(14, 15), gs(57, 58)),
         reads=["Bg_m2", "Bg_rden"], writes=["Bg_gB"])
    P.op("vector", lambda e: e.tensor_scalar(gs(60, 61), alv[:], 0.1, 1.0,
                                             ALU.max, ALU.min),
         reads=["B_al"], writes=["Bg_a"])
    P.op("vector", lambda e: e.tensor_scalar(gs(61, 62), gs(60, 61), -1.0, 1.0,
                                             ALU.mult, ALU.add),
         reads=["Bg_a"], writes=["Bg_oma"])
    P.op("vector", lambda e: e.tensor_mul(gs(62, 63), gs(61, 62), gs(58, 59)),
         reads=["Bg_oma", "Bg_gA"], writes=["Bg_cA"])
    P.op("vector", lambda e: e.tensor_mul(gs(63, 64), gs(61, 62), gs(59, 60)),
         reads=["Bg_oma", "Bg_gB"], writes=["Bg_cB"])
    for i, (a0, rr) in enumerate([(60, "Bg_a"), (62, "Bg_cA"), (63, "Bg_cB")]):
        P.op("tensor", lambda e, a0=a0, i=i: e.matmul(
            ps[1][:, i:i + 1], onesr[:], gsc[:, a0:a0 + 1],
            start=True, stop=True),
            reads=[rr, "B_onesr"], writes=["pb1"])
    P.op("vector", lambda e: e.tensor_copy(crep[:], ps[1][:, 0:3]),
         reads=["pb1"], writes=["B_crep"])
    for i, (a0, rr) in enumerate([(13, "Bg_i1"), (15, "Bg_i2")]):
        P.op("tensor", lambda e, a0=a0, i=i: e.matmul(
            ps[2][:, i:i + 1], onesr[:], gsc[:, a0:a0 + 1],
            start=True, stop=True),
            reads=[rr, "B_onesr"], writes=["pb2"])
    P.op("vector", lambda e: e.tensor_copy(selrep[:], ps[2][:, 0:2]),
         reads=["pb2"], writes=["B_selrep"])

    b1ef = ins["b1e"].rearrange("e (gg j) -> (e gg) j", j=128)
    b2ef = ins["b2e"].rearrange("e (gg j) -> (e gg) j", j=128)
    for k in range(2):
        sc = selrep[:, k:k + 1]
        P.op("vector", lambda e, sc=sc: e.tensor_scalar(
            idxf[:, 0:1], sc, 1024.0, None, ALU.mult),
            reads=["B_selrep"], writes=["B_idxt"])
        P.op("vector", lambda e, k=k: e.tensor_scalar(
            idxf[:, 1:9], iotaf[:, 0:8], idxf[:, 0:1], None, ALU.add),
            reads=["B_idxt", "consts"], writes=["B_idxf"])
        P.op("vector", lambda e, k=k: e.tensor_copy(w1idx[:, k, :], idxf[:, 1:9]),
             reads=["B_idxf"], writes=[f"B_w1idx{k}"])
        P.op("vector", lambda e, sc=sc: e.tensor_scalar(
            idxf[:, 0:1], sc, 4096.0, None, ALU.mult),
            reads=["B_selrep"], writes=["B_idxt"])
        P.op("vector", lambda e: e.tensor_scalar(
            idxf[:, 1:33], iotaf[:], idxf[:, 0:1], None, ALU.add),
            reads=["B_idxt", "consts"], writes=["B_idxf"])
        P.op("vector", lambda e, k=k: e.tensor_copy(w2idx[:, k, :], idxf[:, 1:33]),
             reads=["B_idxf"], writes=[f"B_w2idx{k}"])
        P.op("vector", lambda e, sc=sc: e.tensor_scalar(
            bidxf[:], sc[0:32, :], 32.0, None, ALU.mult),
            reads=["B_selrep"], writes=["B_bidxf"])
        P.op("vector", lambda e: e.tensor_scalar(
            bidxf[:], iotag[0:32, :], bidxf[:], None, ALU.add),
            reads=["B_bidxf", "consts"], writes=["B_bidxf"])
        P.op("vector", lambda e: e.tensor_copy(bidx[:], bidxf[:]),
             reads=["B_bidxf"], writes=["B_bidx"])
        P.op("gpsimd", lambda e: e.indirect_dma_start(
            out=btmp[:], out_offset=None, in_=b1ef,
            in_offset=IndirectOffsetOnAxis(ap=bidx[:], axis=0)), dma=True,
            reads=["B_bidx"], writes=["B_btmp"])
        P.op("tensor", lambda e: e.transpose(ps[3][:, 0:32], btmp[:],
                                             g["eyef"][0:32, 0:32]),
             reads=["B_btmp", "consts"], writes=["pb3"])
        P.op("vector", lambda e, k=k: e.tensor_copy(b1x[:, k, :], ps[3][:, 0:32]),
             reads=["pb3"], writes=[f"B_b1x{k}"])
        P.op("vector", lambda e, sc=sc: e.tensor_scalar(
            bidxf[0:8, :], sc[0:8, :], 8.0, None, ALU.mult),
            reads=["B_selrep"], writes=["B_bidxf"])
        P.op("vector", lambda e: e.tensor_scalar(
            bidxf[0:8, :], iotag[0:8, :], bidxf[0:8, :], None, ALU.add),
            reads=["B_bidxf", "consts"], writes=["B_bidxf"])
        P.op("vector", lambda e: e.tensor_copy(bidx[0:8, :], bidxf[0:8, :]),
             reads=["B_bidxf"], writes=["B_bidx"])
        P.op("gpsimd", lambda e: e.indirect_dma_start(
            out=btmp[0:8, :], out_offset=None, in_=b2ef,
            in_offset=IndirectOffsetOnAxis(ap=bidx[0:8, :], axis=0)), dma=True,
            reads=["B_bidx"], writes=["B_btmp"])
        P.op("tensor", lambda e: e.transpose(ps[3][:, 0:8], btmp[0:8, :],
                                             g["eyef"][0:8, 0:8]),
             reads=["B_btmp", "consts"], writes=["pb3"])
        P.op("vector", lambda e, k=k: e.tensor_copy(b2x[:, k, :], ps[3][:, 0:8]),
             reads=["pb3"], writes=[f"B_b2x{k}"])
    P.op("vector", lambda e: e.tensor_scalar(bcomb[:], b2m[:], crep[:, 0:1],
                                             None, ALU.mult),
         reads=["B_b2m", "B_crep"], writes=["B_bc"])
    for k in range(2):
        P.op("vector", lambda e, k=k: e.tensor_scalar(
            idxf[:, 0:8], b2x[:, k, :], crep[:, k + 1:k + 2], None, ALU.mult),
            reads=[f"B_b2x{k}", "B_crep"], writes=["B_idxf"])
        P.op("vector", lambda e: e.tensor_add(bcomb[:], bcomb[:], idxf[:, 0:8]),
             reads=["B_idxf"], writes=["B_bc"])

    paths = [("m", b1m, 0), ("a", b1x[:, 0, :], 1), ("b", b1x[:, 1, :], 2)]
    sl_ctr = 0
    for pi, (path, b1c, ci_) in enumerate(paths):
        for s in range(4):
            drain_bg(g, 2)
            par = sl_ctr % 2
            if s < 3:
                load_w1(path, s + 1, (sl_ctr + 1) % 2)
                load_w2(path, s + 1, (sl_ctr + 1) % 2)
            elif pi < 2:
                load_w1(paths[pi + 1][0], 0, (sl_ctr + 1) % 2)
                load_w2(paths[pi + 1][0], 0, (sl_ctr + 1) % 2)
            for tt in range(NTT):
                for hc in range(NCH):
                    bank = (tt * NCH + hc) % 4
                    st_ = slice(tt * TT, (tt + 1) * TT)
                    for c in range(NCH):
                        P.op("tensor", lambda e, c=c, hc=hc, st_=st_, par=par, bank=bank:
                             e.matmul(ps[bank][:],
                                      w1sl[:, par, c, hc * 128:(hc + 1) * 128],
                                      xln[:, c, st_], start=(c == 0),
                                      stop=(c == NCH - 1)),
                             reads=[f"B_w1b{par}c{c}"] + xln_reads(c, tt),
                             writes=[f"pb{bank}"])
                    gx = s * 8 + hc
                    P.op("scalar", lambda e, hc=hc, st_=st_, bank=bank, b1c=b1c, gx=gx:
                         e.activation(hT[:, hc, st_], ps[bank][:],
                                      AF.Gelu_apprx_tanh,
                                      bias=b1c[:, gx:gx + 1]),
                         reads=[f"pb{bank}", "B_b1m", "B_b1x0", "B_b1x1"],
                         writes=[f"B_h{hc}_{tt}"])
            for tt in range(NTT):
                for ec in range(NCH):
                    bank = 4 + (tt * NCH + ec) % 4
                    st_ = slice(tt * TT, (tt + 1) * TT)
                    for hc in range(NCH):
                        P.op("tensor", lambda e, hc=hc, ec=ec, st_=st_, par=par, bank=bank:
                             e.matmul(ps[bank][:],
                                      w2sl[:, par, hc, ec * 128:(ec + 1) * 128],
                                      hT[:, hc, st_], start=(hc == 0),
                                      stop=(hc == NCH - 1)),
                             reads=[f"B_w2b{par}c{hc}", f"B_h{hc}_{tt}"],
                             writes=[f"pb{bank}"])
                    P.op("vector", lambda e, ec=ec, st_=st_, bank=bank, ci_=ci_:
                         e.scalar_tensor_tensor(x[:, ec, st_], ps[bank][:],
                                                crep[:, ci_:ci_ + 1],
                                                x[:, ec, st_],
                                                ALU.mult, ALU.add),
                         reads=[f"pb{bank}", "B_crep"], writes=[f"rx{ec}_{tt}"])
            sl_ctr += 1
    for c in range(NCH):
        for tt in range(NTT):
            st_ = slice(tt * TT, (tt + 1) * TT)
            P.op("scalar", lambda e, c=c, st_=st_: e.add(
                x[:, c, st_], x[:, c, st_], bcomb[:, c:c + 1]),
                reads=["B_bc"], writes=[f"rx{c}_{tt}"])



def emit_phaseC(P, ctx, g):
    sb = g["ov"].sb
    x, xln, ps = g["x"], g["xln"], g["psum"]
    ins = g["ins"]

    wqp = sb("C_wq", [128, 2, NCH, 128], BF16)
    wkp = sb("C_wk", [128, 2, NCH, 128], BF16)
    wvp = sb("C_wv", [128, 2, NCH, 128], BF16)
    wop = sb("C_wo", [128, 2, 1024], BF16)
    qp = sb("C_qp", [128, 2, T], BF16)
    kp = sb("C_kp", [128, 2, T], BF16)
    vp = sb("C_vp", [128, 2, 16, 256], BF16)
    att = sb("C_att", [128, 2, NTT, TT], BF16)
    pbuf = sb("C_p", [128, 4, TT], BF16)
    rq = sb("C_rq", [128, TT], F32)

    Wq = ins["Wqc"].rearrange("(c p) f -> p c f", p=128)
    Wk = ins["Wkc"].rearrange("(c p) f -> p c f", p=128)
    Wv = ins["Wvc"].rearrange("(c p) f -> p c f", p=128)
    Wo = ins["Wo1"].rearrange("(q p) o -> p q o", p=128)

    P.op("vector", lambda e: e.memset(vp[:, :, :, 64:128], 1.0), writes=["C_vo"])
    P.op("vector", lambda e: e.memset(vp[:, :, :, 192:256], 1.0), writes=["C_vo"])

    def load_pair(pp, par):
        fs = slice(pp * 128, (pp + 1) * 128)
        P.op("gpsimd", lambda e, fs=fs, par=par: e.dma_start(
            out=wqp[:, par, :, :], in_=Wq[:, :, fs]), dma=True,
            writes=[f"C_wq{par}"])
        P.op("gpsimd", lambda e, fs=fs, par=par: e.dma_start(
            out=wkp[:, par, :, :], in_=Wk[:, :, fs]), dma=True,
            writes=[f"C_wk{par}"])
        P.op("gpsimd", lambda e, fs=fs, par=par: e.dma_start(
            out=wvp[:, par, :, :], in_=Wv[:, :, fs]), dma=True,
            writes=[f"C_wv{par}"])
        P.op("gpsimd", lambda e, pp=pp, par=par: e.dma_start(
            out=wop[:, par, :], in_=Wo[:, pp, :]), dma=True,
            writes=[f"C_wo{par}"])

    def proj_units(pp, par):
        units = []
        for tt in range(NTT):
            st_ = slice(tt * TT, (tt + 1) * TT)
            for name, wt, dst in (("q", wqp, qp), ("k", wkp, kp)):
                bank = (4 if name == "q" else 5) + 2 * (tt % 2)

                def u(name=name, wt=wt, dst=dst, bank=bank, st_=st_, tt=tt):
                    for c in range(NCH):
                        P.op("tensor", lambda e, c=c: e.matmul(
                            ps[bank][:], wt[:, par, c, :], xln[:, c, st_],
                            start=(c == 0), stop=(c == NCH - 1)),
                            reads=[f"C_w{name}{par}"] + xln_reads(c, tt),
                            writes=[f"pb{bank}"])
                    P.op("vector", lambda e: e.tensor_copy(dst[:, par, st_],
                                                           ps[bank][:]),
                         reads=[f"pb{bank}"], writes=[f"C_{name}{par}_{tt}"])
                units.append(u)
        for tc in range(16):
            bank = 2 + tc % 2

            def u(tc=tc, bank=bank):
                for c in range(NCH):
                    P.op("tensor", lambda e, c=c: e.matmul(
                        ps[bank][:, 0:128],
                        xln[:, c, tc * 128:(tc + 1) * 128],
                        wvp[:, par, c, :], start=(c == 0),
                        stop=(c == NCH - 1)),
                        reads=[f"C_wv{par}"] + xln_reads(c, tc // 4),
                        writes=[f"pb{bank}"])
                P.op("vector", lambda e: e.tensor_copy(vp[:, par, tc, 0:64],
                                                       ps[bank][:, 0:64]),
                     reads=[f"pb{bank}"], writes=[f"C_v{par}_{tc}"])
                P.op("vector", lambda e: e.tensor_copy(vp[:, par, tc, 128:192],
                                                       ps[bank][:, 64:128]),
                     reads=[f"pb{bank}"], writes=[f"C_v{par}_{tc}"])
            units.append(u)
        return units

    def attn_pair(pp, par):
        def c_sexp(qj, ci):
            sqj = slice(qj * TT, (qj + 1) * TT)
            for hh in range(2):
                hs = slice(hh * 64, (hh + 1) * 64)
                sbank = 4 + hh + 2 * (ci % 2)
                P.op("tensor", lambda e, ci=ci, hs=hs, sqj=sqj, sbank=sbank:
                     e.matmul(ps[sbank][:], kp[hs, par, ci * 128:(ci + 1) * 128],
                              qp[hs, par, sqj], start=True, stop=True),
                     reads=[f"C_k{par}_{ci // 4}", f"C_q{par}_{qj}"],
                     writes=[f"pb{sbank}"])
            for hh in range(2):
                sbank = 4 + hh + 2 * (ci % 2)
                ppx = 2 * (ci % 2) + hh
                P.op("scalar", lambda e, ppx=ppx, sbank=sbank:
                     e.activation(pbuf[:, ppx, :], ps[sbank][:], AF.Exp,
                                  bias=g["cbias"][:, 1:2], scale=1.0 / 8.0),
                     reads=[f"pb{sbank}", "consts"], writes=[f"C_P{ppx}"])

        def c_pv(qj, ci):
            pvb = 2 * (qj % 2)
            for hh in range(2):
                ppx = 2 * (ci % 2) + hh
                P.op("tensor", lambda e, ci=ci, ppx=ppx, hh=hh, pvb=pvb:
                     e.matmul(ps[pvb + hh][:], vp[:, par, ci,
                                                  hh * 128:(hh + 1) * 128],
                              pbuf[:, ppx, :], start=(ci == 0), stop=(ci == 15)),
                     reads=[f"C_v{par}_{ci}", f"C_P{ppx}", "C_vo"],
                     writes=[f"pb{pvb + hh}"])

        def c_evac(qj):
            pvb = 2 * (qj % 2)
            for hh in range(2):
                hs = slice(hh * 64, (hh + 1) * 64)
                rqh = rq[hh * 64:(hh + 1) * 64, :]
                P.op("scalar", lambda e, hh=hh, pvb=pvb: e.activation(
                    ps[pvb + hh][64:128, :], ps[pvb + hh][64:128, :], AF.Ln),
                    reads=[f"pb{pvb + hh}"], writes=[f"pb{pvb + hh}"])
                P.op("scalar", lambda e, hh=hh, rqh=rqh, pvb=pvb: e.activation(
                    rqh, ps[pvb + hh][64:128, :], AF.Exp, scale=-1.0),
                    reads=[f"pb{pvb + hh}"], writes=[f"C_rq{hh}"])
                P.op("vector", lambda e, hh=hh, hs=hs, qj=qj, pvb=pvb, rqh=rqh:
                     e.tensor_mul(att[hs, par, qj, :], ps[pvb + hh][0:64, :],
                                  rqh),
                     reads=[f"pb{pvb + hh}", f"C_rq{hh}"],
                     writes=[f"C_att{par}_{qj}"])

        c_sexp(0, 0)
        for u in range(1, 64):
            qj, ci = divmod(u, 16)
            c_sexp(qj, ci)
            pq, pc = divmod(u - 1, 16)
            c_pv(pq, pc)
            if pc == 15:
                c_evac(pq)
        c_pv(3, 15)
        c_evac(3)

    def outproj_units(pp, par):
        units = []
        for oc in range(NCH):
            for tt in range(NTT):
                bank = (oc * NTT + tt) % 2
                st_ = slice(tt * TT, (tt + 1) * TT)

                def u(oc=oc, tt=tt, bank=bank, st_=st_):
                    P.op("tensor", lambda e: e.matmul(
                        ps[bank][:], wop[:, par, oc * 128:(oc + 1) * 128],
                        att[:, par, tt, :], start=True, stop=True),
                        reads=[f"C_wo{par}", f"C_att{par}_{tt}"],
                        writes=[f"pb{bank}"])
                    P.op("vector", lambda e: e.tensor_add(
                        x[:, oc, st_], x[:, oc, st_], ps[bank][:]),
                        reads=[f"pb{bank}"], writes=[f"rx{oc}_{tt}"])
                units.append(u)
        return units

    load_pair(0, 0)
    for u in proj_units(0, 0):
        u()
    load_pair(1, 1)
    for pp in range(8):
        drain_bg(g, 2)
        par = pp % 2
        attn_pair(pp, par)
        pu = proj_units(pp + 1, (pp + 1) % 2) if pp < 7 else []
        ou = outproj_units(pp, par)
        n = max(len(pu), len(ou))
        for i in range(n):
            if i < len(pu):
                pu[i]()
            if i < len(ou):
                ou[i]()
        if pp < 6:
            load_pair(pp + 2, par)



def emit_phaseD(P, ctx, g, out_ap):
    nc = P.nc
    x, xln, ps = g["x"], g["xln"], g["psum"]
    ins = g["ins"]
    ov = g["ov"]
    jtok = g["jtok"]
    NEX, CAP, W = 8, 512, 1040

    gwT = ov.sb("D_gw", [128, NCH, 8], BF16)
    dsm = ov.sb("D_sm", [128, 48], F32)
    ptok = ov.sb("D_pt", [128, 16, 8], F32)
    ptok2 = ov.sb("D_pt2", [128, 8, 16], F32)
    cmpb = ov.sb("D_cmpb", [128, 8, 16], F32)
    bis = ov.sb("D_bis", [128, 6, 8], F32)
    cnt8 = ov.sb("D_cnt8", [128, 8], BF16)
    dsc = ov.sb("D_sc", [8, T], F32)
    cat = ov.sb("D_cat", [128, 16, W], BF16)
    w1sl = ov.sb("D_w1", [128, 2, NCH, 512], BF16)
    w2c = ov.sb("D_w2", [128, 2, 4, E], BF16)
    stage = ov.sb("D_stage", [128, 2, E], F32)
    xin = ov.sb("D_xin", [128, 4, W], BF16)
    eo = ov.sb("D_eo", [128, 4, E], BF16)
    tidu = ov.sb("D_tid", [128, 2, 4], U32)
    tscr = ov.sb("D_tscr", [128, 2], F32)
    gf = ov.sb("D_gf", [128, 2, 4], F32)
    xinTb = [nc.alloc_sbuf_tensor_at("D_xinT0", [128, NCH, CAP], BF16,
                                     offset=nc.lookup_mloc(stage).addr),
             nc.alloc_sbuf_tensor_at("D_xinT1", [128, NCH, CAP], BF16,
                                     offset=nc.lookup_mloc(dsc).addr)]
    hT = nc.alloc_sbuf_tensor_at("D_hT", [128, 32, CAP], BF16,
                                 offset=nc.lookup_mloc(xln).addr)
    xbc = xln[:].rearrange("p c t -> p (c t)").bitcast(F32)
    bufB = xbc[0:8, 0:2048]
    bufD = xbc[0:8, 2048:4096]
    alB = [f"xln{c}_{t}" for c in (0, 1) for t in range(NTT)]
    alD = [f"xln{c}_{t}" for c in (2, 3) for t in range(NTT)]
    bufC = dsc[0:8, :]

    xin_d = [nc.dram_tensor(f"D_xin{e}", [CAP, W], BF16, kind="Internal").ap()
             for e in range(NEX)]
    bndreg = {}

    def bnd(e):
        if "r" not in bndreg:
            bndreg["r"] = e.to_reg(CAP - 1)
        return bndreg["r"]

    W1B = g["w1b"].rearrange("ee (c p) h -> ee p c h", p=128)
    W2B = g["w2b"].rearrange("ee (hc p) d -> ee p hc d", p=128)

    def load_w1(ex, s, par):
        P.op("sync", lambda e, ex=ex, s=s, par=par: e.dma_start(
            out=w1sl[:, par, :, :],
            in_=W1B[ex, :, :, s * 512:(s + 1) * 512]), dma=True,
            reads=[f"L1s1{ex}"], writes=[f"D_w1{par}"])

    def load_w2(ex, g4, par):
        P.op("scalar", lambda e, ex=ex, g4=g4, par=par: e.dma_start(
            out=w2c[:, par, :, :], in_=W2B[ex, :, g4 * 4:(g4 + 1) * 4, :]),
            dma=True, reads=[f"L1s2{ex}"], writes=[f"D_w2{par}"])

    load_w1(0, 0, 0)
    load_w2(0, 0, 0)
    P.op("gpsimd", lambda e: e.dma_start(out=gwT[:], in_=ins["gwT"].rearrange(
        "(c p) e8 -> p c e8", p=128)), dma=True, writes=["D_gw"])
    P.op("gpsimd", lambda e: e.dma_start(out=cat[:, :, 1032:1034],
                                         in_=ins["tidhl"]), dma=True,
         writes=[f"D_cat{tc}" for tc in range(16)])
    drain_bg(g, 99)

    for tt in range(NTT):
        st_ = slice(tt * TT, (tt + 1) * TT)
        bank = tt % 2
        for c in range(NCH):
            P.op("tensor", lambda e, c=c, st_=st_, bank=bank:
                 e.matmul(ps[bank][0:8, :], gwT[:, c, :], xln[:, c, st_],
                          start=(c == 0), stop=(c == NCH - 1)),
                 reads=["D_gw"] + xln_reads(c, tt), writes=[f"pb{bank}"])
        P.op("vector", lambda e, st_=st_, bank=bank:
             e.tensor_copy(dsc[:, st_], ps[bank][0:8, :]),
             reads=[f"pb{bank}"], writes=["D_dsc"])

    fill_units = []

    def fill_copy(dst, src_ps, bank, wr, idx):
        if idx % 3 == 2:
            P.op("vector", lambda e: e.tensor_copy(dst, src_ps),
                 reads=[f"pb{bank}"], writes=wr)
        else:
            P.op("scalar", lambda e: e.activation(dst, src_ps, AF.Copy),
                 reads=[f"pb{bank}"], writes=wr)

    for tc in range(16):
        for ec in range(NCH):
            idx = tc * NCH + ec
            bank = 4 + idx % 4

            def u(tc=tc, ec=ec, bank=bank, idx=idx):
                psb = ps[bank][:].bitcast(BF16)
                P.op("tensor", lambda e, psb=psb: e.transpose(
                    psb[:, 0:128], xln[:, ec, tc * 128:(tc + 1) * 128],
                    g["eyeb"][:]),
                    reads=xln_reads(ec, tc // 4) + ["consts"],
                    writes=[f"pb{bank}"])
                fill_copy(cat[:, tc, ec * 128:(ec + 1) * 128], psb[:, 0:128],
                          bank, [f"D_cat{tc}"], idx)
            fill_units.append(u)
    for tc in range(16):
        spar = tc % 2
        for ec in range(NCH):
            idx = tc * NCH + ec
            bank = 4 + idx % 4

            def u(tc=tc, ec=ec, bank=bank, spar=spar, idx=idx):
                P.op("tensor", lambda e: e.transpose(
                    ps[bank][:, 0:128], x[:, ec, tc * 128:(tc + 1) * 128],
                    g["eyef"][:]),
                    reads=[f"rx{ec}_{tc // 4}", "consts"], writes=[f"pb{bank}"])
                fill_copy(stage[:, spar, ec * 128:(ec + 1) * 128],
                          ps[bank][:, 0:128], bank, [f"D_st{spar}"], idx + 1)
            fill_units.append(u)

        def u(tc=tc, spar=spar):
            qeng = "sync" if tc % 2 == 0 else "scalar"
            P.op(qeng, lambda e: e.dma_start(
                out=out_ap[tc * 128:(tc + 1) * 128, :], in_=stage[:, spar, :]),
                dma=True, reads=[f"D_st{spar}"], writes=["outdram"])
        fill_units.append(u)

    def fdrain(k):
        for _ in range(k):
            if fill_units:
                fill_units.pop(0)()

    PT = [f"D_ptok{t}" for t in range(16)]
    for tc in range(16):
        bank = 2 + tc % 2
        P.op("tensor", lambda e, tc=tc, bank=bank: e.transpose(
            ps[bank][:, 0:8], dsc[0:8, tc * 128:(tc + 1) * 128],
            g["eyef"][0:8, 0:8]),
            reads=["D_dsc", "consts"], writes=[f"pb{bank}"])
        P.op("vector", lambda e, tc=tc, bank=bank: e.tensor_copy(
            ptok[:, tc, :], ps[bank][:, 0:8]),
            reads=[f"pb{bank}"], writes=[f"D_ptok{tc}"])
    p3 = ptok[:]
    mxb = dsm[:, 0:16].unsqueeze(2).broadcast_to([128, 16, 8])
    smb = dsm[:, 16:32].unsqueeze(2).broadcast_to([128, 16, 8])
    P.op("vector", lambda e: e.reduce_max(dsm[:, 0:16], p3, axis=AX.X),
         reads=PT, writes=["D_mx"])
    fdrain(8)
    P.op("vector", lambda e: e.tensor_sub(p3, p3, mxb),
         reads=PT + ["D_mx"], writes=PT)
    fdrain(8)
    P.op("scalar", lambda e: e.activation(
        ptok[:].rearrange("p a b -> p (a b)"),
        ptok[:].rearrange("p a b -> p (a b)"), AF.Exp),
        reads=PT, writes=PT)
    P.op("vector", lambda e: e.reduce_sum(dsm[:, 16:32], p3, axis=AX.X),
         reads=PT, writes=["D_sme"])
    fdrain(8)
    P.op("vector", lambda e: e.reciprocal(dsm[:, 16:32], dsm[:, 16:32]),
         reads=["D_sme"], writes=["D_sme"])
    P.op("vector", lambda e: e.tensor_mul(p3, p3, smb),
         reads=PT + ["D_sme"], writes=PT)
    for tc in range(16):
        P.op("vector", lambda e, tc=tc: e.tensor_copy(
            ptok2[:, :, tc:tc + 1], ptok[:, tc, :].unsqueeze(2)),
            reads=[f"D_ptok{tc}"], writes=["D_pt2"])
        P.op("vector", lambda e, tc=tc: e.tensor_copy(
            cat[:, tc, 1024:1032], ptok[:, tc, :]),
            reads=[f"D_ptok{tc}"], writes=[f"D_cat{tc}"])

    for tc in range(16):
        P.op("tensor", lambda e, tc=tc: e.transpose(
            ps[3][0:8, 0:128], ptok[:, tc, :], g["eyef"][:]),
            reads=[f"D_ptok{tc}", "consts"], writes=["pb3"])
        P.op("vector", lambda e, tc=tc: e.tensor_copy(
            dsc[:, tc * 128:(tc + 1) * 128], ps[3][0:8, 0:128]),
            reads=["pb3"], writes=["D_dsc"])

    blo = bis[:, 0, :]
    bhi = bis[:, 1, :]
    bmid = bis[:, 2, :]
    bsel = bis[:, 3, :]
    bt1 = bis[:, 4, :]
    bt2 = bis[:, 5, :]
    midb = bmid.unsqueeze(2).broadcast_to([128, 8, 16])
    P.op("vector", lambda e: e.memset(blo, 0.0), writes=["D_lo"])
    P.op("vector", lambda e: e.memset(bhi, 1.0), writes=["D_hi"])
    for it in range(28):
        P.op("vector", lambda e: e.tensor_add(bmid, blo, bhi),
             reads=["D_lo", "D_hi"], writes=["D_mid"])
        P.op("vector", lambda e: e.tensor_scalar(bmid, bmid, 0.5, None,
                                                 ALU.mult),
             reads=["D_mid"], writes=["D_mid"])
        P.op("vector", lambda e: e.tensor_tensor(cmpb[:], ptok2[:], midb,
                                                 ALU.is_ge),
             reads=["D_pt2", "D_mid"], writes=["D_cmpb"])
        P.op("vector", lambda e: e.reduce_sum(dsm[:, 32:40], cmpb[:],
                                              axis=AX.X),
             reads=["D_cmpb"], writes=["D_cntf"])
        P.op("vector", lambda e: e.tensor_copy(cnt8[:], dsm[:, 32:40]),
             reads=["D_cntf"], writes=["D_cnt8"])
        P.op("tensor", lambda e: e.matmul(ps[2][:, 0:8], g["ln"]["ones128"][:],
                                          cnt8[:], start=True, stop=True),
             reads=["D_cnt8", "consts"], writes=["pb2"])
        fdrain(4)
        P.op("vector", lambda e: e.tensor_scalar(bsel, ps[2][:, 0:8],
                                                 float(CAP), None, ALU.is_ge),
             reads=["pb2"], writes=["D_sel"])
        P.op("vector", lambda e: e.tensor_sub(bt1, bmid, blo),
             reads=["D_mid", "D_lo"], writes=["D_t1"])
        P.op("vector", lambda e: e.tensor_mul(bt1, bt1, bsel),
             reads=["D_t1", "D_sel"], writes=["D_t1"])
        P.op("vector", lambda e: e.tensor_add(blo, blo, bt1),
             reads=["D_lo", "D_t1"], writes=["D_lo"])
        P.op("vector", lambda e: e.tensor_sub(bt2, bhi, bmid),
             reads=["D_hi", "D_mid"], writes=["D_t2"])
        P.op("vector", lambda e: e.tensor_mul(bt2, bt2, bsel),
             reads=["D_t2", "D_sel"], writes=["D_t2"])
        P.op("vector", lambda e: e.tensor_add(bhi, bmid, bt2),
             reads=["D_mid", "D_t2"], writes=["D_hi"])
    lo8 = dsm[0:8, 40:41]
    P.op("vector", lambda e: e.tensor_mul(dsm[0:8, 32:40], blo[0:8, :],
                                          g["eyef"][0:8, 0:8]),
         reads=["D_lo", "consts"], writes=["D_l8t"])
    P.op("vector", lambda e: e.reduce_sum(lo8, dsm[0:8, 32:40], axis=AX.X),
         reads=["D_l8t"], writes=["D_lo8"])

    P.op("vector", lambda e: e.tensor_scalar(bufB[:], dsc[0:8, :],
                                             lo8, None, ALU.is_ge),
         reads=["D_dsc", "D_lo8"], writes=["D_bufB"] + alB)
    P.op("vector", lambda e: e.memset(bufD[:], 0.0),
         writes=["D_bufD"] + alD)
    P.op("vector", lambda e: e.tensor_tensor_scan(bufC[:], bufB[:], bufD[:],
                                                  0.0, ALU.add, ALU.add),
         reads=["D_bufB", "D_bufD", "D_dsc"] + alB + alD,
         writes=["D_bufC", "D_dsc"])
    P.op("vector", lambda e: e.tensor_sub(bufC[:], bufC[:], bufB[:]),
         reads=["D_bufC", "D_bufB"] + alB, writes=["D_bufC"])
    P.op("gpsimd", lambda e: e.iota(bufD[:], [[1, 2048]], base=4096,
                                    channel_multiplier=0,
                                    allow_small_or_imprecise_dtypes=True),
         reads=["D_bufD"], writes=["D_bufD"] + alD)
    P.op("vector", lambda e: e.tensor_mul(bufC[:], bufC[:], bufB[:]),
         reads=["D_bufB", "D_bufC"] + alB, writes=["D_bufC"])
    P.op("vector", lambda e: e.tensor_scalar(bufB[:], bufB[:], -1.0,
                                             1.0, ALU.mult, ALU.add),
         reads=["D_bufB"], writes=["D_bufB"] + alB)
    P.op("vector", lambda e: e.tensor_mul(bufD[:], bufD[:], bufB[:]),
         reads=["D_bufB", "D_bufD"] + alB, writes=["D_bufD"] + alD)
    P.op("vector", lambda e: e.tensor_add(bufC[:], bufC[:], bufD[:]),
         reads=["D_bufD", "D_bufC"] + alD, writes=["D_bufC"])
    for tc in range(16):
        P.op("tensor", lambda e, tc=tc: e.transpose(
            ps[2][0:128, 0:8], bufC[:, tc * 128:(tc + 1) * 128],
            g["eyef"][0:8, 0:8]),
            reads=["D_bufC", "consts"], writes=["pb2"])
        P.op("vector", lambda e, tc=tc: e.tensor_copy(jtok[:, tc, :],
                                                      ps[2][:, 0:8]),
             reads=["pb2"], writes=[f"D_jtok{tc}"])

    def scatter(ex):
        for tc in range(16):
            P.op("gpsimd", lambda e, ex=ex, tc=tc: e.indirect_dma_start(
                out=xin_d[ex], out_offset=IndirectOffsetOnAxis(
                    ap=jtok[:, tc, ex:ex + 1], axis=0),
                in_=cat[:, tc, :], in_offset=None,
                bounds_check=bnd(e), oob_is_err=False), dma=True,
                reads=[f"D_jtok{tc}", f"D_cat{tc}"], writes=[f"D_xind{ex}"])

    scatter(0)
    fdrain(999)

    def gather(ex):
        P.op("sync", lambda e, ex=ex: e.dma_start(
            out=xin[:], in_=xin_d[ex].rearrange("(jc p) w -> p jc w", p=128)),
            dma=True, reads=[f"D_xind{ex}"], writes=["D_xinb"])

    def transpose_units(ex):
        par = ex % 2
        xinT = xinTb[par]
        units = []
        for dc in range(NCH):
            for jc in range(4):
                bank = 2 + (dc * 4 + jc) % 2
                psb = ps[bank][:].bitcast(BF16)
                first = (dc == 0 and jc == 0)
                alias = (["D_st0", "D_st1"] if par == 0 else
                         ["D_dsc", "D_bufC"]) if ex < 2 and first else []

                def u(jc=jc, dc=dc, psb=psb, bank=bank, par=par, xinT=xinT,
                      alias=alias):
                    P.op("tensor", lambda e: e.transpose(
                        psb[:, 0:128], xin[:, jc, dc * 128:(dc + 1) * 128],
                        g["eyeb"][:]),
                        reads=["D_xinb", "consts"], writes=[f"pb{bank}"])
                    P.op("vector", lambda e: e.tensor_copy(
                        xinT[:, dc, jc * 128:(jc + 1) * 128], psb[:, 0:128]),
                        reads=[f"pb{bank}"],
                        writes=[f"D_xinT{par}_{dc}"] + alias)
                units.append(u)
        return units

    def extract(ex, par):
        for jc in range(4):
            P.op("vector", lambda e, jc=jc, ex=ex, par=par: e.tensor_copy(
                gf[:, par, jc:jc + 1], xin[:, jc, 1024 + ex:1025 + ex]),
                reads=["D_xinb"], writes=[f"D_gf{par}"])
            P.op("vector", lambda e, jc=jc: e.tensor_copy(
                tscr[:, 0:1], xin[:, jc, 1032:1033]),
                reads=["D_xinb"], writes=["D_tscr"])
            P.op("vector", lambda e, jc=jc: e.tensor_copy(
                tscr[:, 1:2], xin[:, jc, 1033:1034]),
                reads=["D_xinb"], writes=["D_tscr"])
            P.op("vector", lambda e: e.tensor_scalar(
                tscr[:, 0:1], tscr[:, 0:1], 256.0, None, ALU.mult),
                reads=["D_tscr"], writes=["D_tscr"])
            P.op("vector", lambda e: e.tensor_add(
                tscr[:, 0:1], tscr[:, 0:1], tscr[:, 1:2]),
                reads=["D_tscr"], writes=["D_tscr"])
            P.op("vector", lambda e, jc=jc, par=par: e.tensor_copy(
                tidu[:, par, jc:jc + 1], tscr[:, 0:1]),
                reads=["D_tscr"], writes=[f"D_tid{par}"])

    gather(0)
    for u in transpose_units(0):
        u()
    extract(0, 0)
    scatter(1)

    w1ctr = 0
    for ex in range(NEX):
        par = ex % 2
        if ex + 1 < NEX:
            gather(ex + 1)
        if ex + 2 < NEX:
            scatter(ex + 2)
        tun = transpose_units(ex + 1) if ex + 1 < NEX else []
        xinT = xinTb[par]
        for s in range(8):
            wpar = w1ctr % 2
            if s < 7:
                load_w1(ex, s + 1, (w1ctr + 1) % 2)
            elif ex < NEX - 1:
                load_w1(ex + 1, 0, (w1ctr + 1) % 2)
            if s >= 2:
                for _ in range(6):
                    if tun:
                        tun.pop(0)()
            for hc in range(4):
                gh = s * 4 + hc
                bank = gh % 2
                for dc in range(NCH):
                    P.op("tensor", lambda e, dc=dc, hc=hc, wpar=wpar, bank=bank,
                         xinT=xinT, par=par:
                         e.matmul(ps[bank][:],
                                  w1sl[:, wpar, dc, hc * 128:(hc + 1) * 128],
                                  xinT[:, dc, :], start=(dc == 0),
                                  stop=(dc == NCH - 1)),
                         reads=[f"D_w1{wpar}", f"D_xinT{par}_{dc}"],
                         writes=[f"pb{bank}"])
                P.op("scalar", lambda e, gh=gh, bank=bank: e.activation(
                    hT[:, gh, :], ps[bank][:], AF.Gelu),
                    reads=[f"pb{bank}"],
                    writes=[f"D_h{gh}", f"xln{gh // 4}_{gh % 4}"])
            w1ctr += 1
        for g4 in range(8):
            wpar2 = g4 % 2
            if g4 < 7:
                load_w2(ex, g4 + 1, (g4 + 1) % 2)
            elif ex < NEX - 1:
                load_w2(ex + 1, 0, (g4 + 1) % 2)
            if g4 == 6 and ex + 1 < NEX:
                extract(ex + 1, (ex + 1) % 2)
            for ghl in range(4):
                gh = g4 * 4 + ghl
                for jc in range(4):
                    for dh in range(2):
                        bank = jc * 2 + dh
                        P.op("tensor", lambda e, gh=gh, ghl=ghl, jc=jc, dh=dh,
                             wpar2=wpar2, bank=bank:
                             e.matmul(ps[bank][:],
                                      hT[:, gh, jc * 128:(jc + 1) * 128],
                                      w2c[:, wpar2, ghl, dh * 512:(dh + 1) * 512],
                                      start=(gh == 0), stop=(gh == 31)),
                             reads=[f"D_w2{wpar2}", f"D_h{gh}"],
                             writes=[f"pb{bank}"])
        for jc in range(4):
            for dh in range(2):
                bank = jc * 2 + dh
                P.op("vector", lambda e, jc=jc, dh=dh, bank=bank, par=par:
                     e.tensor_scalar(eo[:, jc, dh * 512:(dh + 1) * 512],
                                     ps[bank][:], gf[:, par, jc:jc + 1], None,
                                     ALU.mult),
                     reads=[f"pb{bank}", f"D_gf{par}"], writes=[f"D_eo{jc}"])
        for jc in range(4):
            P.op("gpsimd", lambda e, jc=jc, par=par: e.indirect_dma_start(
                out=out_ap, out_offset=IndirectOffsetOnAxis(
                    ap=tidu[:, par, jc:jc + 1], axis=0),
                in_=eo[:, jc, :], in_offset=None,
                oob_is_err=True, compute_op=ALU.add), dma=True,
                reads=[f"D_eo{jc}", f"D_tid{par}", "outdram"],
                writes=["outdram"])



def _build(phases):
    nc = bass.Bass("TRN2", target_bir_lowering=False,
                   dynamic_dma_scratch_size=16384)
    ins = {}

    def din(name, shape):
        ins[name] = nc.dram_tensor(name, shape, F32, kind="ExternalInput").ap()

    din("xT", [E, T])
    ins["peT"] = nc.dram_tensor("peT", [D0, T], BF16,
                                kind="ExternalInput").ap()
    ins["cmask"] = nc.dram_tensor("cmask", [128, 896], BF16,
                                  kind="ExternalInput").ap()
    din("eye", [128, 128])
    if "A" in phases:
        din("wqkvT", [E, 3 * H0 * D0])
        din("woT", [H0 * D0, E])
    if "B" in phases:
        din("wgate", [E, 8])
        din("bgate", [1, 8])
        din("alpha", [1, 1])
        din("W1m", [E, 4096])
        din("W2m", [4096, E])
        din("b1mcm", [128, 32])
        din("b2mcm", [128, 8])
        din("W1e", [8, E, 4096])
        din("W2e", [8, 4096, E])
        din("b1e", [8, 4096])
        din("b2e", [8, E])
    if "C" in phases:
        din("Wqc", [E, E])
        din("Wkc", [E, E])
        din("Wvc", [E, E])
        din("Wo1", [E, E])
    if "D" in phases:
        din("gwT", [E, 8])
        din("tidhl", [128, 16, 2])
        din("Ew1", [8, E, 4096])
        din("Ew2", [8, 4096, E])
    if "B" in phases or "D" in phases:
        din("iotachf", [128, 32])
        din("iotagf", [128, 1])
    last = phases[-1]
    if last == "D":
        out_ap = nc.dram_tensor("out", [T, E], F32, kind="ExternalOutput").ap()
    else:
        out_ap = nc.dram_tensor("xTo", [E, T], F32, kind="ExternalOutput").ap()

    with ExitStack() as ctx:
        def sb(name, shape, dt, side="left"):
            return ctx.enter_context(nc.sbuf_tensor(name, shape, dt, side=side))

        P = Prog(nc, ctx)
        g = {"ins": ins, "sb": sb}

        x = sb("x", [128, NCH, T], F32, side="right")
        g["x"] = x
        ones128 = sb("ones128", [128, 128], BF16)
        g["eyeb"] = sb("eyeb", [128, 128], BF16)
        g["eyef"] = sb("eyef", [128, 128], F32)
        g["cbias"] = sb("cbias", [128, 2], F32)
        xln = sb("xln", [128, NCH, T], BF16)
        g["xln"] = xln
        ln_stt = sb("ln_stt", [128, TT], F32)
        g["ln"] = {
            "ones128": ones128,
            "sq2": ln_stt[:].bitcast(BF16),
            "stm": sb("ln_stm", [128, TT], F32),
            "stv": sb("ln_stv", [128, TT], F32),
            "sttmp": ln_stt,
        }
        g["bmsum"] = sb("bmsum", [128, NCH], F32)
        g["jtok"] = sb("jtok", [128, 16, 8], U32)
        g["psum"] = [ctx.enter_context(nc.psum_tensor(f"pb{i}", [128, TT], F32))
                     for i in range(8)]
        if "B" in phases or "D" in phases:
            g["iotaf"] = sb("iotaf", [128, 32], F32)
            g["iotag"] = sb("iotag", [128, 1], F32)
            P.op("sync", lambda e: e.dma_start(out=g["iotaf"][:],
                                               in_=ins["iotachf"]), dma=True,
                 writes=["consts"])
            P.op("sync", lambda e: e.dma_start(out=g["iotag"][:],
                                               in_=ins["iotagf"]), dma=True,
                 writes=["consts"])
        g["bg"] = []
        if "D" in phases:
            w1b = nc.dram_tensor("L1w1b", [8, E, 4096], BF16,
                                 kind="Internal").ap()
            w2b = nc.dram_tensor("L1w2b", [8, 4096, E], BF16,
                                 kind="Internal").ap()
            g["w1b"], g["w2b"] = w1b, w2b
            for ex in range(8):
                for half in range(2):
                    hs = slice(half * 512, (half + 1) * 512)
                    hs2 = slice(half * 2048, (half + 1) * 2048)
                    g["bg"].append(lambda P=P, ex=ex, hs=hs: P.op(
                        "gpsimd", lambda e: e.dma_start(
                            out=w1b[ex, hs, :], in_=ins["Ew1"][ex, hs, :]),
                        dma=True, writes=[f"L1s1{ex}"]))
                    g["bg"].append(lambda P=P, ex=ex, hs2=hs2: P.op(
                        "gpsimd", lambda e: e.dma_start(
                            out=w2b[ex, hs2, :], in_=ins["Ew2"][ex, hs2, :]),
                        dma=True, writes=[f"L1s2{ex}"]))
        arena_bytes = (nc.sbuf_bytes_remaining - 64) // 32 * 32
        arena = sb("arena", [128, arena_bytes], mybir.dt.int8)
        g["ov"] = Overlay(nc, arena)

        P.op("vector", lambda e: e.memset(ones128[:], 1.0), writes=["consts"])
        P.op("vector", lambda e: e.memset(g["cbias"][:, 0:1], -C_L0),
             writes=["consts"])
        P.op("vector", lambda e: e.memset(g["cbias"][:, 1:2], -C_L1),
             writes=["consts"])
        P.op("gpsimd", lambda e: e.dma_start(out=g["eyeb"][:], in_=ins["eye"]),
             dma=True, writes=["consts"])
        P.op("sync", lambda e: e.dma_start(out=g["eyef"][:], in_=ins["eye"]),
             dma=True, writes=["consts"])
        xTr = ins["xT"].rearrange("(c p) t -> p c t", p=128)
        for tt in range(NTT):
            s = slice(tt * TT, (tt + 1) * TT)
            P.op("sync", lambda e, s=s: e.dma_start(out=x[:, :, s],
                                                    in_=xTr[:, :, s]), dma=True,
                 writes=[f"rx{c}_{tt}" for c in range(NCH)])

        for i, ph in enumerate(phases):
            if ph == "A":
                emit_ln(P, g)
                g["ov"].reset()
                emit_phaseA(P, ctx, g)
            elif ph == "B":
                emit_ln(P, g, gate_msum=g["bmsum"], gtag="B")
                P.barrier()
                g["ov"].reset()
                emit_phaseB(P, ctx, g)
            elif ph == "C":
                emit_ln(P, g)
                P.barrier()
                g["ov"].reset()
                emit_phaseC(P, ctx, g)
            elif ph == "D":
                emit_ln(P, g)
                P.barrier()
                g["ov"].reset()
                emit_phaseD(P, ctx, g, out_ap)
            else:
                raise NotImplementedError(ph)

        if last != "D":
            P.barrier()
            outr = out_ap.rearrange("(c p) t -> p c t", p=128)
            P.op("sync", lambda e: e.dma_start(out=outr, in_=x[:]), dma=True,
                 reads=[f"rx{c}_{t}" for c in range(NCH) for t in range(NTT)],
                 writes=["out"])

        block = ctx.enter_context(nc.Block())
        P.emit(block)
    return nc



def _host_consts():
    pos = np.arange(T, dtype=np.float32)[:, None]
    div = np.exp(np.arange(0, D0, 2, dtype=np.float32) * (-np.log(10000.0) / D0))
    ang = pos * div
    pe = np.stack([np.sin(ang), np.cos(ang)], axis=-1).reshape(T, D0)
    peT = np.ascontiguousarray(pe.T.astype(ml_dtypes.bfloat16))
    r = np.arange(128)[:, None]
    w = np.arange(896)[None, :]
    cmask = ((w - 384) >= r).astype(ml_dtypes.bfloat16)
    eye = np.eye(128, dtype=np.float32)
    return {"peT": peT, "cmask": cmask, "eye": eye}


def _host_shared(inputs, phases):
    sh = dict(_host_consts())
    f32 = lambda a: np.ascontiguousarray(np.asarray(a, np.float32))
    if "A" in phases:
        sh["wqkvT"] = f32(np.asarray(inputs["L0_attn_Wqkv"]).T)
        sh["woT"] = f32(np.asarray(inputs["L0_attn_Wo"]).T)
    if "B" in phases:
        sh["wgate"] = f32(inputs["L0_ffn_w_gate"])
        sh["bgate"] = f32(inputs["L0_ffn_b_gate"]).reshape(1, 8)
        sh["alpha"] = f32(inputs["L0_ffn_alpha"]).reshape(1, 1)
        sh["W1m"] = f32(inputs["L0_ffn_W1m"])
        sh["W2m"] = f32(inputs["L0_ffn_W2m"])
        sh["b1mcm"] = f32(np.asarray(inputs["L0_ffn_b1m"]).reshape(32, 128).T)
        sh["b2mcm"] = f32(np.asarray(inputs["L0_ffn_b2m"]).reshape(8, 128).T)
        sh["W1e"] = f32(inputs["L0_ffn_W1e"])
        sh["W2e"] = f32(inputs["L0_ffn_W2e"])
        sh["b1e"] = f32(inputs["L0_ffn_b1e"])
        sh["b2e"] = f32(inputs["L0_ffn_b2e"])
    if "C" in phases:
        cat = lambda w: f32(np.asarray(w).transpose(1, 0, 2).reshape(E, E))
        sh["Wqc"] = cat(inputs["L1_attn_W_q"])
        sh["Wkc"] = cat(inputs["L1_attn_W_k"])
        sh["Wvc"] = cat(inputs["L1_attn_W_v"])
        sh["Wo1"] = f32(inputs["L1_attn_W_o"])
    if "D" in phases:
        tid = (np.arange(16)[None, :] * 128 + np.arange(128)[:, None])
        sh["tidhl"] = np.ascontiguousarray(
            np.stack([tid // 256, tid % 256], axis=-1).astype(np.float32))
        sh["gwT"] = f32(np.asarray(inputs["L1_ffn_gate_w"]).T)
        sh["Ew1"] = f32(inputs["L1_ffn_w1"])
        sh["Ew2"] = f32(inputs["L1_ffn_w2"])
    if "B" in phases or "D" in phases:
        p = np.arange(128, dtype=np.float32)[:, None]
        gg = np.arange(32, dtype=np.float32)[None, :]
        sh["iotachf"] = np.ascontiguousarray(gg * 128 + p)
        sh["iotagf"] = np.ascontiguousarray(p)
    return sh


def kernel(**inputs):
    x = np.asarray(inputs["L0_attn_x"], np.float32)
    phase_groups = [_PHASES] if _FUSED else list(_PHASES)
    xcur = [np.ascontiguousarray(x[c].T) for c in range(8)]
    out = None
    for ph in phase_groups:
        sh = _host_shared(inputs, ph)
        nc = _build(ph)
        in_maps = [dict(sh, xT=xcur[c]) for c in range(8)]
        res = run_bass_kernel_spmd(nc, in_maps, core_ids=list(range(8)))
        if ph[-1] == "D":
            out = np.stack([np.asarray(res.results[c]["out"]) for c in range(8)])
        else:
            xcur = [np.ascontiguousarray(np.asarray(res.results[c]["xTo"]))
                    for c in range(8)]
    if out is None:
        out = np.stack([xcur[c].T for c in range(8)])
    return out


# version 16
# speedup vs baseline: 1.2263x; 1.0191x over previous
import math
from contextlib import ExitStack

import numpy as np

import ml_dtypes

import concourse.bass as bass
import concourse.mybir as mybir
from concourse.bass_core_rust import IndirectOffsetOnAxis
from concourse.bass_utils import run_bass_kernel_spmd

F32 = mybir.dt.float32
BF16 = mybir.dt.bfloat16
U32 = mybir.dt.uint32
AF = mybir.ActivationFunctionType
ALU = mybir.AluOpType
AX = mybir.AxisListType

T = 2048
E = 1024
NCH = 8
NTT = 4
TT = 512
H0, D0 = 8, 512
C_L0 = 24.0
C_L1 = 4.0

_FUSED = True
_PHASES = "ABCD"


_NQSEM = 12
_QUEUES = ["qsync", "qscalar", "qgp"]
_STREAMS = (["tensor", "vector", "scalar", "gpsimd", "sync"]
            + [f"{q}{i}" for q in _QUEUES for i in range(_NQSEM)])
_QUEUE_OF = {"sync": "qsync", "scalar": "qscalar", "gpsimd": "qgp"}


class Prog:
    def __init__(self, nc, ctx):
        self.nc = nc
        self.sem = {n: ctx.enter_context(nc.semaphore("sem_" + n))
                    for n in _STREAMS}
        self.cnt = {n: 0 for n in _STREAMS}
        self.qrot = {q: 0 for q in _QUEUES}
        self.hwm = {}
        self.writers = {}
        self.readers = {}
        self.items = []

    def op(self, engine, emit, reads=(), writes=(), dma=False):
        waits = {}

        def need(d):
            for s, v in (d or {}).items():
                if waits.get(s, 0) < v:
                    waits[s] = v

        for r in reads:
            need(self.writers.get(r))
        for w in writes:
            need(self.writers.get(w))
            need(self.readers.get(w))
        if dma:
            q = _QUEUE_OF[engine]
            i = self.qrot[q]
            self.qrot[q] = (i + 1) % _NQSEM
            stream, amt = f"{q}{i}", 16
            if self.cnt[stream] and waits.get(stream, 0) < self.cnt[stream]:
                waits[stream] = self.cnt[stream]
        else:
            stream, amt = engine, 1
        real = {}
        for s, v in waits.items():
            if engine == "tensor" and s == "tensor":
                continue
            k = (engine, s)
            if self.hwm.get(k, 0) >= v:
                continue
            self.hwm[k] = v
            real[s] = v
        self.cnt[stream] += amt
        sig = (stream, self.cnt[stream])
        for w in writes:
            d = self.writers.setdefault(w, {})
            d[sig[0]] = sig[1]
            self.readers.pop(w, None)
        for r in reads:
            d = self.readers.setdefault(r, {})
            if d.get(sig[0], 0) < sig[1]:
                d[sig[0]] = sig[1]
        self.items.append((engine, real, emit, stream, amt))

    def barrier(self):
        snapshot = dict(self.cnt)
        for eng in ["tensor", "vector", "scalar", "gpsimd", "sync"]:
            real = {}
            for s, v in snapshot.items():
                if v == 0 or eng == s:
                    continue
                k = (eng, s)
                if self.hwm.get(k, 0) >= v:
                    continue
                self.hwm[k] = v
                real[s] = v
            if real:
                self.items.append((eng, real, None, None, 0))

    def emit(self, block):
        final = dict(self.cnt)

        def body_for(name):
            def body(e):
                for engine, waits, emitf, stream, amt in self.items:
                    if engine != name:
                        continue
                    for s in sorted(waits):
                        e.wait_ge(self.sem[s], waits[s])
                    if emitf is not None:
                        emitf(e).then_inc(self.sem[stream], amt)
                if name == "gpsimd":
                    for s, v in final.items():
                        if v and s != "gpsimd":
                            e.wait_ge(self.sem[s], v)
            return body

        block.tensor(body_for("tensor"))
        block.vector(body_for("vector"))
        block.scalar(body_for("scalar"))
        block.gpsimd(body_for("gpsimd"))
        block.sync(body_for("sync"))


class Overlay:

    def __init__(self, nc, arena):
        self.nc = nc
        self.base = nc.lookup_mloc(arena).addr
        self.size = int(np.prod(arena.shape[1:]))
        self.off = 0
        self.peak = 0

    def reset(self):
        self.off = 0

    def mark(self):
        return self.off

    def rollback(self, m):
        self.off = m

    def sb(self, name, shape, dt):
        nb = int(np.prod(shape[1:])) * mybir.dt.size(dt)
        nb = (nb + 31) // 32 * 32
        assert self.off + nb <= self.size, (
            f"overlay overflow allocating {name}: {self.off}+{nb} > {self.size}")
        t = self.nc.alloc_sbuf_tensor_at(name, shape, dt,
                                         offset=self.base + self.off)
        self.off += nb
        self.peak = max(self.peak, self.off)
        return t



def emit_ln(P, g, gate_msum=None, gtag=""):
    x, xln = g["x"], g["xln"]
    ln = g["ln"]
    ones = ln["ones128"]
    sq2, stm, stv, stt = ln["sq2"], ln["stm"], ln["stv"], ln["sttmp"]
    ps = g["psum"]
    for tt in range(NTT):
        s = slice(tt * TT, (tt + 1) * TT)
        for c in range(NCH):
            P.op("vector",
                 lambda e, c=c, s=s: e.tensor_copy(xln[:, c, s], x[:, c, s]),
                 reads=[f"rx{c}_{tt}"], writes=[f"xb{c}_{tt}", f"xln{c}_{tt}"])
    for tt in range(NTT):
        s = slice(tt * TT, (tt + 1) * TT)
        for c in range(NCH):
            sq = sq2[:, (c % 2) * TT:(c % 2 + 1) * TT]
            P.op("scalar",
                 lambda e, c=c, s=s, sq=sq: e.activation(sq, xln[:, c, s],
                                                         AF.Square),
                 reads=[f"xb{c}_{tt}"], writes=[f"lnsq{c % 2}"])
            P.op("tensor",
                 lambda e, c=c, s=s: e.matmul(ps[0][:], ones[:], xln[:, c, s],
                                              start=(c == 0), stop=(c == NCH - 1)),
                 reads=[f"xb{c}_{tt}", "consts"], writes=["pb0"])
            P.op("tensor",
                 lambda e, c=c, sq=sq: e.matmul(ps[1][:], ones[:], sq,
                                                start=(c == 0), stop=(c == NCH - 1)),
                 reads=[f"lnsq{c % 2}", "consts"], writes=["pb1"])
        P.op("vector", lambda e: e.tensor_scalar(stm[:], ps[0][:], 1.0 / E, None,
                                                 ALU.mult),
             reads=["pb0"], writes=["lnstm"])
        P.op("vector", lambda e: e.tensor_scalar(stv[:], ps[1][:], 1.0 / E, 1e-5,
                                                 ALU.mult, ALU.add),
             reads=["pb1"], writes=["lnstv"])
        P.op("vector", lambda e: e.tensor_mul(stt[:], stm[:], stm[:]),
             reads=["lnstm", "pb1"], writes=["lnstt", "lnsq0", "lnsq1"])
        P.op("vector", lambda e: e.tensor_sub(stv[:], stv[:], stt[:]),
             reads=["lnstv", "lnstt"], writes=["lnstv"])
        P.op("scalar", lambda e: e.activation(stt[:], stv[:], AF.Ln),
             reads=["lnstv"], writes=["lnstt"])
        P.op("scalar", lambda e: e.activation(stv[:], stt[:], AF.Exp, scale=-0.5),
             reads=["lnstt"], writes=["lnstv"])
        P.op("vector", lambda e: e.tensor_mul(stm[:], stm[:], stv[:]),
             reads=["lnstm", "lnstv"], writes=["lnstm"])
        for c in range(NCH):
            P.op("vector",
                 lambda e, c=c, s=s: e.tensor_mul(xln[:, c, s], x[:, c, s], stv[:]),
                 reads=[f"rx{c}_{tt}", "lnstv"],
                 writes=[f"xln{c}_{tt}", f"xb{c}_{tt}"])
            P.op("vector",
                 lambda e, c=c, s=s: e.tensor_sub(xln[:, c, s], xln[:, c, s], stm[:]),
                 reads=["lnstm"],
                 writes=[f"xln{c}_{tt}"])
    if gate_msum is not None:
        for c in range(NCH):
            P.op("vector",
                 lambda e, c=c: e.reduce_sum(gate_msum[:, c:c + 1], xln[:, c, :],
                                             axis=AX.X),
                 reads=[f"xln{c}_{t}" for t in range(NTT)],
                 writes=[f"{gtag}gmsum{c}"])


def drain_bg(g, k):
    for _ in range(k):
        if g["bg"]:
            g["bg"].pop(0)()


def xln_reads(c, tt=None):
    if tt is None:
        return [f"xln{c}_{t}" for t in range(NTT)]
    return [f"xln{c}_{tt}"]


def rx_w(c, tt=None):
    if tt is None:
        return [f"rx{c}_{t}" for t in range(NTT)]
    return [f"rx{c}_{tt}"]



def emit_phaseA(P, ctx, g):
    sb = g["ov"].sb
    x, xln, ps = g["x"], g["xln"], g["psum"]

    wq = sb("A_wq", [128, NCH, 512], BF16)
    wk = sb("A_wk", [128, NCH, 512], BF16)
    wv = sb("A_wv", [128, NCH, 512], BF16)
    wo = sb("A_wo", [128, 4, 1024], BF16)
    pe = sb("A_pe", [128, 4, T], BF16)
    msk = sb("A_msk", [128, 896], BF16)
    q = sb("A_q", [128, 4, T], BF16)
    k = sb("A_k", [128, 4, T], BF16)
    v = sb("A_v", [128, 16, TT], BF16)
    pbuf = sb("A_p", [128, 2, TT], BF16)
    att = sb("A_att", [128, 4, TT], BF16)
    rrep = pbuf[:].bitcast(F32).rearrange("p a b -> p (a b)")

    peT = g["ins"]["peT"].rearrange("(c p) t -> p c t", p=128)
    P.op("scalar", lambda e: e.dma_start(out=pe[:], in_=peT), dma=True,
         writes=["A_pe"])
    P.op("scalar", lambda e: e.dma_start(out=msk[:], in_=g["ins"]["cmask"]),
         dma=True, writes=["A_msk"])

    wqkvT = g["ins"]["wqkvT"].rearrange("(c p) f -> p c f", p=128)
    woT = g["ins"]["woT"].rearrange("(c p) e2 -> p c e2", p=128)

    for h in range(H0):
        drain_bg(g, 2)
        base = h * 3 * D0
        P.op("gpsimd", lambda e, base=base: e.dma_start(
            out=wq[:], in_=wqkvT[:, :, base:base + 512]), dma=True,
            writes=["A_wq"])
        P.op("gpsimd", lambda e, base=base: e.dma_start(
            out=wk[:], in_=wqkvT[:, :, base + 512:base + 1024]), dma=True,
            writes=["A_wk"])
        P.op("gpsimd", lambda e, base=base: e.dma_start(
            out=wv[:], in_=wqkvT[:, :, base + 1024:base + 1536]), dma=True,
            writes=["A_wv"])
        P.op("gpsimd", lambda e, h=h: e.dma_start(
            out=wo[:], in_=woT[:, 4 * h:4 * h + 4, :]), dma=True,
            writes=["A_wo"])

        for d4 in range(4):
            for tt in range(NTT):
                s = slice(tt * TT, (tt + 1) * TT)
                for name, wt, dst, bank in (("q", wq, q, 2 + (d4 * NTT + tt) % 2),
                                            ("k", wk, k, 4 + (d4 * NTT + tt) % 2)):
                    for c in range(NCH):
                        P.op("tensor", lambda e, c=c, d4=d4, s=s, wt=wt, bank=bank:
                             e.matmul(ps[bank][:], wt[:, c, d4 * 128:(d4 + 1) * 128],
                                      xln[:, c, s], start=(c == 0),
                                      stop=(c == NCH - 1)),
                             reads=[f"A_w{name}"] + xln_reads(c, tt),
                             writes=[f"pb{bank}"])
                    P.op("vector", lambda e, d4=d4, s=s, dst=dst, bank=bank:
                         e.tensor_add(dst[:, d4, s], ps[bank][:], pe[:, d4, s]),
                         reads=[f"pb{bank}", "A_pe"],
                         writes=[f"A_{name}{d4}_{tt}"])

        for tc in range(16):
            bank = 6 + tc % 2
            for c in range(NCH):
                P.op("tensor", lambda e, c=c, tc=tc, bank=bank:
                     e.matmul(ps[bank][:], xln[:, c, tc * 128:(tc + 1) * 128],
                              wv[:, c, :], start=(c == 0), stop=(c == NCH - 1)),
                     reads=["A_wv"] + xln_reads(c, tc // 4),
                     writes=[f"pb{bank}"])
            P.op("vector", lambda e, tc=tc, bank=bank:
                 e.tensor_copy(v[:, tc, :], ps[bank][:]),
                 reads=[f"pb{bank}"], writes=[f"A_v{tc}"])

        def a_sexp(qj, ci):
            sqj = slice(qj * TT, (qj + 1) * TT)
            sbank = 4 + ci % 2
            pp = ci % 2
            for d4 in range(4):
                P.op("tensor", lambda e, ci=ci, d4=d4, sqj=sqj, sbank=sbank:
                     e.matmul(ps[sbank][:], k[:, d4, ci * 128:(ci + 1) * 128],
                              q[:, d4, sqj], start=(d4 == 0), stop=(d4 == 3)),
                     reads=[f"A_k{d4}_{ci // 4}", f"A_q{d4}_{qj}"],
                     writes=[f"pb{sbank}"])
            P.op("scalar", lambda e, pp=pp, sbank=sbank:
                 e.activation(pbuf[:, pp, :], ps[sbank][:], AF.Exp,
                              bias=g["cbias"][:, 0:1],
                              scale=1.0 / math.sqrt(D0)),
                 reads=[f"pb{sbank}", "consts"],
                 writes=[f"A_P{pp}", "A_rrep"])
            if ci >= 4 * qj:
                o = ci - 4 * qj
                ms = slice(384 - 128 * o, 896 - 128 * o)
                P.op("vector", lambda e, pp=pp, ms=ms:
                     e.tensor_mul(pbuf[:, pp, :], pbuf[:, pp, :], msk[:, ms]),
                     reads=["A_msk"], writes=[f"A_P{pp}"])

        def a_pv(qj, ci):
            nck = 4 * qj + 4
            pp = ci % 2
            for d4 in range(4):
                P.op("tensor", lambda e, ci=ci, d4=d4, pp=pp, nck=nck:
                     e.matmul(ps[d4][:], v[:, ci, d4 * 128:(d4 + 1) * 128],
                              pbuf[:, pp, :], start=(ci == 0),
                              stop=(ci == nck - 1)),
                     reads=[f"A_v{ci}", f"A_P{pp}"], writes=[f"pb{d4}"])
            P.op("tensor", lambda e, ci=ci, pp=pp, nck=nck:
                 e.matmul(ps[6][:], g["ln"]["ones128"][:], pbuf[:, pp, :],
                          start=(ci == 0), stop=(ci == nck - 1)),
                 reads=[f"A_P{pp}", "consts"], writes=["pb6"])

        pre = 0
        for qj in range(NTT):
            sqj = slice(qj * TT, (qj + 1) * TT)
            nck = 4 * qj + 4
            for ci in range(nck):
                if ci >= pre:
                    a_sexp(qj, ci)
                if ci >= 1:
                    a_pv(qj, ci - 1)
            a_pv(qj, nck - 1)
            P.op("scalar", lambda e: e.activation(ps[6][:], ps[6][:], AF.Ln),
                 reads=["pb6"], writes=["pb6"])
            P.op("scalar", lambda e: e.activation(rrep[:], ps[6][:], AF.Exp,
                                                  scale=-1.0),
                 reads=["pb6"], writes=["A_rrep"])
            for d4 in range(4):
                P.op("vector", lambda e, d4=d4:
                     e.tensor_mul(att[:, d4, :], ps[d4][:], rrep[:]),
                     reads=[f"pb{d4}", "A_rrep"], writes=[f"A_att{d4}"])
            if qj < NTT - 1:
                a_sexp(qj + 1, 0)
                a_sexp(qj + 1, 1)
                pre = 2
            for ec in range(NCH):
                bank = 6 + ec % 2
                for f4 in range(4):
                    P.op("tensor", lambda e, f4=f4, ec=ec, bank=bank:
                         e.matmul(ps[bank][:],
                                  wo[:, f4, ec * 128:(ec + 1) * 128],
                                  att[:, f4, :], start=(f4 == 0),
                                  stop=(f4 == 3)),
                         reads=["A_wo", f"A_att{f4}"], writes=[f"pb{bank}"])
                P.op("vector", lambda e, ec=ec, sqj=sqj, bank=bank:
                     e.tensor_add(x[:, ec, sqj], x[:, ec, sqj], ps[bank][:]),
                     reads=[f"pb{bank}"], writes=[f"rx{ec}_{qj}"])



def emit_phaseB(P, ctx, g):
    sb = g["ov"].sb
    x, xln, ps = g["x"], g["xln"], g["psum"]
    ins = g["ins"]

    w1sl = sb("B_w1", [128, 2, NCH, 1024], BF16)
    w2sl = sb("B_w2", [128, 2, NCH, 1024], BF16)
    hT = sb("B_h", [128, NCH, T], BF16)
    wg = sb("B_wg", [128, NCH, 8], F32)
    msum = g["bmsum"]
    gsc = sb("B_gsc", [1, 64], F32)
    bgt = sb("B_bg", [1, 8], F32)
    alv = sb("B_al", [1, 1], F32)
    b1m = sb("B_b1m", [128, 32], F32)
    b2m = sb("B_b2m", [128, 8], F32)
    b1x = sb("B_b1x", [128, 2, 32], F32)
    b2x = sb("B_b2x", [128, 2, 8], F32)
    bcomb = sb("B_bc", [128, 8], F32)
    crep = sb("B_crep", [128, 3], F32)
    selrep = sb("B_selrep", [128, 2], F32)
    iotaf, iotag = g["iotaf"], g["iotag"]
    idxf = sb("B_idxf", [128, 33], F32)
    w1idx = sb("B_w1idx", [128, 2, 8], U32)
    w2idx = sb("B_w2idx", [128, 2, 32], U32)
    bidxf = sb("B_bidxf", [32, 1], F32)
    onesr = sb("B_onesr", [1, 128], F32)
    bidx = sb("B_bidx", [32, 1], U32)
    btmp = sb("B_btmp", [32, 128], F32)

    P.op("vector", lambda e: e.memset(onesr[:], 1.0), writes=["B_onesr"])
    P.op("sync", lambda e: e.dma_start(out=wg[:], in_=ins["wgate"].rearrange(
        "(c p) e8 -> p c e8", p=128)), dma=True, writes=["B_wg"])
    P.op("sync", lambda e: e.dma_start(out=bgt[:], in_=ins["bgate"]), dma=True,
         writes=["B_bg"])
    P.op("sync", lambda e: e.dma_start(out=alv[:], in_=ins["alpha"]), dma=True,
         writes=["B_al"])
    P.op("sync", lambda e: e.dma_start(out=b1m[:], in_=ins["b1mcm"]), dma=True,
         writes=["B_b1m"])
    P.op("sync", lambda e: e.dma_start(out=b2m[:], in_=ins["b2mcm"]), dma=True,
         writes=["B_b2m"])

    W1mv = ins["W1m"].rearrange("(c p) hh -> p c hh", p=128)
    W2mv = ins["W2m"].rearrange("(gg p) ee -> p gg ee", p=128)
    W1ef = ins["W1e"].rearrange("e d h -> (e d) h")
    W2ef = ins["W2e"].rearrange("e h d -> (e h) d")

    def load_w1(path, s, par):
        if path == "m":
            P.op("gpsimd", lambda e, s=s, par=par: e.dma_start(
                out=w1sl[:, par, :, :],
                in_=W1mv[:, :, s * 1024:(s + 1) * 1024]), dma=True,
                writes=[f"B_w1b{par}c{c}" for c in range(NCH)])
        else:
            k = 0 if path == "a" else 1
            for c in range(NCH):
                P.op("gpsimd", lambda e, s=s, par=par, c=c, k=k:
                     e.indirect_dma_start(
                         out=w1sl[:, par, c, :], out_offset=None, in_=W1ef,
                         in_offset=IndirectOffsetOnAxis(ap=w1idx[:, k, :][:, c:c + 1],
                                                        axis=0),
                         element_offset=s * 1024), dma=True,
                     reads=[f"B_w1idx{k}"], writes=[f"B_w1b{par}c{c}"])

    def load_w2(path, s, par):
        if path == "m":
            P.op("gpsimd", lambda e, s=s, par=par: e.dma_start(
                out=w2sl[:, par, :, :],
                in_=W2mv[:, s * 8:(s + 1) * 8, :]), dma=True,
                writes=[f"B_w2b{par}c{c}" for c in range(NCH)])
        else:
            k = 0 if path == "a" else 1
            for gg in range(NCH):
                P.op("gpsimd", lambda e, s=s, par=par, gg=gg, k=k:
                     e.indirect_dma_start(
                         out=w2sl[:, par, gg, :], out_offset=None, in_=W2ef,
                         in_offset=IndirectOffsetOnAxis(
                             ap=w2idx[:, k, :][:, s * 8 + gg:s * 8 + gg + 1],
                             axis=0)), dma=True,
                     reads=[f"B_w2idx{k}"], writes=[f"B_w2b{par}c{gg}"])

    load_w1("m", 0, 0)
    load_w2("m", 0, 0)

    gs = lambda a, b: gsc[:, a:b]
    for c in range(NCH):
        P.op("tensor", lambda e, c=c: e.matmul(ps[0][0:1, 0:8], msum[:, c:c + 1],
                                               wg[:, c, :], start=(c == 0),
                                               stop=(c == NCH - 1)),
             reads=[f"Bgmsum{c}", "B_wg"], writes=["pb0"])
    P.op("vector", lambda e: e.tensor_scalar(gs(0, 8), ps[0][0:1, 0:8],
                                             1.0 / T, None, ALU.mult),
         reads=["pb0"], writes=["Bg_lg"])
    P.op("vector", lambda e: e.tensor_add(gs(0, 8), gs(0, 8), bgt[:]),
         reads=["B_bg"], writes=["Bg_lg"])
    P.op("vector", lambda e: e.reduce_max(gs(8, 9), gs(0, 8), axis=AX.X),
         reads=["Bg_lg"], writes=["Bg_mx"])
    P.op("vector", lambda e: e.tensor_scalar(gs(9, 10), gs(8, 9), -1.0, None,
                                             ALU.mult),
         reads=["Bg_mx"], writes=["Bg_nmx"])
    P.op("scalar", lambda e: e.activation(gs(16, 24), gs(0, 8), AF.Exp,
                                          bias=gs(9, 10)),
         reads=["Bg_lg", "Bg_nmx"], writes=["Bg_ex"])
    P.op("vector", lambda e: e.reduce_sum(gs(10, 11), gs(16, 24), axis=AX.X),
         reads=["Bg_ex"], writes=["Bg_sm"])
    P.op("vector", lambda e: e.reciprocal(gs(11, 12), gs(10, 11)),
         reads=["Bg_sm"], writes=["Bg_rs"])
    P.op("vector", lambda e: e.tensor_scalar(gs(24, 32), gs(16, 24), gs(11, 12),
                                             None, ALU.mult),
         reads=["Bg_ex", "Bg_rs"], writes=["Bg_p"])
    P.op("vector", lambda e: e.reduce_max(gs(12, 13), gs(24, 32), axis=AX.X),
         reads=["Bg_p"], writes=["Bg_m1"])
    P.op("vector", lambda e: e.tensor_scalar(gs(32, 40), gs(24, 32), gs(12, 13),
                                             None, ALU.is_ge),
         reads=["Bg_p", "Bg_m1"], writes=["Bg_is"])
    P.op("vector", lambda e: e.tensor_mul(gs(40, 48), gs(32, 40), iotaf[0:1, 0:8]),
         reads=["Bg_is", "consts"], writes=["Bg_t"])
    P.op("vector", lambda e: e.reduce_sum(gs(13, 14), gs(40, 48), axis=AX.X),
         reads=["Bg_t"], writes=["Bg_i1"])
    P.op("vector", lambda e: e.tensor_scalar(gs(13, 14), gs(13, 14), 1.0 / 128,
                                             None, ALU.mult),
         reads=["Bg_i1"], writes=["Bg_i1"])
    P.op("vector", lambda e: e.tensor_scalar(gs(32, 40), gs(32, 40), 2.0, None,
                                             ALU.mult),
         reads=["Bg_is"], writes=["Bg_is"])
    P.op("vector", lambda e: e.tensor_sub(gs(48, 56), gs(24, 32), gs(32, 40)),
         reads=["Bg_p", "Bg_is"], writes=["Bg_pw"])
    P.op("vector", lambda e: e.reduce_max(gs(14, 15), gs(48, 56), axis=AX.X),
         reads=["Bg_pw"], writes=["Bg_m2"])
    P.op("vector", lambda e: e.tensor_scalar(gs(32, 40), gs(48, 56), gs(14, 15),
                                             None, ALU.is_ge),
         reads=["Bg_pw", "Bg_m2"], writes=["Bg_is"])
    P.op("vector", lambda e: e.tensor_mul(gs(40, 48), gs(32, 40), iotaf[0:1, 0:8]),
         reads=["Bg_is", "consts"], writes=["Bg_t"])
    P.op("vector", lambda e: e.reduce_sum(gs(15, 16), gs(40, 48), axis=AX.X),
         reads=["Bg_t"], writes=["Bg_i2"])
    P.op("vector", lambda e: e.tensor_scalar(gs(15, 16), gs(15, 16), 1.0 / 128,
                                             None, ALU.mult),
         reads=["Bg_i2"], writes=["Bg_i2"])
    P.op("vector", lambda e: e.tensor_add(gs(56, 57), gs(12, 13), gs(14, 15)),
         reads=["Bg_m1", "Bg_m2"], writes=["Bg_den"])
    P.op("vector", lambda e: e.reciprocal(gs(57, 58), gs(56, 57)),
         reads=["Bg_den"], writes=["Bg_rden"])
    P.op("vector", lambda e: e.tensor_mul(gs(58, 59), gs(12, 13), gs(57, 58)),
         reads=["Bg_m1", "Bg_rden"], writes=["Bg_gA"])
    P.op("vector", lambda e: e.tensor_mul(gs(59, 60), gs(14, 15), gs(57, 58)),
         reads=["Bg_m2", "Bg_rden"], writes=["Bg_gB"])
    P.op("vector", lambda e: e.tensor_scalar(gs(60, 61), alv[:], 0.1, 1.0,
                                             ALU.max, ALU.min),
         reads=["B_al"], writes=["Bg_a"])
    P.op("vector", lambda e: e.tensor_scalar(gs(61, 62), gs(60, 61), -1.0, 1.0,
                                             ALU.mult, ALU.add),
         reads=["Bg_a"], writes=["Bg_oma"])
    P.op("vector", lambda e: e.tensor_mul(gs(62, 63), gs(61, 62), gs(58, 59)),
         reads=["Bg_oma", "Bg_gA"], writes=["Bg_cA"])
    P.op("vector", lambda e: e.tensor_mul(gs(63, 64), gs(61, 62), gs(59, 60)),
         reads=["Bg_oma", "Bg_gB"], writes=["Bg_cB"])
    for i, (a0, rr) in enumerate([(60, "Bg_a"), (62, "Bg_cA"), (63, "Bg_cB")]):
        P.op("tensor", lambda e, a0=a0, i=i: e.matmul(
            ps[1][:, i:i + 1], onesr[:], gsc[:, a0:a0 + 1],
            start=True, stop=True),
            reads=[rr, "B_onesr"], writes=["pb1"])
    P.op("vector", lambda e: e.tensor_copy(crep[:], ps[1][:, 0:3]),
         reads=["pb1"], writes=["B_crep"])
    for i, (a0, rr) in enumerate([(13, "Bg_i1"), (15, "Bg_i2")]):
        P.op("tensor", lambda e, a0=a0, i=i: e.matmul(
            ps[2][:, i:i + 1], onesr[:], gsc[:, a0:a0 + 1],
            start=True, stop=True),
            reads=[rr, "B_onesr"], writes=["pb2"])
    P.op("vector", lambda e: e.tensor_copy(selrep[:], ps[2][:, 0:2]),
         reads=["pb2"], writes=["B_selrep"])

    b1ef = ins["b1e"].rearrange("e (gg j) -> (e gg) j", j=128)
    b2ef = ins["b2e"].rearrange("e (gg j) -> (e gg) j", j=128)
    for k in range(2):
        sc = selrep[:, k:k + 1]
        P.op("vector", lambda e, sc=sc: e.tensor_scalar(
            idxf[:, 0:1], sc, 1024.0, None, ALU.mult),
            reads=["B_selrep"], writes=["B_idxt"])
        P.op("vector", lambda e, k=k: e.tensor_scalar(
            idxf[:, 1:9], iotaf[:, 0:8], idxf[:, 0:1], None, ALU.add),
            reads=["B_idxt", "consts"], writes=["B_idxf"])
        P.op("vector", lambda e, k=k: e.tensor_copy(w1idx[:, k, :], idxf[:, 1:9]),
             reads=["B_idxf"], writes=[f"B_w1idx{k}"])
        P.op("vector", lambda e, sc=sc: e.tensor_scalar(
            idxf[:, 0:1], sc, 4096.0, None, ALU.mult),
            reads=["B_selrep"], writes=["B_idxt"])
        P.op("vector", lambda e: e.tensor_scalar(
            idxf[:, 1:33], iotaf[:], idxf[:, 0:1], None, ALU.add),
            reads=["B_idxt", "consts"], writes=["B_idxf"])
        P.op("vector", lambda e, k=k: e.tensor_copy(w2idx[:, k, :], idxf[:, 1:33]),
             reads=["B_idxf"], writes=[f"B_w2idx{k}"])
        P.op("vector", lambda e, sc=sc: e.tensor_scalar(
            bidxf[:], sc[0:32, :], 32.0, None, ALU.mult),
            reads=["B_selrep"], writes=["B_bidxf"])
        P.op("vector", lambda e: e.tensor_scalar(
            bidxf[:], iotag[0:32, :], bidxf[:], None, ALU.add),
            reads=["B_bidxf", "consts"], writes=["B_bidxf"])
        P.op("vector", lambda e: e.tensor_copy(bidx[:], bidxf[:]),
             reads=["B_bidxf"], writes=["B_bidx"])
        P.op("gpsimd", lambda e: e.indirect_dma_start(
            out=btmp[:], out_offset=None, in_=b1ef,
            in_offset=IndirectOffsetOnAxis(ap=bidx[:], axis=0)), dma=True,
            reads=["B_bidx"], writes=["B_btmp"])
        P.op("tensor", lambda e: e.transpose(ps[3][:, 0:32], btmp[:],
                                             g["eyef"][0:32, 0:32]),
             reads=["B_btmp", "consts"], writes=["pb3"])
        P.op("vector", lambda e, k=k: e.tensor_copy(b1x[:, k, :], ps[3][:, 0:32]),
             reads=["pb3"], writes=[f"B_b1x{k}"])
        P.op("vector", lambda e, sc=sc: e.tensor_scalar(
            bidxf[0:8, :], sc[0:8, :], 8.0, None, ALU.mult),
            reads=["B_selrep"], writes=["B_bidxf"])
        P.op("vector", lambda e: e.tensor_scalar(
            bidxf[0:8, :], iotag[0:8, :], bidxf[0:8, :], None, ALU.add),
            reads=["B_bidxf", "consts"], writes=["B_bidxf"])
        P.op("vector", lambda e: e.tensor_copy(bidx[0:8, :], bidxf[0:8, :]),
             reads=["B_bidxf"], writes=["B_bidx"])
        P.op("gpsimd", lambda e: e.indirect_dma_start(
            out=btmp[0:8, :], out_offset=None, in_=b2ef,
            in_offset=IndirectOffsetOnAxis(ap=bidx[0:8, :], axis=0)), dma=True,
            reads=["B_bidx"], writes=["B_btmp"])
        P.op("tensor", lambda e: e.transpose(ps[3][:, 0:8], btmp[0:8, :],
                                             g["eyef"][0:8, 0:8]),
             reads=["B_btmp", "consts"], writes=["pb3"])
        P.op("vector", lambda e, k=k: e.tensor_copy(b2x[:, k, :], ps[3][:, 0:8]),
             reads=["pb3"], writes=[f"B_b2x{k}"])
    P.op("vector", lambda e: e.tensor_scalar(bcomb[:], b2m[:], crep[:, 0:1],
                                             None, ALU.mult),
         reads=["B_b2m", "B_crep"], writes=["B_bc"])
    for k in range(2):
        P.op("vector", lambda e, k=k: e.tensor_scalar(
            idxf[:, 0:8], b2x[:, k, :], crep[:, k + 1:k + 2], None, ALU.mult),
            reads=[f"B_b2x{k}", "B_crep"], writes=["B_idxf"])
        P.op("vector", lambda e: e.tensor_add(bcomb[:], bcomb[:], idxf[:, 0:8]),
             reads=["B_idxf"], writes=["B_bc"])

    paths = [("m", b1m, 0), ("a", b1x[:, 0, :], 1), ("b", b1x[:, 1, :], 2)]
    sl_ctr = 0
    for pi, (path, b1c, ci_) in enumerate(paths):
        for s in range(4):
            drain_bg(g, 2)
            par = sl_ctr % 2
            if s < 3:
                load_w1(path, s + 1, (sl_ctr + 1) % 2)
                load_w2(path, s + 1, (sl_ctr + 1) % 2)
            elif pi < 2:
                load_w1(paths[pi + 1][0], 0, (sl_ctr + 1) % 2)
                load_w2(paths[pi + 1][0], 0, (sl_ctr + 1) % 2)
            for tt in range(NTT):
                for hc in range(NCH):
                    bank = (tt * NCH + hc) % 4
                    st_ = slice(tt * TT, (tt + 1) * TT)
                    for c in range(NCH):
                        P.op("tensor", lambda e, c=c, hc=hc, st_=st_, par=par, bank=bank:
                             e.matmul(ps[bank][:],
                                      w1sl[:, par, c, hc * 128:(hc + 1) * 128],
                                      xln[:, c, st_], start=(c == 0),
                                      stop=(c == NCH - 1)),
                             reads=[f"B_w1b{par}c{c}"] + xln_reads(c, tt),
                             writes=[f"pb{bank}"])
                    gx = s * 8 + hc
                    P.op("scalar", lambda e, hc=hc, st_=st_, bank=bank, b1c=b1c, gx=gx:
                         e.activation(hT[:, hc, st_], ps[bank][:],
                                      AF.Gelu_apprx_tanh,
                                      bias=b1c[:, gx:gx + 1]),
                         reads=[f"pb{bank}", "B_b1m", "B_b1x0", "B_b1x1"],
                         writes=[f"B_h{hc}_{tt}"])
            for tt in range(NTT):
                for ec in range(NCH):
                    bank = 4 + (tt * NCH + ec) % 4
                    st_ = slice(tt * TT, (tt + 1) * TT)
                    for hc in range(NCH):
                        P.op("tensor", lambda e, hc=hc, ec=ec, st_=st_, par=par, bank=bank:
                             e.matmul(ps[bank][:],
                                      w2sl[:, par, hc, ec * 128:(ec + 1) * 128],
                                      hT[:, hc, st_], start=(hc == 0),
                                      stop=(hc == NCH - 1)),
                             reads=[f"B_w2b{par}c{hc}", f"B_h{hc}_{tt}"],
                             writes=[f"pb{bank}"])
                    P.op("vector", lambda e, ec=ec, st_=st_, bank=bank, ci_=ci_:
                         e.scalar_tensor_tensor(x[:, ec, st_], ps[bank][:],
                                                crep[:, ci_:ci_ + 1],
                                                x[:, ec, st_],
                                                ALU.mult, ALU.add),
                         reads=[f"pb{bank}", "B_crep"], writes=[f"rx{ec}_{tt}"])
            sl_ctr += 1
    for c in range(NCH):
        for tt in range(NTT):
            st_ = slice(tt * TT, (tt + 1) * TT)
            P.op("scalar", lambda e, c=c, st_=st_: e.add(
                x[:, c, st_], x[:, c, st_], bcomb[:, c:c + 1]),
                reads=["B_bc"], writes=[f"rx{c}_{tt}"])



def emit_phaseC(P, ctx, g):
    sb = g["ov"].sb
    x, xln, ps = g["x"], g["xln"], g["psum"]
    ins = g["ins"]

    wqp = sb("C_wq", [128, 2, NCH, 128], BF16)
    wkp = sb("C_wk", [128, 2, NCH, 128], BF16)
    wvp = sb("C_wv", [128, 2, NCH, 128], BF16)
    wop = sb("C_wo", [128, 2, 1024], BF16)
    qp = sb("C_qp", [128, 2, T], BF16)
    kp = sb("C_kp", [128, 2, T], BF16)
    vp = sb("C_vp", [128, 2, 16, 256], BF16)
    att = sb("C_att", [128, 2, NTT, TT], BF16)
    pbuf = sb("C_p", [128, 4, TT], BF16)
    rq = sb("C_rq", [128, TT], F32)

    Wq = ins["Wqc"].rearrange("(c p) f -> p c f", p=128)
    Wk = ins["Wkc"].rearrange("(c p) f -> p c f", p=128)
    Wv = ins["Wvc"].rearrange("(c p) f -> p c f", p=128)
    Wo = ins["Wo1"].rearrange("(q p) o -> p q o", p=128)

    P.op("vector", lambda e: e.memset(vp[:, :, :, 64:128], 1.0), writes=["C_vo"])
    P.op("vector", lambda e: e.memset(vp[:, :, :, 192:256], 1.0), writes=["C_vo"])

    def load_pair(pp, par):
        fs = slice(pp * 128, (pp + 1) * 128)
        P.op("gpsimd", lambda e, fs=fs, par=par: e.dma_start(
            out=wqp[:, par, :, :], in_=Wq[:, :, fs]), dma=True,
            writes=[f"C_wq{par}"])
        P.op("gpsimd", lambda e, fs=fs, par=par: e.dma_start(
            out=wkp[:, par, :, :], in_=Wk[:, :, fs]), dma=True,
            writes=[f"C_wk{par}"])
        P.op("gpsimd", lambda e, fs=fs, par=par: e.dma_start(
            out=wvp[:, par, :, :], in_=Wv[:, :, fs]), dma=True,
            writes=[f"C_wv{par}"])
        P.op("gpsimd", lambda e, pp=pp, par=par: e.dma_start(
            out=wop[:, par, :], in_=Wo[:, pp, :]), dma=True,
            writes=[f"C_wo{par}"])

    def proj_units(pp, par):
        units = []
        for tt in range(NTT):
            st_ = slice(tt * TT, (tt + 1) * TT)
            for name, wt, dst in (("q", wqp, qp), ("k", wkp, kp)):
                bank = (4 if name == "q" else 5) + 2 * (tt % 2)

                def u(name=name, wt=wt, dst=dst, bank=bank, st_=st_, tt=tt):
                    for c in range(NCH):
                        P.op("tensor", lambda e, c=c: e.matmul(
                            ps[bank][:], wt[:, par, c, :], xln[:, c, st_],
                            start=(c == 0), stop=(c == NCH - 1)),
                            reads=[f"C_w{name}{par}"] + xln_reads(c, tt),
                            writes=[f"pb{bank}"])
                    P.op("vector", lambda e: e.tensor_copy(dst[:, par, st_],
                                                           ps[bank][:]),
                         reads=[f"pb{bank}"], writes=[f"C_{name}{par}_{tt}"])
                units.append(u)
        for tc in range(16):
            bank = 2 + tc % 2

            def u(tc=tc, bank=bank):
                for c in range(NCH):
                    P.op("tensor", lambda e, c=c: e.matmul(
                        ps[bank][:, 0:128],
                        xln[:, c, tc * 128:(tc + 1) * 128],
                        wvp[:, par, c, :], start=(c == 0),
                        stop=(c == NCH - 1)),
                        reads=[f"C_wv{par}"] + xln_reads(c, tc // 4),
                        writes=[f"pb{bank}"])
                P.op("vector", lambda e: e.tensor_copy(vp[:, par, tc, 0:64],
                                                       ps[bank][:, 0:64]),
                     reads=[f"pb{bank}"], writes=[f"C_v{par}_{tc}"])
                P.op("vector", lambda e: e.tensor_copy(vp[:, par, tc, 128:192],
                                                       ps[bank][:, 64:128]),
                     reads=[f"pb{bank}"], writes=[f"C_v{par}_{tc}"])
            units.append(u)
        return units

    def attn_pair(pp, par):
        def c_sexp(qj, ci):
            sqj = slice(qj * TT, (qj + 1) * TT)
            for hh in range(2):
                hs = slice(hh * 64, (hh + 1) * 64)
                sbank = 4 + hh + 2 * (ci % 2)
                P.op("tensor", lambda e, ci=ci, hs=hs, sqj=sqj, sbank=sbank:
                     e.matmul(ps[sbank][:], kp[hs, par, ci * 128:(ci + 1) * 128],
                              qp[hs, par, sqj], start=True, stop=True),
                     reads=[f"C_k{par}_{ci // 4}", f"C_q{par}_{qj}"],
                     writes=[f"pb{sbank}"])
            for hh in range(2):
                sbank = 4 + hh + 2 * (ci % 2)
                ppx = 2 * (ci % 2) + hh
                P.op("scalar", lambda e, ppx=ppx, sbank=sbank:
                     e.activation(pbuf[:, ppx, :], ps[sbank][:], AF.Exp,
                                  bias=g["cbias"][:, 1:2], scale=1.0 / 8.0),
                     reads=[f"pb{sbank}", "consts"], writes=[f"C_P{ppx}"])

        def c_pv(qj, ci):
            pvb = 2 * (qj % 2)
            for hh in range(2):
                ppx = 2 * (ci % 2) + hh
                P.op("tensor", lambda e, ci=ci, ppx=ppx, hh=hh, pvb=pvb:
                     e.matmul(ps[pvb + hh][:], vp[:, par, ci,
                                                  hh * 128:(hh + 1) * 128],
                              pbuf[:, ppx, :], start=(ci == 0), stop=(ci == 15)),
                     reads=[f"C_v{par}_{ci}", f"C_P{ppx}", "C_vo"],
                     writes=[f"pb{pvb + hh}"])

        def c_evac(qj):
            pvb = 2 * (qj % 2)
            for hh in range(2):
                hs = slice(hh * 64, (hh + 1) * 64)
                rqh = rq[hh * 64:(hh + 1) * 64, :]
                P.op("scalar", lambda e, hh=hh, pvb=pvb: e.activation(
                    ps[pvb + hh][64:128, :], ps[pvb + hh][64:128, :], AF.Ln),
                    reads=[f"pb{pvb + hh}"], writes=[f"pb{pvb + hh}"])
                P.op("scalar", lambda e, hh=hh, rqh=rqh, pvb=pvb: e.activation(
                    rqh, ps[pvb + hh][64:128, :], AF.Exp, scale=-1.0),
                    reads=[f"pb{pvb + hh}"], writes=[f"C_rq{hh}"])
                P.op("vector", lambda e, hh=hh, hs=hs, qj=qj, pvb=pvb, rqh=rqh:
                     e.tensor_mul(att[hs, par, qj, :], ps[pvb + hh][0:64, :],
                                  rqh),
                     reads=[f"pb{pvb + hh}", f"C_rq{hh}"],
                     writes=[f"C_att{par}_{qj}"])

        c_sexp(0, 0)
        for u in range(1, 64):
            qj, ci = divmod(u, 16)
            c_sexp(qj, ci)
            pq, pc = divmod(u - 1, 16)
            c_pv(pq, pc)
            if pc == 15:
                c_evac(pq)
        c_pv(3, 15)
        c_evac(3)

    def outproj_units(pp, par):
        units = []
        for oc in range(NCH):
            for tt in range(NTT):
                bank = (oc * NTT + tt) % 2
                st_ = slice(tt * TT, (tt + 1) * TT)

                def u(oc=oc, tt=tt, bank=bank, st_=st_):
                    P.op("tensor", lambda e: e.matmul(
                        ps[bank][:], wop[:, par, oc * 128:(oc + 1) * 128],
                        att[:, par, tt, :], start=True, stop=True),
                        reads=[f"C_wo{par}", f"C_att{par}_{tt}"],
                        writes=[f"pb{bank}"])
                    P.op("vector", lambda e: e.tensor_add(
                        x[:, oc, st_], x[:, oc, st_], ps[bank][:]),
                        reads=[f"pb{bank}"], writes=[f"rx{oc}_{tt}"])
                units.append(u)
        return units

    load_pair(0, 0)
    for u in proj_units(0, 0):
        u()
    load_pair(1, 1)
    for pp in range(8):
        drain_bg(g, 2)
        par = pp % 2
        attn_pair(pp, par)
        pu = proj_units(pp + 1, (pp + 1) % 2) if pp < 7 else []
        ou = outproj_units(pp, par)
        n = max(len(pu), len(ou))
        for i in range(n):
            if i < len(pu):
                pu[i]()
            if i < len(ou):
                ou[i]()
        if pp < 6:
            load_pair(pp + 2, par)



def emit_phaseD(P, ctx, g, out_ap):
    nc = P.nc
    x, xln, ps = g["x"], g["xln"], g["psum"]
    ins = g["ins"]
    ov = g["ov"]
    jtok = g["jtok"]
    NEX, CAP, W = 8, 512, 1040

    gwT = ov.sb("D_gw", [128, NCH, 8], BF16)
    dsm = ov.sb("D_sm", [128, 48], F32)
    ptok = ov.sb("D_pt", [128, 16, 8], F32)
    ptok2 = ov.sb("D_pt2", [128, 8, 16], F32)
    cmpb = ov.sb("D_cmpb", [128, 8, 16], F32)
    bis = ov.sb("D_bis", [128, 6, 8], F32)
    cnt8 = ov.sb("D_cnt8", [128, 8], BF16)
    dsc = ov.sb("D_sc", [8, T], F32)
    cat = ov.sb("D_cat", [128, 16, W], BF16)
    w1sl = ov.sb("D_w1", [128, 2, NCH, 512], BF16)
    w2c = ov.sb("D_w2", [128, 2, 4, E], BF16)
    stage = ov.sb("D_stage", [128, 2, E], F32)
    xin = ov.sb("D_xin", [128, 4, W], BF16)
    eo = ov.sb("D_eo", [128, 4, E], BF16)
    tidu = ov.sb("D_tid", [128, 2, 4], U32)
    tscr = ov.sb("D_tscr", [128, 2], F32)
    gf = ov.sb("D_gf", [128, 2, 4], F32)
    xinTb = [nc.alloc_sbuf_tensor_at("D_xinT0", [128, NCH, CAP], BF16,
                                     offset=nc.lookup_mloc(stage).addr),
             nc.alloc_sbuf_tensor_at("D_xinT1", [128, NCH, CAP], BF16,
                                     offset=nc.lookup_mloc(dsc).addr)]
    hT = nc.alloc_sbuf_tensor_at("D_hT", [128, 32, CAP], BF16,
                                 offset=nc.lookup_mloc(xln).addr)
    xbc = xln[:].rearrange("p c t -> p (c t)").bitcast(F32)
    bufB = xbc[0:8, 0:2048]
    bufD = xbc[0:8, 2048:4096]
    alB = [f"xln{c}_{t}" for c in (0, 1) for t in range(NTT)]
    alD = [f"xln{c}_{t}" for c in (2, 3) for t in range(NTT)]
    bufC = dsc[0:8, :]

    xin_d = [nc.dram_tensor(f"D_xin{e}", [CAP, W], BF16, kind="Internal").ap()
             for e in range(NEX)]
    bndreg = {}

    def bnd(e):
        if "r" not in bndreg:
            bndreg["r"] = e.to_reg(CAP - 1)
        return bndreg["r"]

    W1B = g["w1b"].rearrange("ee (c p) h -> ee p c h", p=128)
    W2B = g["w2b"].rearrange("ee (hc p) d -> ee p hc d", p=128)

    def load_w1(ex, s, par):
        P.op("sync", lambda e, ex=ex, s=s, par=par: e.dma_start(
            out=w1sl[:, par, :, :],
            in_=W1B[ex, :, :, s * 512:(s + 1) * 512]), dma=True,
            reads=[f"L1s1{ex}"], writes=[f"D_w1{par}"])

    def load_w2(ex, g4, par):
        P.op("scalar", lambda e, ex=ex, g4=g4, par=par: e.dma_start(
            out=w2c[:, par, :, :], in_=W2B[ex, :, g4 * 4:(g4 + 1) * 4, :]),
            dma=True, reads=[f"L1s2{ex}"], writes=[f"D_w2{par}"])

    load_w1(0, 0, 0)
    load_w2(0, 0, 0)
    P.op("gpsimd", lambda e: e.dma_start(out=gwT[:], in_=ins["gwT"].rearrange(
        "(c p) e8 -> p c e8", p=128)), dma=True, writes=["D_gw"])
    P.op("gpsimd", lambda e: e.dma_start(out=cat[:, :, 1032:1034],
                                         in_=ins["tidhl"]), dma=True,
         writes=[f"D_cat{tc}" for tc in range(16)])
    drain_bg(g, 99)

    for tt in range(NTT):
        st_ = slice(tt * TT, (tt + 1) * TT)
        bank = tt % 2
        for c in range(NCH):
            P.op("tensor", lambda e, c=c, st_=st_, bank=bank:
                 e.matmul(ps[bank][0:8, :], gwT[:, c, :], xln[:, c, st_],
                          start=(c == 0), stop=(c == NCH - 1)),
                 reads=["D_gw"] + xln_reads(c, tt), writes=[f"pb{bank}"])
        P.op("vector", lambda e, st_=st_, bank=bank:
             e.tensor_copy(dsc[:, st_], ps[bank][0:8, :]),
             reads=[f"pb{bank}"], writes=["D_dsc"])

    fill_units = []

    def fill_copy(dst, src_ps, bank, wr, idx):
        if idx % 3 == 2:
            P.op("vector", lambda e: e.tensor_copy(dst, src_ps),
                 reads=[f"pb{bank}"], writes=wr)
        else:
            P.op("scalar", lambda e: e.activation(dst, src_ps, AF.Copy),
                 reads=[f"pb{bank}"], writes=wr)

    for tc in range(16):
        for ec in range(NCH):
            idx = tc * NCH + ec
            bank = 4 + idx % 4

            def u(tc=tc, ec=ec, bank=bank, idx=idx):
                psb = ps[bank][:].bitcast(BF16)
                P.op("tensor", lambda e, psb=psb: e.transpose(
                    psb[:, 0:128], xln[:, ec, tc * 128:(tc + 1) * 128],
                    g["eyeb"][:]),
                    reads=xln_reads(ec, tc // 4) + ["consts"],
                    writes=[f"pb{bank}"])
                fill_copy(cat[:, tc, ec * 128:(ec + 1) * 128], psb[:, 0:128],
                          bank, [f"D_cat{tc}"], idx)
            fill_units.append(u)
    for tc in range(16):
        spar = tc % 2
        for ec in range(NCH):
            idx = tc * NCH + ec
            bank = 4 + idx % 4

            def u(tc=tc, ec=ec, bank=bank, spar=spar, idx=idx):
                P.op("tensor", lambda e: e.transpose(
                    ps[bank][:, 0:128], x[:, ec, tc * 128:(tc + 1) * 128],
                    g["eyef"][:]),
                    reads=[f"rx{ec}_{tc // 4}", "consts"], writes=[f"pb{bank}"])
                fill_copy(stage[:, spar, ec * 128:(ec + 1) * 128],
                          ps[bank][:, 0:128], bank, [f"D_st{spar}"], idx + 1)
            fill_units.append(u)

        def u(tc=tc, spar=spar):
            qeng = "sync" if tc % 2 == 0 else "scalar"
            P.op(qeng, lambda e: e.dma_start(
                out=out_ap[tc * 128:(tc + 1) * 128, :], in_=stage[:, spar, :]),
                dma=True, reads=[f"D_st{spar}"], writes=["outdram"])
        fill_units.append(u)

    def fdrain(k):
        for _ in range(k):
            if fill_units:
                fill_units.pop(0)()

    PT = [f"D_ptok{t}" for t in range(16)]
    for tc in range(16):
        bank = 2 + tc % 2
        P.op("tensor", lambda e, tc=tc, bank=bank: e.transpose(
            ps[bank][:, 0:8], dsc[0:8, tc * 128:(tc + 1) * 128],
            g["eyef"][0:8, 0:8]),
            reads=["D_dsc", "consts"], writes=[f"pb{bank}"])
        P.op("vector", lambda e, tc=tc, bank=bank: e.tensor_copy(
            ptok[:, tc, :], ps[bank][:, 0:8]),
            reads=[f"pb{bank}"], writes=[f"D_ptok{tc}"])
    p3 = ptok[:]
    mxb = dsm[:, 0:16].unsqueeze(2).broadcast_to([128, 16, 8])
    smb = dsm[:, 16:32].unsqueeze(2).broadcast_to([128, 16, 8])
    P.op("vector", lambda e: e.reduce_max(dsm[:, 0:16], p3, axis=AX.X),
         reads=PT, writes=["D_mx"])
    fdrain(8)
    P.op("vector", lambda e: e.tensor_sub(p3, p3, mxb),
         reads=PT + ["D_mx"], writes=PT)
    fdrain(8)
    P.op("scalar", lambda e: e.activation(
        ptok[:].rearrange("p a b -> p (a b)"),
        ptok[:].rearrange("p a b -> p (a b)"), AF.Exp),
        reads=PT, writes=PT)
    P.op("vector", lambda e: e.reduce_sum(dsm[:, 16:32], p3, axis=AX.X),
         reads=PT, writes=["D_sme"])
    fdrain(8)
    P.op("vector", lambda e: e.reciprocal(dsm[:, 16:32], dsm[:, 16:32]),
         reads=["D_sme"], writes=["D_sme"])
    P.op("vector", lambda e: e.tensor_mul(p3, p3, smb),
         reads=PT + ["D_sme"], writes=PT)
    for tc in range(16):
        P.op("vector", lambda e, tc=tc: e.tensor_copy(
            ptok2[:, :, tc:tc + 1], ptok[:, tc, :].unsqueeze(2)),
            reads=[f"D_ptok{tc}"], writes=["D_pt2"])
        P.op("vector", lambda e, tc=tc: e.tensor_copy(
            cat[:, tc, 1024:1032], ptok[:, tc, :]),
            reads=[f"D_ptok{tc}"], writes=[f"D_cat{tc}"])

    for tc in range(16):
        P.op("tensor", lambda e, tc=tc: e.transpose(
            ps[3][0:8, 0:128], ptok[:, tc, :], g["eyef"][:]),
            reads=[f"D_ptok{tc}", "consts"], writes=["pb3"])
        P.op("vector", lambda e, tc=tc: e.tensor_copy(
            dsc[:, tc * 128:(tc + 1) * 128], ps[3][0:8, 0:128]),
            reads=["pb3"], writes=["D_dsc"])

    blo = bis[:, 0, :]
    bhi = bis[:, 1, :]
    bmid = bis[:, 2, :]
    bsel = bis[:, 3, :]
    bt1 = bis[:, 4, :]
    bt2 = bis[:, 5, :]
    midb = bmid.unsqueeze(2).broadcast_to([128, 8, 16])
    P.op("vector", lambda e: e.memset(blo, 0.0), writes=["D_lo"])
    P.op("vector", lambda e: e.memset(bhi, 1.0), writes=["D_hi"])
    for it in range(28):
        P.op("vector", lambda e: e.tensor_add(bmid, blo, bhi),
             reads=["D_lo", "D_hi"], writes=["D_mid"])
        P.op("vector", lambda e: e.tensor_scalar(bmid, bmid, 0.5, None,
                                                 ALU.mult),
             reads=["D_mid"], writes=["D_mid"])
        P.op("vector", lambda e: e.tensor_tensor(cmpb[:], ptok2[:], midb,
                                                 ALU.is_ge),
             reads=["D_pt2", "D_mid"], writes=["D_cmpb"])
        P.op("vector", lambda e: e.reduce_sum(dsm[:, 32:40], cmpb[:],
                                              axis=AX.X),
             reads=["D_cmpb"], writes=["D_cntf"])
        P.op("vector", lambda e: e.tensor_copy(cnt8[:], dsm[:, 32:40]),
             reads=["D_cntf"], writes=["D_cnt8"])
        P.op("tensor", lambda e: e.matmul(ps[2][:, 0:8], g["ln"]["ones128"][:],
                                          cnt8[:], start=True, stop=True),
             reads=["D_cnt8", "consts"], writes=["pb2"])
        fdrain(4)
        P.op("vector", lambda e: e.tensor_scalar(bsel, ps[2][:, 0:8],
                                                 float(CAP), None, ALU.is_ge),
             reads=["pb2"], writes=["D_sel"])
        P.op("vector", lambda e: e.tensor_sub(bt1, bmid, blo),
             reads=["D_mid", "D_lo"], writes=["D_t1"])
        P.op("vector", lambda e: e.tensor_mul(bt1, bt1, bsel),
             reads=["D_t1", "D_sel"], writes=["D_t1"])
        P.op("vector", lambda e: e.tensor_add(blo, blo, bt1),
             reads=["D_lo", "D_t1"], writes=["D_lo"])
        P.op("vector", lambda e: e.tensor_sub(bt2, bhi, bmid),
             reads=["D_hi", "D_mid"], writes=["D_t2"])
        P.op("vector", lambda e: e.tensor_mul(bt2, bt2, bsel),
             reads=["D_t2", "D_sel"], writes=["D_t2"])
        P.op("vector", lambda e: e.tensor_add(bhi, bmid, bt2),
             reads=["D_mid", "D_t2"], writes=["D_hi"])
    lo8 = dsm[0:8, 40:41]
    P.op("vector", lambda e: e.tensor_mul(dsm[0:8, 32:40], blo[0:8, :],
                                          g["eyef"][0:8, 0:8]),
         reads=["D_lo", "consts"], writes=["D_l8t"])
    P.op("vector", lambda e: e.reduce_sum(lo8, dsm[0:8, 32:40], axis=AX.X),
         reads=["D_l8t"], writes=["D_lo8"])

    P.op("vector", lambda e: e.tensor_scalar(bufB[:], dsc[0:8, :],
                                             lo8, None, ALU.is_ge),
         reads=["D_dsc", "D_lo8"], writes=["D_bufB"] + alB)
    P.op("vector", lambda e: e.memset(bufD[:], 0.0),
         writes=["D_bufD"] + alD)
    P.op("vector", lambda e: e.tensor_tensor_scan(bufC[:], bufB[:], bufD[:],
                                                  0.0, ALU.add, ALU.add),
         reads=["D_bufB", "D_bufD", "D_dsc"] + alB + alD,
         writes=["D_bufC", "D_dsc"])
    P.op("vector", lambda e: e.tensor_sub(bufC[:], bufC[:], bufB[:]),
         reads=["D_bufC", "D_bufB"] + alB, writes=["D_bufC"])
    P.op("gpsimd", lambda e: e.iota(bufD[:], [[1, 2048]], base=4096,
                                    channel_multiplier=0,
                                    allow_small_or_imprecise_dtypes=True),
         reads=["D_bufD"], writes=["D_bufD"] + alD)
    P.op("vector", lambda e: e.tensor_mul(bufC[:], bufC[:], bufB[:]),
         reads=["D_bufB", "D_bufC"] + alB, writes=["D_bufC"])
    P.op("vector", lambda e: e.tensor_scalar(bufB[:], bufB[:], -1.0,
                                             1.0, ALU.mult, ALU.add),
         reads=["D_bufB"], writes=["D_bufB"] + alB)
    P.op("vector", lambda e: e.tensor_mul(bufD[:], bufD[:], bufB[:]),
         reads=["D_bufB", "D_bufD"] + alB, writes=["D_bufD"] + alD)
    P.op("vector", lambda e: e.tensor_add(bufC[:], bufC[:], bufD[:]),
         reads=["D_bufD", "D_bufC"] + alD, writes=["D_bufC"])
    for tc in range(16):
        P.op("tensor", lambda e, tc=tc: e.transpose(
            ps[2][0:128, 0:8], bufC[:, tc * 128:(tc + 1) * 128],
            g["eyef"][0:8, 0:8]),
            reads=["D_bufC", "consts"], writes=["pb2"])
        P.op("vector", lambda e, tc=tc: e.tensor_copy(jtok[:, tc, :],
                                                      ps[2][:, 0:8]),
             reads=["pb2"], writes=[f"D_jtok{tc}"])

    def scatter(ex):
        for tc in range(16):
            P.op("gpsimd", lambda e, ex=ex, tc=tc: e.indirect_dma_start(
                out=xin_d[ex], out_offset=IndirectOffsetOnAxis(
                    ap=jtok[:, tc, ex:ex + 1], axis=0),
                in_=cat[:, tc, :], in_offset=None,
                bounds_check=bnd(e), oob_is_err=False), dma=True,
                reads=[f"D_jtok{tc}", f"D_cat{tc}"], writes=[f"D_xind{ex}"])

    scatter(0)
    fdrain(999)

    def gather(ex):
        P.op("sync", lambda e, ex=ex: e.dma_start(
            out=xin[:], in_=xin_d[ex].rearrange("(jc p) w -> p jc w", p=128)),
            dma=True, reads=[f"D_xind{ex}"], writes=["D_xinb"])

    def transpose_units(ex):
        par = ex % 2
        xinT = xinTb[par]
        units = []
        for dc in range(NCH):
            for jc in range(4):
                bank = 2 + (dc * 4 + jc) % 2
                psb = ps[bank][:].bitcast(BF16)
                first = (dc == 0 and jc == 0)
                alias = (["D_st0", "D_st1"] if par == 0 else
                         ["D_dsc", "D_bufC"]) if ex < 2 and first else []

                def u(jc=jc, dc=dc, psb=psb, bank=bank, par=par, xinT=xinT,
                      alias=alias):
                    P.op("tensor", lambda e: e.transpose(
                        psb[:, 0:128], xin[:, jc, dc * 128:(dc + 1) * 128],
                        g["eyeb"][:]),
                        reads=["D_xinb", "consts"], writes=[f"pb{bank}"])
                    P.op("vector", lambda e: e.tensor_copy(
                        xinT[:, dc, jc * 128:(jc + 1) * 128], psb[:, 0:128]),
                        reads=[f"pb{bank}"],
                        writes=[f"D_xinT{par}_{dc}"] + alias)
                units.append(u)
        return units

    def extract(ex, par):
        for jc in range(4):
            P.op("vector", lambda e, jc=jc, ex=ex, par=par: e.tensor_copy(
                gf[:, par, jc:jc + 1], xin[:, jc, 1024 + ex:1025 + ex]),
                reads=["D_xinb"], writes=[f"D_gf{par}"])
            P.op("vector", lambda e, jc=jc: e.tensor_copy(
                tscr[:, 0:1], xin[:, jc, 1032:1033]),
                reads=["D_xinb"], writes=["D_tscr"])
            P.op("vector", lambda e, jc=jc: e.tensor_copy(
                tscr[:, 1:2], xin[:, jc, 1033:1034]),
                reads=["D_xinb"], writes=["D_tscr"])
            P.op("vector", lambda e: e.tensor_scalar(
                tscr[:, 0:1], tscr[:, 0:1], 256.0, None, ALU.mult),
                reads=["D_tscr"], writes=["D_tscr"])
            P.op("vector", lambda e: e.tensor_add(
                tscr[:, 0:1], tscr[:, 0:1], tscr[:, 1:2]),
                reads=["D_tscr"], writes=["D_tscr"])
            P.op("vector", lambda e, jc=jc, par=par: e.tensor_copy(
                tidu[:, par, jc:jc + 1], tscr[:, 0:1]),
                reads=["D_tscr"], writes=[f"D_tid{par}"])

    gather(0)
    for u in transpose_units(0):
        u()
    extract(0, 0)
    scatter(1)

    w1ctr = 0
    for ex in range(NEX):
        par = ex % 2
        if ex + 1 < NEX:
            gather(ex + 1)
        if ex + 2 < NEX:
            scatter(ex + 2)
        tun = transpose_units(ex + 1) if ex + 1 < NEX else []
        xinT = xinTb[par]
        for s in range(8):
            wpar = w1ctr % 2
            if s < 7:
                load_w1(ex, s + 1, (w1ctr + 1) % 2)
            elif ex < NEX - 1:
                load_w1(ex + 1, 0, (w1ctr + 1) % 2)
            if s >= 2:
                for _ in range(6):
                    if tun:
                        tun.pop(0)()
            for hc in range(4):
                gh = s * 4 + hc
                bank = gh % 2
                for dc in range(NCH):
                    P.op("tensor", lambda e, dc=dc, hc=hc, wpar=wpar, bank=bank,
                         xinT=xinT, par=par:
                         e.matmul(ps[bank][:],
                                  w1sl[:, wpar, dc, hc * 128:(hc + 1) * 128],
                                  xinT[:, dc, :], start=(dc == 0),
                                  stop=(dc == NCH - 1)),
                         reads=[f"D_w1{wpar}", f"D_xinT{par}_{dc}"],
                         writes=[f"pb{bank}"])
                P.op("scalar", lambda e, gh=gh, bank=bank: e.activation(
                    hT[:, gh, :], ps[bank][:], AF.Gelu),
                    reads=[f"pb{bank}"],
                    writes=[f"D_h{gh}", f"xln{gh // 4}_{gh % 4}"])
            w1ctr += 1
        for g4 in range(8):
            wpar2 = g4 % 2
            if g4 < 7:
                load_w2(ex, g4 + 1, (g4 + 1) % 2)
            elif ex < NEX - 1:
                load_w2(ex + 1, 0, (g4 + 1) % 2)
            if g4 == 6 and ex + 1 < NEX:
                extract(ex + 1, (ex + 1) % 2)
            for ghl in range(4):
                gh = g4 * 4 + ghl
                for jc in range(4):
                    for dh in range(2):
                        bank = jc * 2 + dh
                        P.op("tensor", lambda e, gh=gh, ghl=ghl, jc=jc, dh=dh,
                             wpar2=wpar2, bank=bank:
                             e.matmul(ps[bank][:],
                                      hT[:, gh, jc * 128:(jc + 1) * 128],
                                      w2c[:, wpar2, ghl, dh * 512:(dh + 1) * 512],
                                      start=(gh == 0), stop=(gh == 31)),
                             reads=[f"D_w2{wpar2}", f"D_h{gh}"],
                             writes=[f"pb{bank}"])
        for jc in range(4):
            for dh in range(2):
                bank = jc * 2 + dh
                P.op("vector", lambda e, jc=jc, dh=dh, bank=bank, par=par:
                     e.tensor_scalar(eo[:, jc, dh * 512:(dh + 1) * 512],
                                     ps[bank][:], gf[:, par, jc:jc + 1], None,
                                     ALU.mult),
                     reads=[f"pb{bank}", f"D_gf{par}"], writes=[f"D_eo{jc}"])
        for jc in range(4):
            P.op("gpsimd", lambda e, jc=jc, par=par: e.indirect_dma_start(
                out=out_ap, out_offset=IndirectOffsetOnAxis(
                    ap=tidu[:, par, jc:jc + 1], axis=0),
                in_=eo[:, jc, :], in_offset=None,
                oob_is_err=True, compute_op=ALU.add), dma=True,
                reads=[f"D_eo{jc}", f"D_tid{par}", "outdram"],
                writes=["outdram"])



def _build(phases):
    nc = bass.Bass("TRN2", target_bir_lowering=False,
                   dynamic_dma_scratch_size=16384)
    ins = {}

    def din(name, shape):
        ins[name] = nc.dram_tensor(name, shape, F32, kind="ExternalInput").ap()

    din("xT", [E, T])
    ins["peT"] = nc.dram_tensor("peT", [D0, T], BF16,
                                kind="ExternalInput").ap()
    ins["cmask"] = nc.dram_tensor("cmask", [128, 896], BF16,
                                  kind="ExternalInput").ap()
    din("eye", [128, 128])
    if "A" in phases:
        din("wqkvT", [E, 3 * H0 * D0])
        din("woT", [H0 * D0, E])
    if "B" in phases:
        din("wgate", [E, 8])
        din("bgate", [1, 8])
        din("alpha", [1, 1])
        din("W1m", [E, 4096])
        din("W2m", [4096, E])
        din("b1mcm", [128, 32])
        din("b2mcm", [128, 8])
        din("W1e", [8, E, 4096])
        din("W2e", [8, 4096, E])
        din("b1e", [8, 4096])
        din("b2e", [8, E])
    if "C" in phases:
        din("Wqc", [E, E])
        din("Wkc", [E, E])
        din("Wvc", [E, E])
        din("Wo1", [E, E])
    if "D" in phases:
        din("gwT", [E, 8])
        din("tidhl", [128, 16, 2])
        din("Ew1", [8, E, 4096])
        din("Ew2", [8, 4096, E])
    if "B" in phases or "D" in phases:
        din("iotachf", [128, 32])
        din("iotagf", [128, 1])
    last = phases[-1]
    if last == "D":
        out_ap = nc.dram_tensor("out", [T, E], F32, kind="ExternalOutput").ap()
    else:
        out_ap = nc.dram_tensor("xTo", [E, T], F32, kind="ExternalOutput").ap()

    with ExitStack() as ctx:
        def sb(name, shape, dt, side="left"):
            return ctx.enter_context(nc.sbuf_tensor(name, shape, dt, side=side))

        P = Prog(nc, ctx)
        g = {"ins": ins, "sb": sb}

        x = sb("x", [128, NCH, T], F32, side="right")
        g["x"] = x
        ones128 = sb("ones128", [128, 128], BF16)
        g["eyeb"] = sb("eyeb", [128, 128], BF16)
        g["eyef"] = sb("eyef", [128, 128], F32)
        g["cbias"] = sb("cbias", [128, 2], F32)
        xln = sb("xln", [128, NCH, T], BF16)
        g["xln"] = xln
        ln_stt = sb("ln_stt", [128, TT], F32)
        g["ln"] = {
            "ones128": ones128,
            "sq2": ln_stt[:].bitcast(BF16),
            "stm": sb("ln_stm", [128, TT], F32),
            "stv": sb("ln_stv", [128, TT], F32),
            "sttmp": ln_stt,
        }
        g["bmsum"] = sb("bmsum", [128, NCH], F32)
        g["jtok"] = sb("jtok", [128, 16, 8], U32)
        g["psum"] = [ctx.enter_context(nc.psum_tensor(f"pb{i}", [128, TT], F32))
                     for i in range(8)]
        if "B" in phases or "D" in phases:
            g["iotaf"] = sb("iotaf", [128, 32], F32)
            g["iotag"] = sb("iotag", [128, 1], F32)
            P.op("sync", lambda e: e.dma_start(out=g["iotaf"][:],
                                               in_=ins["iotachf"]), dma=True,
                 writes=["consts"])
            P.op("sync", lambda e: e.dma_start(out=g["iotag"][:],
                                               in_=ins["iotagf"]), dma=True,
                 writes=["consts"])
        g["bg"] = []
        if "D" in phases:
            w1b = nc.dram_tensor("L1w1b", [8, E, 4096], BF16,
                                 kind="Internal").ap()
            w2b = nc.dram_tensor("L1w2b", [8, 4096, E], BF16,
                                 kind="Internal").ap()
            g["w1b"], g["w2b"] = w1b, w2b
            for ex in range(8):
                for half in range(2):
                    hs = slice(half * 512, (half + 1) * 512)
                    hs2 = slice(half * 2048, (half + 1) * 2048)
                    g["bg"].append(lambda P=P, ex=ex, hs=hs: P.op(
                        "gpsimd", lambda e: e.dma_start(
                            out=w1b[ex, hs, :], in_=ins["Ew1"][ex, hs, :]),
                        dma=True, writes=[f"L1s1{ex}"]))
                    g["bg"].append(lambda P=P, ex=ex, hs2=hs2: P.op(
                        "gpsimd", lambda e: e.dma_start(
                            out=w2b[ex, hs2, :], in_=ins["Ew2"][ex, hs2, :]),
                        dma=True, writes=[f"L1s2{ex}"]))
        arena_bytes = (nc.sbuf_bytes_remaining - 64) // 32 * 32
        arena = sb("arena", [128, arena_bytes], mybir.dt.int8)
        g["ov"] = Overlay(nc, arena)

        P.op("vector", lambda e: e.memset(ones128[:], 1.0), writes=["consts"])
        P.op("vector", lambda e: e.memset(g["cbias"][:, 0:1], -C_L0),
             writes=["consts"])
        P.op("vector", lambda e: e.memset(g["cbias"][:, 1:2], -C_L1),
             writes=["consts"])
        P.op("gpsimd", lambda e: e.dma_start(out=g["eyeb"][:], in_=ins["eye"]),
             dma=True, writes=["consts"])
        P.op("sync", lambda e: e.dma_start(out=g["eyef"][:], in_=ins["eye"]),
             dma=True, writes=["consts"])
        xTr = ins["xT"].rearrange("(c p) t -> p c t", p=128)
        for tt in range(NTT):
            s = slice(tt * TT, (tt + 1) * TT)
            qeng = "sync" if tt % 2 == 0 else "scalar"
            P.op(qeng, lambda e, s=s: e.dma_start(out=x[:, :, s],
                                                  in_=xTr[:, :, s]), dma=True,
                 writes=[f"rx{c}_{tt}" for c in range(NCH)])

        for i, ph in enumerate(phases):
            if ph == "A":
                emit_ln(P, g)
                g["ov"].reset()
                emit_phaseA(P, ctx, g)
            elif ph == "B":
                emit_ln(P, g, gate_msum=g["bmsum"], gtag="B")
                P.barrier()
                g["ov"].reset()
                emit_phaseB(P, ctx, g)
            elif ph == "C":
                emit_ln(P, g)
                P.barrier()
                g["ov"].reset()
                emit_phaseC(P, ctx, g)
            elif ph == "D":
                emit_ln(P, g)
                P.barrier()
                g["ov"].reset()
                emit_phaseD(P, ctx, g, out_ap)
            else:
                raise NotImplementedError(ph)

        if last != "D":
            P.barrier()
            outr = out_ap.rearrange("(c p) t -> p c t", p=128)
            P.op("sync", lambda e: e.dma_start(out=outr, in_=x[:]), dma=True,
                 reads=[f"rx{c}_{t}" for c in range(NCH) for t in range(NTT)],
                 writes=["out"])

        block = ctx.enter_context(nc.Block())
        P.emit(block)
    return nc



def _host_consts():
    pos = np.arange(T, dtype=np.float32)[:, None]
    div = np.exp(np.arange(0, D0, 2, dtype=np.float32) * (-np.log(10000.0) / D0))
    ang = pos * div
    pe = np.stack([np.sin(ang), np.cos(ang)], axis=-1).reshape(T, D0)
    peT = np.ascontiguousarray(pe.T.astype(ml_dtypes.bfloat16))
    r = np.arange(128)[:, None]
    w = np.arange(896)[None, :]
    cmask = ((w - 384) >= r).astype(ml_dtypes.bfloat16)
    eye = np.eye(128, dtype=np.float32)
    return {"peT": peT, "cmask": cmask, "eye": eye}


def _host_shared(inputs, phases):
    sh = dict(_host_consts())
    f32 = lambda a: np.ascontiguousarray(np.asarray(a, np.float32))
    if "A" in phases:
        sh["wqkvT"] = f32(np.asarray(inputs["L0_attn_Wqkv"]).T)
        sh["woT"] = f32(np.asarray(inputs["L0_attn_Wo"]).T)
    if "B" in phases:
        sh["wgate"] = f32(inputs["L0_ffn_w_gate"])
        sh["bgate"] = f32(inputs["L0_ffn_b_gate"]).reshape(1, 8)
        sh["alpha"] = f32(inputs["L0_ffn_alpha"]).reshape(1, 1)
        sh["W1m"] = f32(inputs["L0_ffn_W1m"])
        sh["W2m"] = f32(inputs["L0_ffn_W2m"])
        sh["b1mcm"] = f32(np.asarray(inputs["L0_ffn_b1m"]).reshape(32, 128).T)
        sh["b2mcm"] = f32(np.asarray(inputs["L0_ffn_b2m"]).reshape(8, 128).T)
        sh["W1e"] = f32(inputs["L0_ffn_W1e"])
        sh["W2e"] = f32(inputs["L0_ffn_W2e"])
        sh["b1e"] = f32(inputs["L0_ffn_b1e"])
        sh["b2e"] = f32(inputs["L0_ffn_b2e"])
    if "C" in phases:
        cat = lambda w: f32(np.asarray(w).transpose(1, 0, 2).reshape(E, E))
        sh["Wqc"] = cat(inputs["L1_attn_W_q"])
        sh["Wkc"] = cat(inputs["L1_attn_W_k"])
        sh["Wvc"] = cat(inputs["L1_attn_W_v"])
        sh["Wo1"] = f32(inputs["L1_attn_W_o"])
    if "D" in phases:
        tid = (np.arange(16)[None, :] * 128 + np.arange(128)[:, None])
        sh["tidhl"] = np.ascontiguousarray(
            np.stack([tid // 256, tid % 256], axis=-1).astype(np.float32))
        sh["gwT"] = f32(np.asarray(inputs["L1_ffn_gate_w"]).T)
        sh["Ew1"] = f32(inputs["L1_ffn_w1"])
        sh["Ew2"] = f32(inputs["L1_ffn_w2"])
    if "B" in phases or "D" in phases:
        p = np.arange(128, dtype=np.float32)[:, None]
        gg = np.arange(32, dtype=np.float32)[None, :]
        sh["iotachf"] = np.ascontiguousarray(gg * 128 + p)
        sh["iotagf"] = np.ascontiguousarray(p)
    return sh


def kernel(**inputs):
    x = np.asarray(inputs["L0_attn_x"], np.float32)
    phase_groups = [_PHASES] if _FUSED else list(_PHASES)
    xcur = [np.ascontiguousarray(x[c].T) for c in range(8)]
    out = None
    for ph in phase_groups:
        sh = _host_shared(inputs, ph)
        nc = _build(ph)
        in_maps = [dict(sh, xT=xcur[c]) for c in range(8)]
        res = run_bass_kernel_spmd(nc, in_maps, core_ids=list(range(8)))
        if ph[-1] == "D":
            out = np.stack([np.asarray(res.results[c]["out"]) for c in range(8)])
        else:
            xcur = [np.ascontiguousarray(np.asarray(res.results[c]["xTo"]))
                    for c in range(8)]
    if out is None:
        out = np.stack([xcur[c].T for c in range(8)])
    return out
